# version 25
# baseline (speedup 1.0000x reference)
import numpy as np
import ml_dtypes

import concourse.bass as bass
import concourse.mybir as mybir
from concourse.bass_utils import run_bass_kernel_spmd

F32 = mybir.dt.float32
BF16 = mybir.dt.bfloat16
AF = mybir.ActivationFunctionType
ALU = mybir.AluOpType
AX = mybir.AxisListType

NCORES = 8
D = 1024
H = 16
HD = 64
DM = 4096
NE = 8
T = 2048
NT = T // 128
PRE = 6144
NPRE = PRE // 128
NPASS = 2
TP = T // NPASS
LN_EPS = 1e-5
BIGM = 60.0
BIGL = 1e9

_TAPS = False

CONST_VALS = [0.0, 1.0, LN_EPS, -BIGM, BIGM, 1.0 / D, -0.5, 128.0, -1.0, -BIGL]

ENGS = ("tensor", "vector", "scalar", "sync", "gpsimd")
ENG_SEM = {"tensor": "st", "vector": "sv", "scalar": "ss"}


class Prog:

    def __init__(self):
        self.ops = {e: [] for e in ENGS}
        self.cnt = {}
        self.hwm = {e: {} for e in ENGS}
        self.sem_names = []
        self.needed = {}
        self.regions = {}

    def sem(self, name):
        if name not in self.cnt:
            self.sem_names.append(name)
            self.cnt[name] = 0

    def region(self, name, space, lo, hi):
        self.regions[name] = [space, lo, hi, None, {}]

    def _wait(self, eng, waits, tok):
        if tok is None:
            return
        s, v = tok
        if v <= 0:
            return
        self.needed.setdefault(s, set()).add(v)
        if self.hwm[eng].get(s, 0) >= v:
            return
        self.hwm[eng][s] = v
        waits.append((s, v))

    def op(self, eng, fn, reads=(), writes=(), extra=(), sem=None, amt=1,
           skip_waw=False):
        sem = sem or ENG_SEM[eng]
        self.sem(sem)
        waits = []
        for (s, v) in extra:
            self._wait(eng, waits, (s, v))
        for r in reads:
            R = self.regions[r]
            if R[3] is not None:
                if not (eng == "tensor" and R[3][2] == "tensor"):
                    self._wait(eng, waits, (R[3][0], R[3][1]))
        for r in writes:
            R = self.regions[r]
            if R[3] is not None and not skip_waw:
                if not (eng == "tensor" and R[3][2] == "tensor"):
                    self._wait(eng, waits, (R[3][0], R[3][1]))
            for (e2, s2), v2 in R[4].items():
                if not (eng == "tensor" and e2 == "tensor"):
                    self._wait(eng, waits, (s2, v2))
            sp, lo, hi = R[0], R[1], R[2]
            for q, Q in self.regions.items():
                if q == r or Q[0] != sp or Q[2] <= lo or Q[1] >= hi:
                    continue
                if Q[3] is None and not Q[4]:
                    continue
                if Q[3] is not None:
                    if not (eng == "tensor" and Q[3][2] == "tensor"):
                        self._wait(eng, waits, (Q[3][0], Q[3][1]))
                for (e2, s2), v2 in Q[4].items():
                    if not (eng == "tensor" and e2 == "tensor"):
                        self._wait(eng, waits, (s2, v2))
                Q[3] = None
                Q[4] = {}
        self.ops[eng].append((fn, waits, sem, amt))
        self.cnt[sem] += amt
        post = self.cnt[sem]
        for r in writes:
            R = self.regions[r]
            R[3] = (sem, post, eng)
            R[4] = {}
        for r in reads:
            self.regions[r][4][(eng, sem)] = post
        return post


def build_nc():
    nc = bass.Bass("TRN2", target_bir_lowering=False, debug=False)
    p = Prog()

    def din(name, shape, dtype=F32):
        return nc.dram_tensor(name, list(shape), dtype, kind="ExternalInput").ap()

    x_own = din("x_own", (T, D))
    x_pre = din("x_pre", (PRE, D))
    w1h = din("w1h", (NE, 16, 128, 2048))
    w2d = din("w2d", (NE, 32, 128, D))
    w_ind = din("w_ind", (8, 128, D))
    w_qd = din("w_qd", (8, 128, D))
    w_vd = din("w_vd", (8, 128, D))
    w_gd = din("w_gd", (8, 128, D))
    w_od = din("w_od", (8, 128, D))
    w_outd = din("w_outd", (8, 128, D))
    w_gated = din("w_gated", (8, 128, 8))
    b_in_pm = din("b_in_pm", (128, 8))
    b1_pm = din("b1_pm", (128, NE * 32))
    b2_d = din("b2_d", (NE, D))
    b_out_d = din("b_out_d", (1, D))
    b_gate_d = din("b_gate_d", (1, NE))
    gamma_d = din("gamma_d", (1, D))
    beta_d = din("beta_d", (1, D))
    decay_c = din("decay_c", (16, 1))
    decay_r = din("decay_r", (1, 16))
    ident_bfd = din("ident_bfd", (128, 128), BF16)
    ident_fd = din("ident_fd", (128, 128))
    tri_d = din("tri_d", (128, 128))
    iota_rd = din("iota_rd", (1, 128))
    iota1_rd = din("iota1_rd", (1, 128))
    iotar_rd = din("iotar_rd", (1, 128))
    ones_bfd = din("ones_bfd", (1, 512), BF16)
    ones_fd = din("ones_fd", (1, 128))
    cvals_d = din("cvals_d", (128, 16))

    y_out = nc.dram_tensor("y_out", [T, D], F32, kind="ExternalOutput").ap()
    g_scr = nc.dram_tensor("g_scr", [T, D], BF16, kind="Internal").ap()
    p.region("g_scr", "dram", 0, T * D * 2)
    w2_bf = nc.dram_tensor("w2_bf", [NE, 32, 128, D], BF16, kind="Internal").ap()
    q_scr = nc.dram_tensor("q_scr", [T, D], BF16, kind="Internal").ap()
    v_scr = nc.dram_tensor("v_scr", [T, D], BF16, kind="Internal").ap()
    p.region("q_scr", "dram", 10 ** 9, 10 ** 9 + T * D * 2)
    p.region("v_scr", "dram", 10 ** 9 + T * D * 2, 10 ** 9 + 2 * T * D * 2)
    base = T * D * 2
    for e in range(NE):
        p.region("w2s%d" % e, "dram", base, base + 32 * 128 * D * 2)
        base += 32 * 128 * D * 2
    taps = {}
    if _TAPS:
        for nm, shape, dt in [
            ("tap_h", (128, 8 * T), BF16), ("tap_route", (128, NT * NE), F32),
            ("tap_moe", (128, 8 * D), BF16), ("tap_c", (1, D), F32),
            ("tap_yln", (128, NT * D), BF16), ("tap_g", (T, D), BF16),
            ("tap_stats", (128, 32), F32),
        ]:
            taps[nm] = nc.dram_tensor(nm, list(shape), dt, kind="ExternalOutput").ap()

    SBUF_BASE = 16896
    CAP = 211500
    _off = [0]
    aps = {}

    def salloc(name, shape, dtype, at=None):
        nbytes = int(np.prod(shape[1:])) * mybir.dt.size(dtype)
        if at is None:
            at = _off[0]
            _off[0] = (at + nbytes + 31) & ~31
        h = nc.alloc_sbuf_tensor_at(name, list(shape), dtype, offset=SBUF_BASE + at)
        p.region(name, "sbuf", at, at + nbytes)
        aps[name] = h.ap()
        return aps[name], at

    route, _ = salloc("route", (128, NT * NE), F32)
    routeT, _ = salloc("routeT", (128, T), BF16)
    c_row, _ = salloc("c_row", (1, D), F32)
    identb, _ = salloc("identb", (128, 128), BF16)
    identf, _ = salloc("identf", (128, 128), F32)
    iota, _ = salloc("iota", (1, 128), F32)
    iota1, _ = salloc("iota1", (1, 128), F32)
    iotar, _ = salloc("iotar", (1, 128), F32)
    onesb, _ = salloc("onesb", (1, 512), BF16)
    onesf, _ = salloc("onesf", (1, 128), F32)
    decayc, _ = salloc("decayc", (16, 1), F32)
    decayr, _ = salloc("decayr", (1, 16), F32)
    loglamc, _ = salloc("loglamc", (16, 1), F32)
    loglamr, _ = salloc("loglamr", (1, 16), F32)
    lam128c, _ = salloc("lam128c", (16, 1), F32)
    w0b, _ = salloc("w0b", (128, 16), BF16)
    binpm, _ = salloc("binpm", (128, 8), F32)
    b1pm, _ = salloc("b1pm", (128, NE * 32), F32)
    b2b, _ = salloc("b2b", (8, D), BF16)
    boutb, _ = salloc("boutb", (1, D), BF16)
    bgateb, _ = salloc("bgateb", (1, NE), BF16)
    wgateb, _ = salloc("wgateb", (128, 8 * NE), BF16)
    hfm, _ = salloc("hfm", (128, 8 * T), BF16)
    cvals, _ = salloc("cvals", (128, 16), F32)
    stats, STATS_AT = salloc("stats", (128, 24), F32)
    del p.regions["stats"]
    p.region("statsA", "sbuf", STATS_AT, STATS_AT + 32)
    p.region("statsR", "sbuf", STATS_AT + 32, STATS_AT + 64)
    p.region("statsB", "sbuf", STATS_AT + 64, STATS_AT + 96)
    stats_all, _ = salloc("stats_all", (128, 64), F32)
    SE = _off[0]

    _off[0] = SE
    x_st, xn_tm = [], []
    for i in range(2):
        a, _ = salloc(f"x_st{i}", (128, D), F32)
        x_st.append(a)
    for i in range(2):
        a, _ = salloc(f"xn_tm{i}", (128, D), BF16)
        xn_tm.append(a)
    w_inb, _ = salloc("w_inb", (128, 8 * D), BF16)
    xnfm_g, _ = salloc("xnfm_g", (128, 8 * 512), BF16)
    dumA, _ = salloc("dumA", (128, D), F32)
    t_sm, _ = salloc("t_sm", (128, 8), F32)
    ge1, _ = salloc("ge1", (128, 8), F32)
    ge2, _ = salloc("ge2", (128, 8), F32)
    wvT_a, _ = salloc("wvT_a", (128, 8 * D), BF16)
    assert _off[0] <= SE + 65536, _off[0]
    _off[0] = SE + 65536
    wvnat, _ = salloc("wvnat", (128, 8 * D), BF16)
    pre_st, prexn = [], []
    for i in range(2):
        a, _ = salloc(f"pre_st{i}", (128, D), F32)
        pre_st.append(a)
    for i in range(2):
        a, _ = salloc(f"prexn{i}", (128, D), BF16)
        prexn.append(a)
    A_acc, _ = salloc("A_acc", (16, D), F32)
    A_fm, _ = salloc("A_fm", (128, 128), BF16)
    assert _off[0] <= SE + 131072, _off[0]

    _off[0] = SE
    mid, MID_AT = salloc("mid", (128, 32 * TP), BF16)
    woutb, _ = salloc("woutb", (128, 8 * D), BF16, at=MID_AT)
    w2sb, W2_AT = salloc("w2sb", (128, 32 * D), BF16, at=SE + 65536)
    del p.regions["w2sb"]
    for qd in range(4):
        p.region("w2sb%d" % qd, "sbuf",
                 W2_AT + qd * 8 * D * 2, W2_AT + (qd + 1) * 8 * D * 2)
    _off[0] = SE + 131072
    w1sl = []
    for i in range(2):
        a, _ = salloc(f"w1sl{i}", (128, 2048), BF16)
        w1sl.append(a)
    moe_tm, _ = salloc("moe_tm", (128, 8 * D), BF16)
    moefm, _ = salloc("moefm", (128, D), BF16)
    g_stage, _ = salloc("g_stage", (128, D), BF16)
    assert _off[0] <= CAP, _off[0]

    _off[0] = SE
    x_stc, xn_tmc = [], []
    for i in range(2):
        a, _ = salloc(f"x_stc{i}", (128, D), F32)
        x_stc.append(a)
    for i in range(2):
        a, _ = salloc(f"xn_tmc{i}", (128, D), BF16)
        xn_tmc.append(a)
    xnfm_cc, _ = salloc("xnfm_cc", (128, D), BF16)
    wT1, _ = salloc("wT1", (128, 8 * D), BF16)
    wT2, _ = salloc("wT2", (128, 8 * D), BF16)
    wstage, _ = salloc("wstage", (128, 8 * D), BF16)
    yln_all, _ = salloc("yln_all", (128, NT * D), BF16)
    gamma_f, _ = salloc("gamma_f", (128, D), F32)
    beta_f, _ = salloc("beta_f", (128, D), F32)
    L_bf, _ = salloc("L_bf", (128, H * 128), BF16)
    tri, _ = salloc("tri", (128, 128), F32)
    v_sb, VSB_AT = salloc("v_sb", (128, D), BF16)
    q_sb, QSB_AT = salloc("q_sb", (128, D), F32)
    qs = [salloc("qs0", (128, D), BF16, at=QSB_AT)[0],
          salloc("qs1", (128, D), BF16, at=QSB_AT + 2048)[0]]
    y_sb, _ = salloc("y_sb", (128, D), F32)
    t1f, _ = salloc("t1f", (128, D), F32)
    gate_sb, _ = salloc("gate_sb", (128, D), BF16)
    y2fm, _ = salloc("y2fm", (128, D), BF16)
    g_st = []
    for i in range(2):
        a, _ = salloc(f"g_st{i}", (128, D), BF16)
        g_st.append(a)
    out_sb = []
    for i in range(2):
        a, _ = salloc(f"out_sb{i}", (128, D), F32)
        out_sb.append(a)
    dumC, DUMC_AT = salloc("dumC", (128, D), F32)
    vs = [v_sb, salloc("vs1", (128, D), BF16, at=DUMC_AT)[0]]
    VSNM = ["v_sb", "vs1"]
    y2fm2, _ = salloc("y2fm2", (128, D), BF16, at=DUMC_AT)
    powv, _ = salloc("powv", (1, H * 128), F32)
    gate_sb2, _ = salloc("gate_sb2", (128, D), BF16)
    y2b2, _ = salloc("y2b2", (128, D), BF16, at=DUMC_AT + 2048)
    r1row, _ = salloc("r1row", (1, 128), F32)
    r2row, _ = salloc("r2row", (1, 128), F32)
    assert _off[0] <= CAP, _off[0]

    PAIRS = []
    for j in range(4):
        h = nc.alloc_psum_tensor(f"pp{j}", [128, 1024], F32)
        PAIRS.append(h.ap())
    for i in range(8):
        p.region(f"pb{i}", "psum", i * 2048, (i + 1) * 2048)
    BANKS = [PAIRS[i // 2][:, (i % 2) * 512:(i % 2) * 512 + 512] for i in range(8)]
    BK = [f"pb{i}" for i in range(8)]
    PAIRSBF = [pr.bitcast(BF16) for pr in PAIRS]
    BANKSBF = [PAIRSBF[i // 2][:, (i % 2) * 1024:(i % 2) * 1024 + 1024]
               for i in range(8)]

    def dma(eng, out, in_, sem, reads=(), writes=(), extra=(), skip_waw=False):
        p.sem(sem)
        ex = [(sem, p.cnt[sem])] + list(extra)
        return p.op(eng, (lambda o, i: (lambda e: e.dma_start(out=o, in_=i)))(out, in_),
                    reads=reads, writes=writes, extra=ex, sem=sem, amt=16,
                    skip_waw=skip_waw)

    def mm(lhsT, rhs, out, start, stop, reads=(), writes=(), extra=()):
        return p.op("tensor",
                    (lambda l, r, o, b, e2: (lambda e: e.matmul(o, l, r, start=b, stop=e2)))
                    (lhsT, rhs, out, start, stop),
                    reads=reads, writes=writes, extra=extra)

    def tp(in_, out, ident, reads=(), writes=(), extra=()):
        return p.op("tensor",
                    (lambda i, o, d: (lambda e: e.transpose(o, i, d)))(in_, out, ident),
                    reads=reads, writes=writes, extra=extra)

    def vop(fn, reads=(), writes=(), extra=(), skip_waw=False):
        return p.op("vector", fn, reads=reads, writes=writes, extra=extra,
                    skip_waw=skip_waw)

    def sop(fn, reads=(), writes=(), extra=(), skip_waw=False):
        return p.op("scalar", fn, reads=list(reads) + ["cvals"], writes=writes,
                    extra=extra, skip_waw=skip_waw)

    for dst, src, nm in ((identb, ident_bfd, "identb"), (identf, ident_fd, "identf"),
                         (iota, iota_rd, "iota"), (iota1, iota1_rd, "iota1"),
                         (iotar, iotar_rd, "iotar"), (onesb, ones_bfd, "onesb"),
                         (onesf, ones_fd, "onesf"),
                         (decayc, decay_c, "decayc"), (decayr, decay_r, "decayr"),
                         (binpm, b_in_pm, "binpm"), (b1pm, b1_pm, "b1pm"),
                         (cvals, cvals_d, "cvals")):
        dma("sync", dst, src, "s_c", writes=[nm])
    for ci, cv in enumerate(CONST_VALS):
        nc.const_aps.register(F32, cv, cvals[:, ci:ci + 1])
    dma("gpsimd", b2b, b2_d, "s_w", writes=["b2b"])
    dma("gpsimd", boutb, b_out_d, "s_w", writes=["boutb"])
    dma("gpsimd", bgateb, b_gate_d, "s_w", writes=["bgateb"])
    dma("gpsimd", wgateb, w_gated.transpose([1, 0, 2]), "s_w", writes=["wgateb"])
    dma("gpsimd", w_inb, w_ind.transpose([1, 0, 2]), "s_w", writes=["w_inb"])
    dma("gpsimd", wvnat, w_vd.transpose([1, 0, 2]), "s_w", writes=["wvnat"])
    for e in range(2, 5):
        dma("gpsimd", w2_bf[e], w2d[e], "s_pc", writes=["w2s%d" % e])

    sop(lambda e: e.activation(loglamc, decayc, AF.Ln), reads=["decayc"], writes=["loglamc"])
    sop(lambda e: e.activation(loglamr, decayr, AF.Ln), reads=["decayr"], writes=["loglamr"])
    sop(lambda e: e.activation(lam128c, loglamc, AF.Exp, scale=128.0),
        reads=["loglamc"], writes=["lam128c"])
    mm(iotar[0:1, 0:128], loglamr, BANKS[0][0:128, 0:16], True, True,
       reads=["iotar", "loglamr"], writes=[BK[0]])
    sop(lambda e: e.activation(w0b, BANKS[0][0:128, 0:16], AF.Exp),
        reads=[BK[0]], writes=["w0b"])
    for ec in range(8):
        for dc in range(8):
            b = BK[4 + ((ec * 8 + dc) % 2)]
            tp(wvnat[:, ec * D + dc * 128: ec * D + dc * 128 + 128],
               BANKSBF[4 + ((ec * 8 + dc) % 2)][:, 0:128], identb,
               reads=["wvnat", "identb"], writes=[b])
            vop((lambda bb, ec_, dc_: (lambda e: e.tensor_copy(
                wvT_a[:, dc_ * D + ec_ * 128: dc_ * D + ec_ * 128 + 128],
                bb[:, 0:128])))(BANKSBF[4 + ((ec * 8 + dc) % 2)], ec, dc),
                reads=[b], writes=["wvT_a"], skip_waw=True)

    vop(lambda e: e.memset(A_acc, 0.0), writes=["A_acc"])
    vop(lambda e: e.memset(c_row, 0.0), writes=["c_row"])

    def emit_ln(src_ap, src_nm, dst_ap, dst_nm, mu_ap, inv_ap, statkeys,
                dum_ap, dum_nm, c0=0, statnm="statsA"):
        mu_nm, inv_nm = statkeys
        sop((lambda s, dd: (lambda e: e.activation(dd, s, AF.Copy,
            accum_out=stats[:, c0:c0 + 1])))(src_ap, dum_ap),
            reads=[src_nm], writes=[dum_nm, statnm])
        sop((lambda s, dd: (lambda e: e.activation(dd, s, AF.Square,
            accum_out=stats[:, c0 + 1:c0 + 2])))(src_ap, dum_ap),
            reads=[src_nm, statnm], writes=[dum_nm, statnm])
        sop((lambda m: (lambda e: e.activation(m, stats[:, c0:c0 + 1], AF.Copy,
            scale=1.0 / D)))(mu_ap), reads=[statnm], writes=[mu_nm], skip_waw=True)
        sop((lambda m: (lambda e: e.activation(stats[:, c0 + 3:c0 + 4], m,
            AF.Square)))(mu_ap), reads=[mu_nm, statnm], writes=[statnm])
        vop(lambda e: e.tensor_scalar(stats[:, c0 + 4:c0 + 5],
            stats[:, c0 + 1:c0 + 2], 1.0 / D, stats[:, c0 + 3:c0 + 4],
            ALU.mult, ALU.subtract), reads=[statnm], writes=[statnm])
        sop(lambda e: e.activation(stats[:, c0 + 5:c0 + 6], stats[:, c0 + 4:c0 + 5],
            AF.Ln, bias=LN_EPS), reads=[statnm], writes=[statnm])
        sop((lambda iv: (lambda e: e.activation(iv, stats[:, c0 + 5:c0 + 6], AF.Exp,
            scale=-0.5)))(inv_ap), reads=[statnm], writes=[inv_nm], skip_waw=True)
        if dst_ap is not None:
            vop((lambda s, o, m, iv: (lambda e: e.tensor_scalar(o, s, m, iv,
                ALU.subtract, ALU.mult)))(src_ap, dst_ap, mu_ap, inv_ap),
                reads=[src_nm, mu_nm, inv_nm], writes=[dst_nm])

    def emit_P_chunk(cix):
        par = cix % 2
        dma("sync", pre_st[par], x_pre[cix * 128:(cix + 1) * 128, :],
            "s_p%d" % par, writes=["pre_st%d" % par])
        c0, snm = (0, "statsA") if cix % 2 == 0 else (16, "statsB")
        emit_ln(pre_st[par], "pre_st%d" % par, prexn[par], "prexn%d" % par,
                stats[:, c0 + 2:c0 + 3], stats[:, c0 + 6:c0 + 7], (snm, snm),
                dumA, "dumA", c0=c0, statnm=snm)
        mm(w0b, prexn[par][:, 0:512], BANKS[6][0:16, :], True, True,
           reads=["w0b", "prexn%d" % par], writes=[BK[6]])
        mm(w0b, prexn[par][:, 512:1024], BANKS[7][0:16, :], True, True,
           reads=["w0b", "prexn%d" % par], writes=[BK[7]])
        vop(lambda e: e.scalar_tensor_tensor(A_acc[:, 0:512], A_acc[:, 0:512],
            lam128c, BANKS[6][0:16, :], ALU.mult, ALU.add),
            reads=[BK[6], "A_acc", "lam128c"], writes=["A_acc"])
        vop(lambda e: e.scalar_tensor_tensor(A_acc[:, 512:1024], A_acc[:, 512:1024],
            lam128c, BANKS[7][0:16, :], ALU.mult, ALU.add),
            reads=[BK[7], "A_acc", "lam128c"], writes=["A_acc"])

    for t5 in range(4):
        for cpx in range(12 * t5, 12 * t5 + 12):
            emit_P_chunk(cpx)
        for i in range(4):
            t = t5 * 4 + i
            par = t % 2
            dma("sync", x_st[par], x_own[t * 128:(t + 1) * 128, :],
                "s_x%d" % par, writes=["x_st%d" % par])
            c0, snm = (0, "statsA") if t % 2 == 0 else (16, "statsB")
            emit_ln(x_st[par], "x_st%d" % par, xn_tm[par], "xn_tm%d" % par,
                    stats[:, c0 + 2:c0 + 3], stats[:, c0 + 6:c0 + 7], (snm, snm),
                    dumA, "dumA", c0=c0, statnm=snm)
            tb = 4 + (i % 2)
            for kc in range(8):
                tp(xn_tm[par][:, kc * 128:(kc + 1) * 128],
                   BANKSBF[tb][:, kc * 128:(kc + 1) * 128], identb,
                   reads=["xn_tm%d" % par, "identb"], writes=[BK[tb]])
            vop((lambda bb, i_: (lambda e: e.tensor_copy(
                xnfm_g[:, i_ * 1024:(i_ + 1) * 1024], bb)))(BANKSBF[tb], i),
                reads=[BK[tb]], writes=["xnfm_g"])
        xnv_g = xnfm_g.rearrange("p (i kc j) -> p kc i j", i=4, kc=8)
        for m in range(8):
            bk = BK[m % 4]
            for kc in range(8):
                mm(w_inb[:, kc * D + m * 128: kc * D + (m + 1) * 128],
                   xnv_g[:, kc],
                   BANKS[m % 4], kc == 0, kc == 7,
                   reads=["w_inb", "xnfm_g"], writes=[bk])
            sop((lambda bb, m_, t5_: (lambda e: e.activation(
                hfm[:, m_ * T + t5_ * 512: m_ * T + (t5_ + 1) * 512], bb,
                AF.Identity, bias=binpm[:, m_:m_ + 1])))(BANKS[m % 4], m, t5),
                reads=[bk, "binpm"], writes=["hfm"], skip_waw=True)
        for i in range(4):
            t = t5 * 4 + i
            for kc in range(8):
                mm(hfm[:, kc * T + t * 128: kc * T + (t + 1) * 128],
                   wgateb[:, kc * 8:(kc + 1) * 8],
                   BANKS[4][:, 0:8], kc == 0, False,
                   reads=["hfm", "wgateb"], writes=[BK[4]])
            mm(onesb[0:1, 0:128], bgateb, BANKS[4][:, 0:8], False, True,
               reads=["onesb", "bgateb"], writes=[BK[4]])
            vop(lambda e: e.reduce_max(stats[:, 8:9], BANKS[4][:, 0:8], AX.X),
                reads=[BK[4]], writes=["statsR"])
            vop(lambda e: e.tensor_scalar(ge1, BANKS[4][:, 0:8], stats[:, 8:9],
                None, ALU.is_ge), reads=[BK[4], "statsR"], writes=["ge1"])
            vop(lambda e: e.scalar_tensor_tensor(t_sm, ge1, -BIGL, BANKS[4][:, 0:8],
                ALU.mult, ALU.add), reads=["ge1", BK[4]], writes=["t_sm"])
            vop(lambda e: e.reduce_max(stats[:, 9:10], t_sm, AX.X),
                reads=["t_sm"], writes=["statsR"])
            vop(lambda e: e.tensor_scalar(ge2, BANKS[4][:, 0:8], stats[:, 9:10],
                None, ALU.is_ge), reads=[BK[4], "statsR"], writes=["ge2"])
            vop(lambda e: e.tensor_scalar(stats[:, 10:11], stats[:, 8:9], -1.0,
                None, ALU.mult), reads=["statsR"], writes=["statsR"])
            sop(lambda e: e.activation(t_sm, BANKS[4][:, 0:8], AF.Exp,
                bias=stats[:, 10:11]), reads=[BK[4], "statsR"], writes=["t_sm"])
            vop(lambda e: e.tensor_mul(t_sm, t_sm, ge2),
                reads=["t_sm", "ge2"], writes=["t_sm"])
            vop(lambda e: e.reduce_sum(stats[:, 11:12], t_sm, AX.X),
                reads=["t_sm"], writes=["statsR"])
            vop(lambda e: e.reciprocal(stats[:, 12:13], stats[:, 11:12]),
                reads=["statsR"], writes=["statsR"])
            vop((lambda t_: (lambda e: e.tensor_scalar(
                route[:, t_ * 8:(t_ + 1) * 8], t_sm, stats[:, 12:13], None,
                ALU.mult)))(t), reads=["t_sm", "statsR"], writes=["route"],
                skip_waw=True)
            tp(route[:, t * 8:(t + 1) * 8], BANKS[5][0:8, 0:128], identf,
               reads=["route", "identf"], writes=[BK[5]])
            vop((lambda t_: (lambda e: e.tensor_copy(
                routeT[0:8, t_ * 128:(t_ + 1) * 128], BANKS[5][0:8, 0:128])))(t),
                reads=[BK[5]], writes=["routeT"], skip_waw=True)
    for kc in range(8):
        tp(A_acc[:, kc * 128:(kc + 1) * 128], BANKS[4][:, kc * 16:(kc + 1) * 16],
           identf[0:16, 0:16], reads=["A_acc", "identf"], writes=[BK[4]])
    vop(lambda e: e.tensor_copy(A_fm, BANKS[4][:, 0:128]),
        reads=[BK[4]], writes=["A_fm"])
    for h in range(H):
        for kc in range(8):
            mm(A_fm[:, kc * 16 + h: kc * 16 + h + 1],
               wvT_a[:, kc * D + h * HD: kc * D + (h + 1) * HD],
               BANKS[5][0:1, 0:64], kc == 0, kc == 7,
               reads=["A_fm", "wvT_a"], writes=[BK[5]])
        vop((lambda h_: (lambda e: e.tensor_copy(
            c_row[0:1, h_ * HD:(h_ + 1) * HD], BANKS[5][0:1, 0:64])))(h),
            reads=[BK[5]], writes=["c_row"], skip_waw=True)
    if _TAPS:
        dma("sync", taps["tap_c"], c_row, "s_tap", reads=["c_row"])

    if _TAPS:
        dma("sync", taps["tap_h"], hfm, "s_tap", reads=["hfm"])
        dma("sync", taps["tap_route"], route, "s_tap", reads=["route"])

    w1_cnt = [0, 0]

    def load_w1(e, mg):
        slot = mg % 2
        w1_cnt[slot] = dma("gpsimd", w1sl[slot], w1h[e, mg], "s_w1%d" % slot,
                           writes=["w1sl%d" % slot])

    for pas in range(NPASS):
        poff = pas * TP
        for i in range(8):
            tglob = pas * 8 + i
            b0, b1 = BK[2 * (i % 4)], BK[2 * (i % 4) + 1]
            mm(routeT[0:8, tglob * 128:(tglob + 1) * 128], b2b[0:8, 0:512],
               BANKS[2 * (i % 4)], True, True, reads=["routeT", "b2b"], writes=[b0])
            mm(routeT[0:8, tglob * 128:(tglob + 1) * 128], b2b[0:8, 512:1024],
               BANKS[2 * (i % 4) + 1], True, True, reads=["routeT", "b2b"], writes=[b1])
            vop((lambda pp, i_: (lambda e: e.tensor_copy(
                moe_tm[:, i_ * D:(i_ + 1) * D], pp)))(PAIRS[(i % 4)], i),
                reads=[b0, b1], writes=["moe_tm"], skip_waw=True)
        for e in range(NE):
            load_w1(e, 0)
            load_w1(e, 1)
            for qd in range(4):
                if e >= 2:
                    dma("sync", w2sb[:, qd * 8 * D:(qd + 1) * 8 * D],
                        w2_bf[e, 8 * qd:8 * qd + 8].transpose([1, 0, 2]),
                        "s_w2%d" % (qd % 2), reads=["w2s%d" % e],
                        writes=["w2sb%d" % qd])
                else:
                    dma("gpsimd", w2sb[:, qd * 8 * D:(qd + 1) * 8 * D],
                        w2d[e, 8 * qd:8 * qd + 8].transpose([1, 0, 2]),
                        "s_w2g%d" % (qd % 2), writes=["w2sb%d" % qd])
            for mg in range(16):
                slot = mg % 2
                bsel = 0 if mg % 2 == 0 else 4
                for mi in range(2):
                    for kc in range(8):
                        for th in range(2):
                            bk = BK[bsel + 2 * mi + th]
                            mm(w1sl[slot][:, (mi * 8 + kc) * 128:(mi * 8 + kc + 1) * 128],
                               hfm[:, kc * T + poff + th * 512: kc * T + poff + (th + 1) * 512],
                               BANKS[bsel + 2 * mi + th], kc == 0, kc == 7,
                               reads=["w1sl%d" % slot, "hfm"], writes=[bk])
                if mg < 14:
                    load_w1(e, mg + 2)
                for mi in range(2):
                    m = mg * 2 + mi
                    pr2 = (bsel + 2 * mi) // 2
                    sop((lambda pp, m_, e_: (lambda eng: eng.activation(
                        mid[:, m_ * TP:(m_ + 1) * TP], pp, AF.Relu,
                        bias=b1pm[:, e_ * 32 + m_: e_ * 32 + m_ + 1])))
                        (PAIRS[pr2], m, e),
                        reads=[BK[bsel + 2 * mi], BK[bsel + 2 * mi + 1], "b1pm"],
                        writes=["mid"], skip_waw=True)
            for i in range(8):
                tglob = pas * 8 + i
                b0, b1 = BK[2 * (i % 4)], BK[2 * (i % 4) + 1]
                for kc in range(32):
                    w2r = "w2sb%d" % (kc // 8)
                    mm(mid[:, kc * TP + i * 128: kc * TP + (i + 1) * 128],
                       w2sb[:, kc * D: kc * D + 512],
                       BANKS[2 * (i % 4)], kc == 0, kc == 31,
                       reads=["mid", w2r], writes=[b0])
                    mm(mid[:, kc * TP + i * 128: kc * TP + (i + 1) * 128],
                       w2sb[:, kc * D + 512:(kc + 1) * D],
                       BANKS[2 * (i % 4) + 1], kc == 0, kc == 31,
                       reads=["mid", w2r], writes=[b1])
                vop((lambda pp, i_, e_, tg: (lambda eng: eng.scalar_tensor_tensor(
                    moe_tm[:, i_ * D:(i_ + 1) * D], pp,
                    route[:, tg * 8 + e_: tg * 8 + e_ + 1],
                    moe_tm[:, i_ * D:(i_ + 1) * D], ALU.mult, ALU.add)))
                    (PAIRS[(i % 4)], i, e, tglob),
                    reads=[b0, b1, "route", "moe_tm"], writes=["moe_tm"])
            if pas == 0 and e < 3:
                dma("gpsimd", w2_bf[e + 5], w2d[e + 5], "s_pc",
                    writes=["w2s%d" % (e + 5)])
        if _TAPS and pas == 0:
            dma("sync", taps["tap_moe"], moe_tm, "s_tap", reads=["moe_tm"])
        dma("gpsimd", woutb, w_outd.transpose([1, 0, 2]), "s_wo", writes=["woutb"])
        for i in range(8):
            tglob = pas * 8 + i
            for j in range(8):
                tp(moe_tm[:, i * D + j * 128: i * D + (j + 1) * 128],
                   BANKSBF[4][:, j * 128:(j + 1) * 128], identb,
                   reads=["moe_tm", "identb"], writes=[BK[4]])
            vop(lambda e: e.tensor_copy(moefm, BANKSBF[4]),
                reads=[BK[4]], writes=["moefm"])
            for th in range(2):
                for j in range(8):
                    mm(moefm[:, j * 128:(j + 1) * 128],
                       woutb[:, j * D + th * 512: j * D + (th + 1) * 512],
                       BANKS[th], j == 0, False,
                       reads=["moefm", "woutb"], writes=[BK[th]])
                mm(onesb[0:1, 0:128], boutb[0:1, th * 512:(th + 1) * 512],
                   BANKS[th], False, True,
                   reads=["onesb", "boutb"], writes=[BK[th]])
                sop((lambda bb, th_: (lambda e: e.activation(
                    g_stage[:, th_ * 512:(th_ + 1) * 512], bb, AF.Silu)))
                    (BANKS[th], th),
                    reads=[BK[th]], writes=["g_stage"], skip_waw=(th == 1))
            dma("sync", g_scr[tglob * 128:(tglob + 1) * 128, :], g_stage, "s_gw",
                reads=["g_stage"], writes=["g_scr"], skip_waw=True)

    dma("sync", gamma_f[0:1, :], gamma_d, "s_c", writes=["gamma_f"])
    dma("sync", beta_f[0:1, :], beta_d, "s_c", writes=["beta_f"])
    dma("sync", tri, tri_d, "s_c", writes=["tri"])
    for th in range(2):
        mm(onesf[0:1, 0:128], gamma_f[0:1, th * 512:(th + 1) * 512],
           BANKS[4 + th], True, True, reads=["onesf", "gamma_f"], writes=[BK[4 + th]])
        vop((lambda bb, th_: (lambda e: e.tensor_copy(
            gamma_f[:, th_ * 512:(th_ + 1) * 512], bb)))(BANKS[4 + th], th),
            reads=[BK[4 + th]], writes=["gamma_f"], skip_waw=True)
    for th in range(2):
        mm(onesf[0:1, 0:128], beta_f[0:1, th * 512:(th + 1) * 512],
           BANKS[4 + th], True, True, reads=["onesf", "beta_f"], writes=[BK[4 + th]])
        vop((lambda bb, th_: (lambda e: e.tensor_copy(
            beta_f[:, th_ * 512:(th_ + 1) * 512], bb)))(BANKS[4 + th], th),
            reads=[BK[4 + th]], writes=["beta_f"], skip_waw=True)
    for h in range(H):
        vop((lambda h_: (lambda e: e.tensor_scalar(r1row, iota,
            loglamr[0:1, h_:h_ + 1], None, ALU.mult)))(h),
            reads=["iota", "loglamr"], writes=["r1row"])
        vop((lambda h_: (lambda e: e.tensor_scalar(r2row, iota,
            loglamr[0:1, h_:h_ + 1], -1.0, ALU.mult, ALU.mult)))(h),
            reads=["iota", "loglamr"], writes=["r2row"])
        mm(onesf[0:1, 0:128], r1row, BANKS[6][:, 0:128], True, False,
           reads=["onesf", "r1row"], writes=[BK[6]])
        mm(r2row, onesf[0:1, 0:128], BANKS[6][:, 0:128], False, True,
           reads=["r2row", "onesf"], writes=[BK[6]])
        vop(lambda e: e.scalar_tensor_tensor(t1f[:, 0:128], tri, BIGM,
            BANKS[6][:, 0:128], ALU.mult, ALU.add),
            reads=["tri", BK[6]], writes=["t1f"])
        sop((lambda h_: (lambda e: e.activation(L_bf[:, h_ * 128:(h_ + 1) * 128],
            t1f[:, 0:128], AF.Exp, bias=-BIGM)))(h),
            reads=["t1f"], writes=["L_bf"], skip_waw=True)
        vop((lambda h_: (lambda e: e.tensor_scalar(
            powv[0:1, h_ * 128:(h_ + 1) * 128], iota1,
            loglamr[0:1, h_:h_ + 1], None, ALU.mult)))(h),
            reads=["iota1", "loglamr"], writes=["powv"], skip_waw=True)
    sop(lambda e: e.activation(powv, powv, AF.Exp), reads=["powv"], writes=["powv"])
    dma("gpsimd", wstage, w_qd.transpose([1, 0, 2]), "s_w", writes=["wstage"])
    for dst, dnm, nxt in ((wT1, "wT1", w_vd), (wT2, "wT2", None)):
        for dc in range(8):
            wb = 4 + (dc % 2)
            for ec in range(8):
                tp(wstage[:, ec * D + dc * 128: ec * D + dc * 128 + 128],
                   BANKSBF[wb][:, ec * 128:(ec + 1) * 128], identb,
                   reads=["wstage", "identb"], writes=[BK[wb]])
            vop((lambda bb, dc_, d_: (lambda e: e.tensor_copy(
                d_[:, dc_ * D:(dc_ + 1) * D], bb)))(BANKSBF[wb], dc, dst),
                reads=[BK[wb]], writes=[dnm], skip_waw=True)
        if nxt is not None:
            dma("gpsimd", wstage, nxt.transpose([1, 0, 2]), "s_w", writes=["wstage"])

    for t in range(NT):
        par = t % 2
        dma("sync", x_stc[par], x_own[t * 128:(t + 1) * 128, :],
            "s_x%d" % par, writes=["x_stc%d" % par])
        c0, snm = (0, "statsA") if t % 2 == 0 else (16, "statsB")
        emit_ln(x_stc[par], "x_stc%d" % par, None, None,
                stats_all[:, t:t + 1], stats_all[:, 16 + t:17 + t],
                ("stats_all", "stats_all"), dumC, "dumC", c0=c0, statnm=snm)

    def ca_pre(t):
        par = t % 2
        dma("sync", x_stc[par], x_own[t * 128:(t + 1) * 128, :],
            "s_x%d" % par, writes=["x_stc%d" % par])
        vop((lambda s, o, t_: (lambda e: e.tensor_scalar(o, s,
            stats_all[:, t_:t_ + 1], stats_all[:, 16 + t_:17 + t_],
            ALU.subtract, ALU.mult)))(x_stc[par], xn_tmc[par], t),
            reads=["x_stc%d" % par, "stats_all"], writes=["xn_tmc%d" % par])

    def ca_project(t):
        par = t % 2
        for kc in range(8):
            tp(xn_tmc[par][:, kc * 128:(kc + 1) * 128],
               BANKSBF[0][:, kc * 128:(kc + 1) * 128], identb,
               reads=["xn_tmc%d" % par, "identb"], writes=[BK[0]])
        vop(lambda e: e.tensor_copy(xnfm_cc, BANKSBF[0]),
            reads=[BK[0]], writes=["xnfm_cc"])
        for kc in range(8):
            mm(xnfm_cc[:, kc * 128:(kc + 1) * 128], wT1[:, kc * D: kc * D + 512],
               BANKS[2], kc == 0, kc == 7, reads=["xnfm_cc", "wT1"], writes=[BK[2]])
            mm(xnfm_cc[:, kc * 128:(kc + 1) * 128], wT1[:, kc * D + 512:(kc + 1) * D],
               BANKS[3], kc == 0, kc == 7, reads=["xnfm_cc", "wT1"], writes=[BK[3]])
        for kc in range(8):
            mm(xnfm_cc[:, kc * 128:(kc + 1) * 128], wT2[:, kc * D: kc * D + 512],
               BANKS[4], kc == 0, kc == 7, reads=["xnfm_cc", "wT2"], writes=[BK[4]])
            mm(xnfm_cc[:, kc * 128:(kc + 1) * 128], wT2[:, kc * D + 512:(kc + 1) * D],
               BANKS[5], kc == 0, kc == 7, reads=["xnfm_cc", "wT2"], writes=[BK[5]])
        vop((lambda q: (lambda e: e.tensor_copy(q, PAIRS[1])))(qs[par]),
            reads=[BK[2], BK[3]], writes=["qs%d" % par])
        vop((lambda v: (lambda e: e.tensor_copy(v, PAIRS[2])))(vs[par]),
            reads=[BK[4], BK[5]], writes=[VSNM[par]])

    def ca_scan(t):
        par = t % 2
        for h in range(H):
            bsl = BANKS[6 + (h // 8)][:, (h % 8) * HD:((h % 8) + 1) * HD]
            mm(L_bf[:, h * 128:(h + 1) * 128], vs[par][:, h * HD:(h + 1) * HD],
               bsl, h % 8 == 0, False,
               reads=["L_bf", VSNM[par]], writes=[BK[6 + (h // 8)]])
        for h in range(H):
            bsl = BANKS[6 + (h // 8)][:, (h % 8) * HD:((h % 8) + 1) * HD]
            mm(powv[0:1, h * 128:(h + 1) * 128], c_row[0:1, h * HD:(h + 1) * HD],
               bsl, False, h % 8 == 7,
               reads=["powv", "c_row"], writes=[BK[6 + (h // 8)]])
        for h in range(H):
            bq = 2 + (h // 8)
            bsl = BANKS[bq][0:1, (h % 8) * HD:((h % 8) + 1) * HD]
            mm(w0b[:, h:h + 1], vs[par][:, h * HD:(h + 1) * HD],
               bsl, h % 8 == 0, False, reads=["w0b", VSNM[par]], writes=[BK[bq]])
        for h in range(H):
            bq = 2 + (h // 8)
            bsl = BANKS[bq][0:1, (h % 8) * HD:((h % 8) + 1) * HD]
            mm(powv[0:1, h * 128 + 127: h * 128 + 128],
               c_row[0:1, h * HD:(h + 1) * HD],
               bsl, False, h % 8 == 7, reads=["powv", "c_row"], writes=[BK[bq]])
        vop((lambda q: (lambda e: e.tensor_mul(y_sb, q, PAIRS[3])))(qs[par]),
            reads=["qs%d" % par, BK[6], BK[7]], writes=["y_sb"])
        vop(lambda e: e.tensor_copy(c_row[0:1, 0:512], BANKS[2][0:1, :]),
            reads=[BK[2]], writes=["c_row"])
        vop(lambda e: e.tensor_copy(c_row[0:1, 512:1024], BANKS[3][0:1, :]),
            reads=[BK[3]], writes=["c_row"], skip_waw=True)
        sop(lambda e: e.activation(t1f, y_sb, AF.Copy, accum_out=stats[:, 16:17]),
            reads=["y_sb"], writes=["t1f", "statsB"])
        sop(lambda e: e.activation(t1f, y_sb, AF.Square, accum_out=stats[:, 17:18]),
            reads=["y_sb", "statsB"], writes=["t1f", "statsB"])
        sop(lambda e: e.activation(stats[:, 18:19], stats[:, 16:17], AF.Copy,
            scale=1.0 / D), reads=["statsB"], writes=["statsB"])
        sop(lambda e: e.activation(stats[:, 19:20], stats[:, 18:19], AF.Square),
            reads=["statsB"], writes=["statsB"])
        vop(lambda e: e.tensor_scalar(stats[:, 20:21], stats[:, 17:18], 1.0 / D,
            stats[:, 19:20], ALU.mult, ALU.subtract), reads=["statsB"],
            writes=["statsB"])
        sop(lambda e: e.activation(stats[:, 21:22], stats[:, 20:21], AF.Ln,
            bias=LN_EPS), reads=["statsB"], writes=["statsB"])
        sop(lambda e: e.activation(stats[:, 22:23], stats[:, 21:22], AF.Exp,
            scale=-0.5), reads=["statsB"], writes=["statsB"])
        vop(lambda e: e.scalar_tensor_tensor(t1f, y_sb, stats[:, 18:19], gamma_f,
            ALU.subtract, ALU.mult), reads=["y_sb", "statsB", "gamma_f"],
            writes=["t1f"])
        vop((lambda t_: (lambda e: e.scalar_tensor_tensor(
            yln_all[:, t_ * D:(t_ + 1) * D], t1f, stats[:, 22:23], beta_f,
            ALU.mult, ALU.add)))(t), reads=["t1f", "statsB", "beta_f"],
            writes=["yln_all"], skip_waw=True)

    ca_pre(0)
    ca_pre(1)
    ca_project(0)
    for t in range(1, NT):
        if t + 1 < NT:
            ca_pre(t + 1)
        ca_project(t)
        ca_scan(t - 1)
    ca_scan(NT - 1)
    if _TAPS:
        dma("sync", taps["tap_yln"], yln_all, "s_tap", reads=["yln_all"])
        dma("sync", taps["tap_stats"], stats_all[:, 0:32], "s_tap", reads=["stats_all"])

    dma("gpsimd", wstage, w_gd.transpose([1, 0, 2]), "s_w", writes=["wstage"])
    for dst, dnm, nxt in ((wT1, "wT1", w_od), (wT2, "wT2", None)):
        for dc in range(8):
            wb = 4 + (dc % 2)
            for ec in range(8):
                tp(wstage[:, ec * D + dc * 128: ec * D + dc * 128 + 128],
                   BANKSBF[wb][:, ec * 128:(ec + 1) * 128], identb,
                   reads=["wstage", "identb"], writes=[BK[wb]])
            vop((lambda bb, dc_, d_: (lambda e: e.tensor_copy(
                d_[:, dc_ * D:(dc_ + 1) * D], bb)))(BANKSBF[wb], dc, dst),
                reads=[BK[wb]], writes=[dnm], skip_waw=True)
        if nxt is not None:
            dma("gpsimd", wstage, nxt.transpose([1, 0, 2]), "s_w", writes=["wstage"])

    def cb_pre(t):
        par = t % 2
        dma("sync", x_stc[par], x_own[t * 128:(t + 1) * 128, :],
            "s_x%d" % par, writes=["x_stc%d" % par])
        dma("sync", g_st[par], g_scr[t * 128:(t + 1) * 128, :],
            "s_g%d" % par, reads=["g_scr"], writes=["g_st%d" % par])
        vop((lambda s, o, t_: (lambda e: e.tensor_scalar(o, s,
            stats_all[:, t_:t_ + 1], stats_all[:, 16 + t_:17 + t_],
            ALU.subtract, ALU.mult)))(x_stc[par], xn_tmc[par], t),
            reads=["x_stc%d" % par, "stats_all"], writes=["xn_tmc%d" % par])

    cb_pre(0)
    for t in range(NT):
        par = t % 2
        if t + 1 < NT:
            cb_pre(t + 1)
        for kc in range(8):
            tp(xn_tmc[par][:, kc * 128:(kc + 1) * 128],
               BANKSBF[4][:, kc * 128:(kc + 1) * 128], identb,
               reads=["xn_tmc%d" % par, "identb"], writes=[BK[4]])
        vop(lambda e: e.tensor_copy(xnfm_cc, BANKSBF[4]),
            reads=[BK[4]], writes=["xnfm_cc"])
        gb = 0 if t % 2 == 0 else 2
        yb = 2 if t % 2 == 0 else 0
        gs_ap, gs_nm = (gate_sb, "gate_sb") if t % 2 == 0 else (gate_sb2, "gate_sb2")
        y2_ap, y2_nm = (v_sb, "v_sb") if t % 2 == 0 else (y2b2, "y2b2")
        yf_ap, yf_nm = (y2fm, "y2fm") if t % 2 == 0 else (y2fm2, "y2fm2")
        for kc in range(8):
            mm(xnfm_cc[:, kc * 128:(kc + 1) * 128], wT1[:, kc * D: kc * D + 512],
               BANKS[gb], kc == 0, kc == 7, reads=["xnfm_cc", "wT1"], writes=[BK[gb]])
            mm(xnfm_cc[:, kc * 128:(kc + 1) * 128], wT1[:, kc * D + 512:(kc + 1) * D],
               BANKS[gb + 1], kc == 0, kc == 7, reads=["xnfm_cc", "wT1"],
               writes=[BK[gb + 1]])
        sop((lambda g, pp: (lambda e: e.activation(g, pp, AF.Silu)))
            (gs_ap, PAIRS[gb // 2]), reads=[BK[gb], BK[gb + 1]], writes=[gs_nm])
        vop((lambda t_, g, y2: (lambda e: e.tensor_mul(y2,
            yln_all[:, t_ * D:(t_ + 1) * D], g)))(t, gs_ap, y2_ap),
            reads=["yln_all", gs_nm], writes=[y2_nm])
        for kc in range(8):
            tp(y2_ap[:, kc * 128:(kc + 1) * 128],
               BANKSBF[yb][:, kc * 128:(kc + 1) * 128],
               identb, reads=[y2_nm, "identb"], writes=[BK[yb]])
        vop((lambda yf, bb: (lambda e: e.tensor_copy(yf, bb)))(yf_ap, BANKSBF[yb]),
            reads=[BK[yb]], writes=[yf_nm])
        for kc in range(8):
            mm(yf_ap[:, kc * 128:(kc + 1) * 128], wT2[:, kc * D: kc * D + 512],
               BANKS[6], kc == 0, kc == 7, reads=[yf_nm, "wT2"], writes=[BK[6]])
            mm(yf_ap[:, kc * 128:(kc + 1) * 128], wT2[:, kc * D + 512:(kc + 1) * D],
               BANKS[7], kc == 0, kc == 7, reads=[yf_nm, "wT2"], writes=[BK[7]])
        ob = out_sb[t % 2]
        onm = "out_sb%d" % (t % 2)
        vop((lambda o, gp: (lambda e: e.tensor_mul(o, PAIRS[3], gp)))
            (ob, g_st[par]),
            reads=[BK[6], BK[7], "g_st%d" % par], writes=[onm])
        vop((lambda o, xb: (lambda e: e.tensor_add(o, o, xb)))(ob, x_stc[par]),
            reads=[onm, "x_stc%d" % par], writes=[onm])
        dma("sync", y_out[t * 128:(t + 1) * 128, :], ob, "s_o%d" % (t % 2),
            reads=[onm])

    fin = [("s_o0", p.cnt.get("s_o0", 0)), ("s_o1", p.cnt.get("s_o1", 0))]
    if _TAPS:
        fin.append(("s_tap", p.cnt.get("s_tap", 0)))
    p.op("sync", None, extra=fin, sem="s_fin", amt=0)

    import contextlib
    with contextlib.ExitStack() as stack:
        sems = {}
        for s in p.sem_names:
            sems[s] = stack.enter_context(nc.semaphore(s))
        blk = stack.enter_context(nc.Block())

        eng_sems = set(ENG_SEM.values())
        needed = {s: v for s, v in p.needed.items() if s in eng_sems}
        plans = {}
        for name in ENGS:
            plan = []
            run = {}
            for fn, waits, sem, amt in p.ops[name]:
                if sem in eng_sems and amt:
                    run[sem] = run.get(sem, 0) + amt
                    post = sum(a for f2, w2, s2, a in p.ops[name] if s2 == sem) if False else None
                    plan.append(run[sem])
                else:
                    plan.append(None)
            plans[name] = plan

        def section(name):
            def go(eng):
                carried = {}
                plan = plans[name]
                for idx, (fn, waits, sem, amt) in enumerate(p.ops[name]):
                    for (s, v) in waits:
                        eng.wait_ge(sems[s], v)
                    if fn is not None:
                        inst = fn(eng)
                        if amt:
                            if sem in eng_sems:
                                carried[sem] = carried.get(sem, 0) + amt
                                if plan[idx] in needed.get(sem, ()):
                                    inst.then_inc(sems[sem], carried[sem])
                                    carried[sem] = 0
                            else:
                                inst.then_inc(sems[sem], amt)
            return go

        blk.tensor(section("tensor"))
        blk.vector(section("vector"))
        blk.scalar(section("scalar"))
        blk.sync(section("sync"))
        blk.gpsimd(section("gpsimd"))
    return nc


def _prep_inputs(inp):
    f = np.float32
    bf = ml_dtypes.bfloat16
    x = np.asarray(inp["L0_attn_x"], f)
    W1 = np.asarray(inp["L0_ffn_W1"], f)
    w1h = np.ascontiguousarray(
        W1.reshape(NE, 8, 128, 16, 2, 128).transpose(0, 3, 2, 4, 1, 5)
    ).reshape(NE, 16, 128, 2048)
    shared = {
        "w1h": w1h,
        "w2d": np.ascontiguousarray(np.asarray(inp["L0_ffn_W2"], f)).reshape(NE, 32, 128, D),
        "w_ind": np.ascontiguousarray(np.asarray(inp["L0_ffn_W_in"], f)).reshape(8, 128, D),
        "w_qd": np.ascontiguousarray(np.asarray(inp["L0_attn_Wq"], f)).reshape(8, 128, D),
        "w_vd": np.ascontiguousarray(np.asarray(inp["L0_attn_Wv"], f)).reshape(8, 128, D),
        "w_gd": np.ascontiguousarray(np.asarray(inp["L0_attn_Wg"], f)).reshape(8, 128, D),
        "w_od": np.ascontiguousarray(np.asarray(inp["L0_attn_Wo"], f)).reshape(8, 128, D),
        "w_outd": np.ascontiguousarray(np.asarray(inp["L0_ffn_W_out"], f)).reshape(8, 128, D),
        "w_gated": np.ascontiguousarray(np.asarray(inp["L0_ffn_W_gate"], f)).reshape(8, 128, 8),
        "b_in_pm": np.ascontiguousarray(np.asarray(inp["L0_ffn_b_in"], f).reshape(8, 128).T),
        "b1_pm": np.ascontiguousarray(
            np.asarray(inp["L0_ffn_b1"], f).reshape(NE, 32, 128).transpose(2, 0, 1)
        ).reshape(128, NE * 32),
        "b2_d": np.ascontiguousarray(np.asarray(inp["L0_ffn_b2"], f)),
        "b_out_d": np.asarray(inp["L0_ffn_b_out"], f).reshape(1, D).copy(),
        "b_gate_d": np.asarray(inp["L0_ffn_b_gate"], f).reshape(1, NE).copy(),
        "gamma_d": np.asarray(inp["L0_attn_ln_gamma"], f).reshape(1, D).copy(),
        "beta_d": np.asarray(inp["L0_attn_ln_beta"], f).reshape(1, D).copy(),
        "decay_c": np.asarray(inp["L0_attn_decay"], f).reshape(16, 1).copy(),
        "decay_r": np.asarray(inp["L0_attn_decay"], f).reshape(1, 16).copy(),
        "ident_bfd": np.eye(128, dtype=bf),
        "ident_fd": np.eye(128, dtype=f),
        "tri_d": np.triu(np.ones((128, 128), f)),
        "iota_rd": np.arange(128, dtype=f).reshape(1, 128),
        "iota1_rd": (np.arange(128, dtype=f) + 1).reshape(1, 128),
        "iotar_rd": (127 - np.arange(128, dtype=f)).reshape(1, 128),
        "ones_bfd": np.ones((1, 512), dtype=bf),
        "ones_fd": np.ones((1, 128), dtype=f),
        "cvals_d": np.tile(np.array(CONST_VALS + [0.0] * (16 - len(CONST_VALS)),
                                    dtype=f), (128, 1)),
    }
    in_maps = []
    for k in range(NCORES):
        b, s0 = k // 4, (k % 4) * T
        xo = np.ascontiguousarray(x[b, s0:s0 + T])
        xp = np.zeros((PRE, D), f)
        if s0 > 0:
            xp[PRE - s0:] = x[b, :s0]
        m = {"x_own": xo, "x_pre": xp}
        m.update(shared)
        in_maps.append(m)
    return in_maps


def kernel(**inputs):
    in_maps = _prep_inputs(inputs)
    nc = build_nc()
    res = run_bass_kernel_spmd(nc, in_maps, core_ids=list(range(NCORES)))
    out = np.empty((2, 8192, D), np.float32)
    for k in range(NCORES):
        b, s0 = k // 4, (k % 4) * T
        out[b, s0:s0 + T] = np.asarray(res.results[k]["y_out"], np.float32)
    return out


# version 26
# speedup vs baseline: 1.0903x; 1.0903x over previous
import numpy as np
import ml_dtypes

import concourse.bass as bass
import concourse.mybir as mybir
from concourse.bass_utils import run_bass_kernel_spmd

F32 = mybir.dt.float32
BF16 = mybir.dt.bfloat16
AF = mybir.ActivationFunctionType
ALU = mybir.AluOpType
AX = mybir.AxisListType

NCORES = 8
D = 1024
H = 16
HD = 64
DM = 4096
NE = 8
T = 2048
NT = T // 128
PRE = 6144
NPRE = PRE // 128
NPASS = 2
TP = T // NPASS
LN_EPS = 1e-5
BIGM = 60.0
BIGL = 1e9

_TAPS = False

CONST_VALS = [0.0, 1.0, LN_EPS, -BIGM, BIGM, 1.0 / D, -0.5, 128.0, -1.0, -BIGL]

ENGS = ("tensor", "vector", "scalar", "sync", "gpsimd")
ENG_SEM = {"tensor": "st", "vector": "sv", "scalar": "ss"}


class Prog:

    def __init__(self):
        self.ops = {e: [] for e in ENGS}
        self.cnt = {}
        self.hwm = {e: {} for e in ENGS}
        self.sem_names = []
        self.needed = {}
        self.regions = {}

    def sem(self, name):
        if name not in self.cnt:
            self.sem_names.append(name)
            self.cnt[name] = 0

    def region(self, name, space, lo, hi):
        self.regions[name] = [space, lo, hi, None, {}]

    def _wait(self, eng, waits, tok):
        if tok is None:
            return
        s, v = tok
        if v <= 0:
            return
        self.needed.setdefault(s, set()).add(v)
        if self.hwm[eng].get(s, 0) >= v:
            return
        self.hwm[eng][s] = v
        waits.append((s, v))

    def op(self, eng, fn, reads=(), writes=(), extra=(), sem=None, amt=1,
           skip_waw=False):
        sem = sem or ENG_SEM[eng]
        self.sem(sem)
        waits = []
        for (s, v) in extra:
            self._wait(eng, waits, (s, v))
        for r in reads:
            R = self.regions[r]
            if R[3] is not None:
                if not (eng == "tensor" and R[3][2] == "tensor"):
                    self._wait(eng, waits, (R[3][0], R[3][1]))
        for r in writes:
            R = self.regions[r]
            if R[3] is not None and not skip_waw:
                if not (eng == "tensor" and R[3][2] == "tensor"):
                    self._wait(eng, waits, (R[3][0], R[3][1]))
            for (e2, s2), v2 in R[4].items():
                if not (eng == "tensor" and e2 == "tensor"):
                    self._wait(eng, waits, (s2, v2))
            sp, lo, hi = R[0], R[1], R[2]
            for q, Q in self.regions.items():
                if q == r or Q[0] != sp or Q[2] <= lo or Q[1] >= hi:
                    continue
                if Q[3] is None and not Q[4]:
                    continue
                if Q[3] is not None:
                    if not (eng == "tensor" and Q[3][2] == "tensor"):
                        self._wait(eng, waits, (Q[3][0], Q[3][1]))
                for (e2, s2), v2 in Q[4].items():
                    if not (eng == "tensor" and e2 == "tensor"):
                        self._wait(eng, waits, (s2, v2))
                Q[3] = None
                Q[4] = {}
        self.ops[eng].append((fn, waits, sem, amt))
        self.cnt[sem] += amt
        post = self.cnt[sem]
        for r in writes:
            R = self.regions[r]
            R[3] = (sem, post, eng)
            R[4] = {}
        for r in reads:
            self.regions[r][4][(eng, sem)] = post
        return post


def build_nc():
    nc = bass.Bass("TRN2", target_bir_lowering=False, debug=False)
    p = Prog()

    def din(name, shape, dtype=F32):
        return nc.dram_tensor(name, list(shape), dtype, kind="ExternalInput").ap()

    x_own = din("x_own", (T, D))
    x_pre = din("x_pre", (PRE, D))
    w1h = din("w1h", (NE, 16, 128, 2048))
    w2d = din("w2d", (NE, 32, 128, D))
    w_ind = din("w_ind", (8, 128, D))
    w_qd = din("w_qd", (8, 128, D))
    w_vd = din("w_vd", (8, 128, D))
    w_gd = din("w_gd", (8, 128, D))
    w_od = din("w_od", (8, 128, D))
    w_outd = din("w_outd", (8, 128, D))
    w_gated = din("w_gated", (8, 128, 8))
    b_in_pm = din("b_in_pm", (128, 8))
    b1_pm = din("b1_pm", (128, NE * 32))
    b2_d = din("b2_d", (NE, D))
    b_out_d = din("b_out_d", (1, D))
    b_gate_d = din("b_gate_d", (1, NE))
    gamma_d = din("gamma_d", (1, D))
    beta_d = din("beta_d", (1, D))
    decay_c = din("decay_c", (16, 1))
    decay_r = din("decay_r", (1, 16))
    ident_bfd = din("ident_bfd", (128, 128), BF16)
    ident_fd = din("ident_fd", (128, 128))
    tri_d = din("tri_d", (128, 128))
    iota_rd = din("iota_rd", (1, 128))
    iota1_rd = din("iota1_rd", (1, 128))
    iotar_rd = din("iotar_rd", (1, 128))
    ones_bfd = din("ones_bfd", (1, 512), BF16)
    ones_fd = din("ones_fd", (1, 128))
    cvals_d = din("cvals_d", (128, 16))

    y_out = nc.dram_tensor("y_out", [T, D], F32, kind="ExternalOutput").ap()
    g_scr = nc.dram_tensor("g_scr", [T, D], BF16, kind="Internal").ap()
    p.region("g_scr", "dram", 0, T * D * 2)
    w2_bf = nc.dram_tensor("w2_bf", [NE, 32, 128, D], BF16, kind="Internal").ap()
    q_scr = nc.dram_tensor("q_scr", [T, D], BF16, kind="Internal").ap()
    v_scr = nc.dram_tensor("v_scr", [T, D], BF16, kind="Internal").ap()
    p.region("q_scr", "dram", 10 ** 9, 10 ** 9 + T * D * 2)
    p.region("v_scr", "dram", 10 ** 9 + T * D * 2, 10 ** 9 + 2 * T * D * 2)
    base = T * D * 2
    for e in range(NE):
        p.region("w2s%d" % e, "dram", base, base + 32 * 128 * D * 2)
        base += 32 * 128 * D * 2
    taps = {}
    if _TAPS:
        for nm, shape, dt in [
            ("tap_h", (128, 8 * T), BF16), ("tap_route", (128, NT * NE), F32),
            ("tap_moe", (128, 8 * D), BF16), ("tap_c", (1, D), F32),
            ("tap_yln", (128, NT * D), BF16), ("tap_g", (T, D), BF16),
            ("tap_stats", (128, 32), F32),
        ]:
            taps[nm] = nc.dram_tensor(nm, list(shape), dt, kind="ExternalOutput").ap()

    SBUF_BASE = 16896
    CAP = 211500
    _off = [0]
    aps = {}

    def salloc(name, shape, dtype, at=None):
        nbytes = int(np.prod(shape[1:])) * mybir.dt.size(dtype)
        if at is None:
            at = _off[0]
            _off[0] = (at + nbytes + 31) & ~31
        h = nc.alloc_sbuf_tensor_at(name, list(shape), dtype, offset=SBUF_BASE + at)
        p.region(name, "sbuf", at, at + nbytes)
        aps[name] = h.ap()
        return aps[name], at

    route, _ = salloc("route", (128, NT * NE), F32)
    routeT, _ = salloc("routeT", (128, T), BF16)
    c_row, _ = salloc("c_row", (1, D), F32)
    identb, _ = salloc("identb", (128, 128), BF16)
    identf, _ = salloc("identf", (128, 128), F32)
    iota, _ = salloc("iota", (1, 128), F32)
    iota1, _ = salloc("iota1", (1, 128), F32)
    iotar, _ = salloc("iotar", (1, 128), F32)
    onesb, _ = salloc("onesb", (1, 512), BF16)
    onesf, _ = salloc("onesf", (1, 128), F32)
    decayc, _ = salloc("decayc", (16, 1), F32)
    decayr, _ = salloc("decayr", (1, 16), F32)
    loglamc, _ = salloc("loglamc", (16, 1), F32)
    loglamr, _ = salloc("loglamr", (1, 16), F32)
    lam128c, _ = salloc("lam128c", (16, 1), F32)
    w0b, _ = salloc("w0b", (128, 16), BF16)
    binpm, _ = salloc("binpm", (128, 8), F32)
    b1pm, _ = salloc("b1pm", (128, NE * 32), F32)
    b2b, _ = salloc("b2b", (8, D), BF16)
    boutb, _ = salloc("boutb", (1, D), BF16)
    bgateb, _ = salloc("bgateb", (1, NE), BF16)
    wgateb, _ = salloc("wgateb", (128, 8 * NE), BF16)
    hfm, _ = salloc("hfm", (128, 8 * T), BF16)
    cvals, _ = salloc("cvals", (128, 16), F32)
    stats, STATS_AT = salloc("stats", (128, 24), F32)
    del p.regions["stats"]
    p.region("statsA", "sbuf", STATS_AT, STATS_AT + 32)
    p.region("statsR", "sbuf", STATS_AT + 32, STATS_AT + 64)
    p.region("statsB", "sbuf", STATS_AT + 64, STATS_AT + 96)
    stats_all, _ = salloc("stats_all", (128, 64), F32)
    SE = _off[0]

    _off[0] = SE
    x_st, xn_tm = [], []
    for i in range(2):
        a, _ = salloc(f"x_st{i}", (128, D), F32)
        x_st.append(a)
    for i in range(2):
        a, _ = salloc(f"xn_tm{i}", (128, D), BF16)
        xn_tm.append(a)
    w_inb, _ = salloc("w_inb", (128, 8 * D), BF16)
    xnfm_g, _ = salloc("xnfm_g", (128, 8 * 512), BF16)
    dumA, _ = salloc("dumA", (128, D), F32)
    t_sm, _ = salloc("t_sm", (128, 8), F32)
    ge1, _ = salloc("ge1", (128, 8), F32)
    ge2, _ = salloc("ge2", (128, 8), F32)
    wvT_a, _ = salloc("wvT_a", (128, 8 * D), BF16)
    assert _off[0] <= SE + 65536, _off[0]
    _off[0] = SE + 65536
    wvnat, _ = salloc("wvnat", (128, 8 * D), BF16)
    pre_st, prexn = [], []
    for i in range(2):
        a, _ = salloc(f"pre_st{i}", (128, D), F32)
        pre_st.append(a)
    for i in range(2):
        a, _ = salloc(f"prexn{i}", (128, D), BF16)
        prexn.append(a)
    A_acc, _ = salloc("A_acc", (16, D), F32)
    A_fm, _ = salloc("A_fm", (128, 128), BF16)
    assert _off[0] <= SE + 131072, _off[0]

    _off[0] = SE
    mid, MID_AT = salloc("mid", (128, 32 * TP), BF16)
    woutb, _ = salloc("woutb", (128, 8 * D), BF16, at=MID_AT)
    w2sb, W2_AT = salloc("w2sb", (128, 32 * D), BF16, at=SE + 65536)
    del p.regions["w2sb"]
    for qd in range(4):
        p.region("w2sb%d" % qd, "sbuf",
                 W2_AT + qd * 8 * D * 2, W2_AT + (qd + 1) * 8 * D * 2)
    _off[0] = SE + 131072
    w1sl = []
    for i in range(2):
        a, _ = salloc(f"w1sl{i}", (128, 2048), BF16)
        w1sl.append(a)
    moe_tm, _ = salloc("moe_tm", (128, 8 * D), BF16)
    moefm, _ = salloc("moefm", (128, D), BF16)
    g_stage, _ = salloc("g_stage", (128, D), BF16)
    assert _off[0] <= CAP, _off[0]

    _off[0] = SE
    x_stc, xn_tmc = [], []
    for i in range(2):
        a, _ = salloc(f"x_stc{i}", (128, D), F32)
        x_stc.append(a)
    for i in range(2):
        a, _ = salloc(f"xn_tmc{i}", (128, D), BF16)
        xn_tmc.append(a)
    xnfm_cc, _ = salloc("xnfm_cc", (128, D), BF16)
    wT1, _ = salloc("wT1", (128, 8 * D), BF16)
    wT2, _ = salloc("wT2", (128, 8 * D), BF16)
    wstage, _ = salloc("wstage", (128, 8 * D), BF16)
    yln_all, _ = salloc("yln_all", (128, NT * D), BF16)
    gamma_f, _ = salloc("gamma_f", (128, D), F32)
    beta_f, _ = salloc("beta_f", (128, D), F32)
    L_bf, _ = salloc("L_bf", (128, H * 128), BF16)
    tri, _ = salloc("tri", (128, 128), F32)
    v_sb, VSB_AT = salloc("v_sb", (128, D), BF16)
    q_sb, QSB_AT = salloc("q_sb", (128, D), F32)
    qs = [salloc("qs0", (128, D), BF16, at=QSB_AT)[0],
          salloc("qs1", (128, D), BF16, at=QSB_AT + 2048)[0]]
    y_sb, _ = salloc("y_sb", (128, D), F32)
    t1f, _ = salloc("t1f", (128, D), F32)
    gate_sb, _ = salloc("gate_sb", (128, D), BF16)
    y2fm, _ = salloc("y2fm", (128, D), BF16)
    g_st = []
    for i in range(2):
        a, _ = salloc(f"g_st{i}", (128, D), BF16)
        g_st.append(a)
    out_sb = []
    for i in range(2):
        a, _ = salloc(f"out_sb{i}", (128, D), F32)
        out_sb.append(a)
    dumC, DUMC_AT = salloc("dumC", (128, D), F32)
    vs = [v_sb, salloc("vs1", (128, D), BF16, at=DUMC_AT)[0]]
    VSNM = ["v_sb", "vs1"]
    y2fm2, _ = salloc("y2fm2", (128, D), BF16, at=DUMC_AT)
    powv, _ = salloc("powv", (1, H * 128), F32)
    gate_sb2, _ = salloc("gate_sb2", (128, D), BF16)
    y2b2, _ = salloc("y2b2", (128, D), BF16, at=DUMC_AT + 2048)
    r1row, _ = salloc("r1row", (1, 128), F32)
    r2row, _ = salloc("r2row", (1, 128), F32)
    assert _off[0] <= CAP, _off[0]

    PAIRS = []
    for j in range(4):
        h = nc.alloc_psum_tensor(f"pp{j}", [128, 1024], F32)
        PAIRS.append(h.ap())
    for i in range(8):
        p.region(f"pb{i}", "psum", i * 2048, (i + 1) * 2048)
    BANKS = [PAIRS[i // 2][:, (i % 2) * 512:(i % 2) * 512 + 512] for i in range(8)]
    BK = [f"pb{i}" for i in range(8)]
    PAIRSBF = [pr.bitcast(BF16) for pr in PAIRS]
    BANKSBF = [PAIRSBF[i // 2][:, (i % 2) * 1024:(i % 2) * 1024 + 1024]
               for i in range(8)]

    def dma(eng, out, in_, sem, reads=(), writes=(), extra=(), skip_waw=False):
        p.sem(sem)
        ex = [(sem, p.cnt[sem])] + list(extra)
        return p.op(eng, (lambda o, i: (lambda e: e.dma_start(out=o, in_=i)))(out, in_),
                    reads=reads, writes=writes, extra=ex, sem=sem, amt=16,
                    skip_waw=skip_waw)

    def mm(lhsT, rhs, out, start, stop, reads=(), writes=(), extra=()):
        return p.op("tensor",
                    (lambda l, r, o, b, e2: (lambda e: e.matmul(o, l, r, start=b, stop=e2)))
                    (lhsT, rhs, out, start, stop),
                    reads=reads, writes=writes, extra=extra)

    def tp(in_, out, ident, reads=(), writes=(), extra=()):
        return p.op("tensor",
                    (lambda i, o, d: (lambda e: e.transpose(o, i, d)))(in_, out, ident),
                    reads=reads, writes=writes, extra=extra)

    def vop(fn, reads=(), writes=(), extra=(), skip_waw=False):
        return p.op("vector", fn, reads=reads, writes=writes, extra=extra,
                    skip_waw=skip_waw)

    def sop(fn, reads=(), writes=(), extra=(), skip_waw=False):
        return p.op("scalar", fn, reads=list(reads) + ["cvals"], writes=writes,
                    extra=extra, skip_waw=skip_waw)

    for dst, src, nm in ((identb, ident_bfd, "identb"), (identf, ident_fd, "identf"),
                         (iota, iota_rd, "iota"), (iota1, iota1_rd, "iota1"),
                         (iotar, iotar_rd, "iotar"), (onesb, ones_bfd, "onesb"),
                         (onesf, ones_fd, "onesf"),
                         (decayc, decay_c, "decayc"), (decayr, decay_r, "decayr"),
                         (binpm, b_in_pm, "binpm"), (b1pm, b1_pm, "b1pm"),
                         (cvals, cvals_d, "cvals")):
        dma("sync", dst, src, "s_c", writes=[nm])
    for ci, cv in enumerate(CONST_VALS):
        nc.const_aps.register(F32, cv, cvals[:, ci:ci + 1])
    dma("gpsimd", b2b, b2_d, "s_w", writes=["b2b"])
    dma("gpsimd", boutb, b_out_d, "s_w", writes=["boutb"])
    dma("gpsimd", bgateb, b_gate_d, "s_w", writes=["bgateb"])
    dma("gpsimd", wgateb, w_gated.transpose([1, 0, 2]), "s_w", writes=["wgateb"])
    dma("gpsimd", w_inb, w_ind.transpose([1, 0, 2]), "s_w", writes=["w_inb"])
    dma("gpsimd", wvnat, w_vd.transpose([1, 0, 2]), "s_w", writes=["wvnat"])
    for e in range(2, 5):
        dma("gpsimd", w2_bf[e], w2d[e], "s_pc", writes=["w2s%d" % e])

    sop(lambda e: e.activation(loglamc, decayc, AF.Ln), reads=["decayc"], writes=["loglamc"])
    sop(lambda e: e.activation(loglamr, decayr, AF.Ln), reads=["decayr"], writes=["loglamr"])
    sop(lambda e: e.activation(lam128c, loglamc, AF.Exp, scale=128.0),
        reads=["loglamc"], writes=["lam128c"])
    mm(iotar[0:1, 0:128], loglamr, BANKS[0][0:128, 0:16], True, True,
       reads=["iotar", "loglamr"], writes=[BK[0]])
    sop(lambda e: e.activation(w0b, BANKS[0][0:128, 0:16], AF.Exp),
        reads=[BK[0]], writes=["w0b"])
    for ec in range(8):
        for dc in range(8):
            b = BK[4 + ((ec * 8 + dc) % 2)]
            tp(wvnat[:, ec * D + dc * 128: ec * D + dc * 128 + 128],
               BANKSBF[4 + ((ec * 8 + dc) % 2)][:, 0:128], identb,
               reads=["wvnat", "identb"], writes=[b])
            vop((lambda bb, ec_, dc_: (lambda e: e.tensor_copy(
                wvT_a[:, dc_ * D + ec_ * 128: dc_ * D + ec_ * 128 + 128],
                bb[:, 0:128])))(BANKSBF[4 + ((ec * 8 + dc) % 2)], ec, dc),
                reads=[b], writes=["wvT_a"], skip_waw=True)

    vop(lambda e: e.memset(A_acc, 0.0), writes=["A_acc"])
    vop(lambda e: e.memset(c_row, 0.0), writes=["c_row"])

    def emit_ln(src_ap, src_nm, dst_ap, dst_nm, mu_ap, inv_ap, statkeys,
                dum_ap, dum_nm, c0=0, statnm="statsA"):
        mu_nm, inv_nm = statkeys
        sop((lambda s, dd: (lambda e: e.activation(dd, s, AF.Copy,
            accum_out=stats[:, c0:c0 + 1])))(src_ap, dum_ap),
            reads=[src_nm], writes=[dum_nm, statnm])
        sop((lambda s, dd: (lambda e: e.activation(dd, s, AF.Square,
            accum_out=stats[:, c0 + 1:c0 + 2])))(src_ap, dum_ap),
            reads=[src_nm, statnm], writes=[dum_nm, statnm])
        sop((lambda m: (lambda e: e.activation(m, stats[:, c0:c0 + 1], AF.Copy,
            scale=1.0 / D)))(mu_ap), reads=[statnm], writes=[mu_nm], skip_waw=True)
        sop((lambda m: (lambda e: e.activation(stats[:, c0 + 3:c0 + 4], m,
            AF.Square)))(mu_ap), reads=[mu_nm, statnm], writes=[statnm])
        vop(lambda e: e.tensor_scalar(stats[:, c0 + 4:c0 + 5],
            stats[:, c0 + 1:c0 + 2], 1.0 / D, stats[:, c0 + 3:c0 + 4],
            ALU.mult, ALU.subtract), reads=[statnm], writes=[statnm])
        sop(lambda e: e.activation(stats[:, c0 + 5:c0 + 6], stats[:, c0 + 4:c0 + 5],
            AF.Ln, bias=LN_EPS), reads=[statnm], writes=[statnm])
        sop((lambda iv: (lambda e: e.activation(iv, stats[:, c0 + 5:c0 + 6], AF.Exp,
            scale=-0.5)))(inv_ap), reads=[statnm], writes=[inv_nm], skip_waw=True)
        if dst_ap is not None:
            vop((lambda s, o, m, iv: (lambda e: e.tensor_scalar(o, s, m, iv,
                ALU.subtract, ALU.mult)))(src_ap, dst_ap, mu_ap, inv_ap),
                reads=[src_nm, mu_nm, inv_nm], writes=[dst_nm])

    def emit_P_chunk(cix):
        par = cix % 2
        dma("sync", pre_st[par], x_pre[cix * 128:(cix + 1) * 128, :],
            "s_p%d" % par, writes=["pre_st%d" % par])
        c0, snm = (0, "statsA") if cix % 2 == 0 else (16, "statsB")
        emit_ln(pre_st[par], "pre_st%d" % par, prexn[par], "prexn%d" % par,
                stats[:, c0 + 2:c0 + 3], stats[:, c0 + 6:c0 + 7], (snm, snm),
                dumA, "dumA", c0=c0, statnm=snm)
        mm(w0b, prexn[par][:, 0:512], BANKS[6][0:16, :], True, True,
           reads=["w0b", "prexn%d" % par], writes=[BK[6]])
        mm(w0b, prexn[par][:, 512:1024], BANKS[7][0:16, :], True, True,
           reads=["w0b", "prexn%d" % par], writes=[BK[7]])
        vop(lambda e: e.scalar_tensor_tensor(A_acc, A_acc,
            lam128c, PAIRS[3][0:16, :], ALU.mult, ALU.add),
            reads=[BK[6], BK[7], "A_acc", "lam128c"], writes=["A_acc"])

    for t5 in range(4):
        for cpx in range(12 * t5, 12 * t5 + 12):
            emit_P_chunk(cpx)
        for i in range(4):
            t = t5 * 4 + i
            par = t % 2
            dma("sync", x_st[par], x_own[t * 128:(t + 1) * 128, :],
                "s_x%d" % par, writes=["x_st%d" % par])
            c0, snm = (0, "statsA") if t % 2 == 0 else (16, "statsB")
            emit_ln(x_st[par], "x_st%d" % par, xn_tm[par], "xn_tm%d" % par,
                    stats[:, c0 + 2:c0 + 3], stats[:, c0 + 6:c0 + 7], (snm, snm),
                    dumA, "dumA", c0=c0, statnm=snm)
            tb = 4 + (i % 2)
            for kc in range(8):
                tp(xn_tm[par][:, kc * 128:(kc + 1) * 128],
                   BANKSBF[tb][:, kc * 128:(kc + 1) * 128], identb,
                   reads=["xn_tm%d" % par, "identb"], writes=[BK[tb]])
            vop((lambda bb, i_: (lambda e: e.tensor_copy(
                xnfm_g[:, i_ * 1024:(i_ + 1) * 1024], bb)))(BANKSBF[tb], i),
                reads=[BK[tb]], writes=["xnfm_g"])
        xnv_g = xnfm_g.rearrange("p (i kc j) -> p kc i j", i=4, kc=8)
        for m in range(8):
            bk = BK[m % 4]
            for kc in range(8):
                mm(w_inb[:, kc * D + m * 128: kc * D + (m + 1) * 128],
                   xnv_g[:, kc],
                   BANKS[m % 4], kc == 0, kc == 7,
                   reads=["w_inb", "xnfm_g"], writes=[bk])
            sop((lambda bb, m_, t5_: (lambda e: e.activation(
                hfm[:, m_ * T + t5_ * 512: m_ * T + (t5_ + 1) * 512], bb,
                AF.Identity, bias=binpm[:, m_:m_ + 1])))(BANKS[m % 4], m, t5),
                reads=[bk, "binpm"], writes=["hfm"], skip_waw=True)
        for i in range(4):
            t = t5 * 4 + i
            for kc in range(8):
                mm(hfm[:, kc * T + t * 128: kc * T + (t + 1) * 128],
                   wgateb[:, kc * 8:(kc + 1) * 8],
                   BANKS[4][:, 0:8], kc == 0, False,
                   reads=["hfm", "wgateb"], writes=[BK[4]])
            mm(onesb[0:1, 0:128], bgateb, BANKS[4][:, 0:8], False, True,
               reads=["onesb", "bgateb"], writes=[BK[4]])
            vop(lambda e: e.reduce_max(stats[:, 8:9], BANKS[4][:, 0:8], AX.X),
                reads=[BK[4]], writes=["statsR"])
            vop(lambda e: e.tensor_scalar(ge1, BANKS[4][:, 0:8], stats[:, 8:9],
                None, ALU.is_ge), reads=[BK[4], "statsR"], writes=["ge1"])
            vop(lambda e: e.scalar_tensor_tensor(t_sm, ge1, -BIGL, BANKS[4][:, 0:8],
                ALU.mult, ALU.add), reads=["ge1", BK[4]], writes=["t_sm"])
            vop(lambda e: e.reduce_max(stats[:, 9:10], t_sm, AX.X),
                reads=["t_sm"], writes=["statsR"])
            vop(lambda e: e.tensor_scalar(ge2, BANKS[4][:, 0:8], stats[:, 9:10],
                None, ALU.is_ge), reads=[BK[4], "statsR"], writes=["ge2"])
            vop(lambda e: e.tensor_scalar(stats[:, 10:11], stats[:, 8:9], -1.0,
                None, ALU.mult), reads=["statsR"], writes=["statsR"])
            sop(lambda e: e.activation(t_sm, BANKS[4][:, 0:8], AF.Exp,
                bias=stats[:, 10:11]), reads=[BK[4], "statsR"], writes=["t_sm"])
            vop(lambda e: e.tensor_mul(t_sm, t_sm, ge2),
                reads=["t_sm", "ge2"], writes=["t_sm"])
            vop(lambda e: e.reduce_sum(stats[:, 11:12], t_sm, AX.X),
                reads=["t_sm"], writes=["statsR"])
            vop(lambda e: e.reciprocal(stats[:, 12:13], stats[:, 11:12]),
                reads=["statsR"], writes=["statsR"])
            vop((lambda t_: (lambda e: e.tensor_scalar(
                route[:, t_ * 8:(t_ + 1) * 8], t_sm, stats[:, 12:13], None,
                ALU.mult)))(t), reads=["t_sm", "statsR"], writes=["route"],
                skip_waw=True)
            tp(route[:, t * 8:(t + 1) * 8], BANKS[5][0:8, 0:128], identf,
               reads=["route", "identf"], writes=[BK[5]])
            vop((lambda t_: (lambda e: e.tensor_copy(
                routeT[0:8, t_ * 128:(t_ + 1) * 128], BANKS[5][0:8, 0:128])))(t),
                reads=[BK[5]], writes=["routeT"], skip_waw=True)
    for kc in range(8):
        tp(A_acc[:, kc * 128:(kc + 1) * 128], BANKS[4][:, kc * 16:(kc + 1) * 16],
           identf[0:16, 0:16], reads=["A_acc", "identf"], writes=[BK[4]])
    vop(lambda e: e.tensor_copy(A_fm, BANKS[4][:, 0:128]),
        reads=[BK[4]], writes=["A_fm"])
    for h in range(H):
        for kc in range(8):
            mm(A_fm[:, kc * 16 + h: kc * 16 + h + 1],
               wvT_a[:, kc * D + h * HD: kc * D + (h + 1) * HD],
               BANKS[5][0:1, 0:64], kc == 0, kc == 7,
               reads=["A_fm", "wvT_a"], writes=[BK[5]])
        vop((lambda h_: (lambda e: e.tensor_copy(
            c_row[0:1, h_ * HD:(h_ + 1) * HD], BANKS[5][0:1, 0:64])))(h),
            reads=[BK[5]], writes=["c_row"], skip_waw=True)
    if _TAPS:
        dma("sync", taps["tap_c"], c_row, "s_tap", reads=["c_row"])

    if _TAPS:
        dma("sync", taps["tap_h"], hfm, "s_tap", reads=["hfm"])
        dma("sync", taps["tap_route"], route, "s_tap", reads=["route"])

    w1_cnt = [0, 0]

    def load_w1(e, mg):
        slot = mg % 2
        w1_cnt[slot] = dma("gpsimd", w1sl[slot], w1h[e, mg], "s_w1%d" % slot,
                           writes=["w1sl%d" % slot])

    for pas in range(NPASS):
        poff = pas * TP
        for i in range(8):
            tglob = pas * 8 + i
            b0, b1 = BK[2 * (i % 4)], BK[2 * (i % 4) + 1]
            mm(routeT[0:8, tglob * 128:(tglob + 1) * 128], b2b[0:8, 0:512],
               BANKS[2 * (i % 4)], True, True, reads=["routeT", "b2b"], writes=[b0])
            mm(routeT[0:8, tglob * 128:(tglob + 1) * 128], b2b[0:8, 512:1024],
               BANKS[2 * (i % 4) + 1], True, True, reads=["routeT", "b2b"], writes=[b1])
            vop((lambda pp, i_: (lambda e: e.tensor_copy(
                moe_tm[:, i_ * D:(i_ + 1) * D], pp)))(PAIRS[(i % 4)], i),
                reads=[b0, b1], writes=["moe_tm"], skip_waw=True)
        for e in range(NE):
            load_w1(e, 0)
            load_w1(e, 1)
            for qd in range(4):
                if e >= 2:
                    dma("sync", w2sb[:, qd * 8 * D:(qd + 1) * 8 * D],
                        w2_bf[e, 8 * qd:8 * qd + 8].transpose([1, 0, 2]),
                        "s_w2%d" % (qd % 2), reads=["w2s%d" % e],
                        writes=["w2sb%d" % qd])
                else:
                    dma("gpsimd", w2sb[:, qd * 8 * D:(qd + 1) * 8 * D],
                        w2d[e, 8 * qd:8 * qd + 8].transpose([1, 0, 2]),
                        "s_w2g%d" % (qd % 2), writes=["w2sb%d" % qd])
            for mg in range(16):
                slot = mg % 2
                bsel = 0 if mg % 2 == 0 else 4
                for mi in range(2):
                    for kc in range(8):
                        for th in range(2):
                            bk = BK[bsel + 2 * mi + th]
                            mm(w1sl[slot][:, (mi * 8 + kc) * 128:(mi * 8 + kc + 1) * 128],
                               hfm[:, kc * T + poff + th * 512: kc * T + poff + (th + 1) * 512],
                               BANKS[bsel + 2 * mi + th], kc == 0, kc == 7,
                               reads=["w1sl%d" % slot, "hfm"], writes=[bk])
                if mg < 14:
                    load_w1(e, mg + 2)
                for mi in range(2):
                    m = mg * 2 + mi
                    pr2 = (bsel + 2 * mi) // 2
                    sop((lambda pp, m_, e_: (lambda eng: eng.activation(
                        mid[:, m_ * TP:(m_ + 1) * TP], pp, AF.Relu,
                        bias=b1pm[:, e_ * 32 + m_: e_ * 32 + m_ + 1])))
                        (PAIRS[pr2], m, e),
                        reads=[BK[bsel + 2 * mi], BK[bsel + 2 * mi + 1], "b1pm"],
                        writes=["mid"], skip_waw=True)
            for i in range(8):
                tglob = pas * 8 + i
                b0, b1 = BK[2 * (i % 4)], BK[2 * (i % 4) + 1]
                for kc in range(32):
                    w2r = "w2sb%d" % (kc // 8)
                    mm(mid[:, kc * TP + i * 128: kc * TP + (i + 1) * 128],
                       w2sb[:, kc * D: kc * D + 512],
                       BANKS[2 * (i % 4)], kc == 0, kc == 31,
                       reads=["mid", w2r], writes=[b0])
                    mm(mid[:, kc * TP + i * 128: kc * TP + (i + 1) * 128],
                       w2sb[:, kc * D + 512:(kc + 1) * D],
                       BANKS[2 * (i % 4) + 1], kc == 0, kc == 31,
                       reads=["mid", w2r], writes=[b1])
                vop((lambda pp, i_, e_, tg: (lambda eng: eng.scalar_tensor_tensor(
                    moe_tm[:, i_ * D:(i_ + 1) * D], pp,
                    route[:, tg * 8 + e_: tg * 8 + e_ + 1],
                    moe_tm[:, i_ * D:(i_ + 1) * D], ALU.mult, ALU.add)))
                    (PAIRS[(i % 4)], i, e, tglob),
                    reads=[b0, b1, "route", "moe_tm"], writes=["moe_tm"])
            if pas == 0 and e < 3:
                dma("gpsimd", w2_bf[e + 5], w2d[e + 5], "s_pc",
                    writes=["w2s%d" % (e + 5)])
        if _TAPS and pas == 0:
            dma("sync", taps["tap_moe"], moe_tm, "s_tap", reads=["moe_tm"])
        dma("gpsimd", woutb, w_outd.transpose([1, 0, 2]), "s_wo", writes=["woutb"])
        for i in range(8):
            tglob = pas * 8 + i
            for j in range(8):
                tp(moe_tm[:, i * D + j * 128: i * D + (j + 1) * 128],
                   BANKSBF[4][:, j * 128:(j + 1) * 128], identb,
                   reads=["moe_tm", "identb"], writes=[BK[4]])
            vop(lambda e: e.tensor_copy(moefm, BANKSBF[4]),
                reads=[BK[4]], writes=["moefm"])
            for th in range(2):
                for j in range(8):
                    mm(moefm[:, j * 128:(j + 1) * 128],
                       woutb[:, j * D + th * 512: j * D + (th + 1) * 512],
                       BANKS[th], j == 0, False,
                       reads=["moefm", "woutb"], writes=[BK[th]])
                mm(onesb[0:1, 0:128], boutb[0:1, th * 512:(th + 1) * 512],
                   BANKS[th], False, True,
                   reads=["onesb", "boutb"], writes=[BK[th]])
                sop((lambda bb, th_: (lambda e: e.activation(
                    g_stage[:, th_ * 512:(th_ + 1) * 512], bb, AF.Silu)))
                    (BANKS[th], th),
                    reads=[BK[th]], writes=["g_stage"], skip_waw=(th == 1))
            dma("sync", g_scr[tglob * 128:(tglob + 1) * 128, :], g_stage, "s_gw",
                reads=["g_stage"], writes=["g_scr"], skip_waw=True)

    dma("sync", gamma_f[0:1, :], gamma_d, "s_c", writes=["gamma_f"])
    dma("sync", beta_f[0:1, :], beta_d, "s_c", writes=["beta_f"])
    dma("sync", tri, tri_d, "s_c", writes=["tri"])
    for th in range(2):
        mm(onesf[0:1, 0:128], gamma_f[0:1, th * 512:(th + 1) * 512],
           BANKS[4 + th], True, True, reads=["onesf", "gamma_f"], writes=[BK[4 + th]])
        vop((lambda bb, th_: (lambda e: e.tensor_copy(
            gamma_f[:, th_ * 512:(th_ + 1) * 512], bb)))(BANKS[4 + th], th),
            reads=[BK[4 + th]], writes=["gamma_f"], skip_waw=True)
    for th in range(2):
        mm(onesf[0:1, 0:128], beta_f[0:1, th * 512:(th + 1) * 512],
           BANKS[4 + th], True, True, reads=["onesf", "beta_f"], writes=[BK[4 + th]])
        vop((lambda bb, th_: (lambda e: e.tensor_copy(
            beta_f[:, th_ * 512:(th_ + 1) * 512], bb)))(BANKS[4 + th], th),
            reads=[BK[4 + th]], writes=["beta_f"], skip_waw=True)
    for h in range(H):
        vop((lambda h_: (lambda e: e.tensor_scalar(r1row, iota,
            loglamr[0:1, h_:h_ + 1], None, ALU.mult)))(h),
            reads=["iota", "loglamr"], writes=["r1row"])
        vop((lambda h_: (lambda e: e.tensor_scalar(r2row, iota,
            loglamr[0:1, h_:h_ + 1], -1.0, ALU.mult, ALU.mult)))(h),
            reads=["iota", "loglamr"], writes=["r2row"])
        mm(onesf[0:1, 0:128], r1row, BANKS[6][:, 0:128], True, False,
           reads=["onesf", "r1row"], writes=[BK[6]])
        mm(r2row, onesf[0:1, 0:128], BANKS[6][:, 0:128], False, True,
           reads=["r2row", "onesf"], writes=[BK[6]])
        vop(lambda e: e.scalar_tensor_tensor(t1f[:, 0:128], tri, BIGM,
            BANKS[6][:, 0:128], ALU.mult, ALU.add),
            reads=["tri", BK[6]], writes=["t1f"])
        sop((lambda h_: (lambda e: e.activation(L_bf[:, h_ * 128:(h_ + 1) * 128],
            t1f[:, 0:128], AF.Exp, bias=-BIGM)))(h),
            reads=["t1f"], writes=["L_bf"], skip_waw=True)
        vop((lambda h_: (lambda e: e.tensor_scalar(
            powv[0:1, h_ * 128:(h_ + 1) * 128], iota1,
            loglamr[0:1, h_:h_ + 1], None, ALU.mult)))(h),
            reads=["iota1", "loglamr"], writes=["powv"], skip_waw=True)
    sop(lambda e: e.activation(powv, powv, AF.Exp), reads=["powv"], writes=["powv"])
    dma("gpsimd", wstage, w_qd.transpose([1, 0, 2]), "s_w", writes=["wstage"])
    for dst, dnm, nxt in ((wT1, "wT1", w_vd), (wT2, "wT2", None)):
        for dc in range(8):
            wb = 4 + (dc % 2)
            for ec in range(8):
                tp(wstage[:, ec * D + dc * 128: ec * D + dc * 128 + 128],
                   BANKSBF[wb][:, ec * 128:(ec + 1) * 128], identb,
                   reads=["wstage", "identb"], writes=[BK[wb]])
            vop((lambda bb, dc_, d_: (lambda e: e.tensor_copy(
                d_[:, dc_ * D:(dc_ + 1) * D], bb)))(BANKSBF[wb], dc, dst),
                reads=[BK[wb]], writes=[dnm], skip_waw=True)
        if nxt is not None:
            dma("gpsimd", wstage, nxt.transpose([1, 0, 2]), "s_w", writes=["wstage"])

    for t in range(NT):
        par = t % 2
        dma("sync", x_stc[par], x_own[t * 128:(t + 1) * 128, :],
            "s_x%d" % par, writes=["x_stc%d" % par])
        c0, snm = (0, "statsA") if t % 2 == 0 else (16, "statsB")
        emit_ln(x_stc[par], "x_stc%d" % par, None, None,
                stats_all[:, t:t + 1], stats_all[:, 16 + t:17 + t],
                ("stats_all", "stats_all"), dumC, "dumC", c0=c0, statnm=snm)

    def ca_pre(t):
        par = t % 2
        dma("sync", x_stc[par], x_own[t * 128:(t + 1) * 128, :],
            "s_x%d" % par, writes=["x_stc%d" % par])
        vop((lambda s, o, t_: (lambda e: e.tensor_scalar(o, s,
            stats_all[:, t_:t_ + 1], stats_all[:, 16 + t_:17 + t_],
            ALU.subtract, ALU.mult)))(x_stc[par], xn_tmc[par], t),
            reads=["x_stc%d" % par, "stats_all"], writes=["xn_tmc%d" % par])

    def ca_project(t):
        par = t % 2
        for kc in range(8):
            tp(xn_tmc[par][:, kc * 128:(kc + 1) * 128],
               BANKSBF[0][:, kc * 128:(kc + 1) * 128], identb,
               reads=["xn_tmc%d" % par, "identb"], writes=[BK[0]])
        vop(lambda e: e.tensor_copy(xnfm_cc, BANKSBF[0]),
            reads=[BK[0]], writes=["xnfm_cc"])
        for kc in range(8):
            mm(xnfm_cc[:, kc * 128:(kc + 1) * 128], wT1[:, kc * D: kc * D + 512],
               BANKS[2], kc == 0, kc == 7, reads=["xnfm_cc", "wT1"], writes=[BK[2]])
            mm(xnfm_cc[:, kc * 128:(kc + 1) * 128], wT1[:, kc * D + 512:(kc + 1) * D],
               BANKS[3], kc == 0, kc == 7, reads=["xnfm_cc", "wT1"], writes=[BK[3]])
        for kc in range(8):
            mm(xnfm_cc[:, kc * 128:(kc + 1) * 128], wT2[:, kc * D: kc * D + 512],
               BANKS[4], kc == 0, kc == 7, reads=["xnfm_cc", "wT2"], writes=[BK[4]])
            mm(xnfm_cc[:, kc * 128:(kc + 1) * 128], wT2[:, kc * D + 512:(kc + 1) * D],
               BANKS[5], kc == 0, kc == 7, reads=["xnfm_cc", "wT2"], writes=[BK[5]])
        vop((lambda q: (lambda e: e.tensor_copy(q, PAIRS[1])))(qs[par]),
            reads=[BK[2], BK[3]], writes=["qs%d" % par])
        vop((lambda v: (lambda e: e.tensor_copy(v, PAIRS[2])))(vs[par]),
            reads=[BK[4], BK[5]], writes=[VSNM[par]])

    def ca_scan(t):
        par = t % 2
        for h in range(H):
            bsl = BANKS[6 + (h // 8)][:, (h % 8) * HD:((h % 8) + 1) * HD]
            mm(L_bf[:, h * 128:(h + 1) * 128], vs[par][:, h * HD:(h + 1) * HD],
               bsl, h % 8 == 0, False,
               reads=["L_bf", VSNM[par]], writes=[BK[6 + (h // 8)]])
        for h in range(H):
            bsl = BANKS[6 + (h // 8)][:, (h % 8) * HD:((h % 8) + 1) * HD]
            mm(powv[0:1, h * 128:(h + 1) * 128], c_row[0:1, h * HD:(h + 1) * HD],
               bsl, False, h % 8 == 7,
               reads=["powv", "c_row"], writes=[BK[6 + (h // 8)]])
        for h in range(H):
            bq = 2 + (h // 8)
            bsl = BANKS[bq][0:1, (h % 8) * HD:((h % 8) + 1) * HD]
            mm(w0b[:, h:h + 1], vs[par][:, h * HD:(h + 1) * HD],
               bsl, h % 8 == 0, False, reads=["w0b", VSNM[par]], writes=[BK[bq]])
        for h in range(H):
            bq = 2 + (h // 8)
            bsl = BANKS[bq][0:1, (h % 8) * HD:((h % 8) + 1) * HD]
            mm(powv[0:1, h * 128 + 127: h * 128 + 128],
               c_row[0:1, h * HD:(h + 1) * HD],
               bsl, False, h % 8 == 7, reads=["powv", "c_row"], writes=[BK[bq]])
        vop((lambda q: (lambda e: e.tensor_mul(y_sb, q, PAIRS[3])))(qs[par]),
            reads=["qs%d" % par, BK[6], BK[7]], writes=["y_sb"])
        vop(lambda e: e.tensor_copy(c_row, PAIRS[1][0:1, :]),
            reads=[BK[2], BK[3]], writes=["c_row"])
        sop(lambda e: e.activation(t1f, y_sb, AF.Copy, accum_out=stats[:, 16:17]),
            reads=["y_sb"], writes=["t1f", "statsB"])
        sop(lambda e: e.activation(t1f, y_sb, AF.Square, accum_out=stats[:, 17:18]),
            reads=["y_sb", "statsB"], writes=["t1f", "statsB"])
        sop(lambda e: e.activation(stats[:, 18:19], stats[:, 16:17], AF.Copy,
            scale=1.0 / D), reads=["statsB"], writes=["statsB"])
        sop(lambda e: e.activation(stats[:, 19:20], stats[:, 18:19], AF.Square),
            reads=["statsB"], writes=["statsB"])
        vop(lambda e: e.tensor_scalar(stats[:, 20:21], stats[:, 17:18], 1.0 / D,
            stats[:, 19:20], ALU.mult, ALU.subtract), reads=["statsB"],
            writes=["statsB"])
        sop(lambda e: e.activation(stats[:, 21:22], stats[:, 20:21], AF.Ln,
            bias=LN_EPS), reads=["statsB"], writes=["statsB"])
        sop(lambda e: e.activation(stats[:, 22:23], stats[:, 21:22], AF.Exp,
            scale=-0.5), reads=["statsB"], writes=["statsB"])
        vop(lambda e: e.scalar_tensor_tensor(t1f, y_sb, stats[:, 18:19], gamma_f,
            ALU.subtract, ALU.mult), reads=["y_sb", "statsB", "gamma_f"],
            writes=["t1f"])
        vop((lambda t_: (lambda e: e.scalar_tensor_tensor(
            yln_all[:, t_ * D:(t_ + 1) * D], t1f, stats[:, 22:23], beta_f,
            ALU.mult, ALU.add)))(t), reads=["t1f", "statsB", "beta_f"],
            writes=["yln_all"], skip_waw=True)

    ca_pre(0)
    ca_pre(1)
    ca_project(0)
    for t in range(1, NT):
        if t + 1 < NT:
            ca_pre(t + 1)
        ca_project(t)
        ca_scan(t - 1)
    ca_scan(NT - 1)
    if _TAPS:
        dma("sync", taps["tap_yln"], yln_all, "s_tap", reads=["yln_all"])
        dma("sync", taps["tap_stats"], stats_all[:, 0:32], "s_tap", reads=["stats_all"])

    dma("gpsimd", wstage, w_gd.transpose([1, 0, 2]), "s_w", writes=["wstage"])
    for dst, dnm, nxt in ((wT1, "wT1", w_od), (wT2, "wT2", None)):
        for dc in range(8):
            wb = 4 + (dc % 2)
            for ec in range(8):
                tp(wstage[:, ec * D + dc * 128: ec * D + dc * 128 + 128],
                   BANKSBF[wb][:, ec * 128:(ec + 1) * 128], identb,
                   reads=["wstage", "identb"], writes=[BK[wb]])
            vop((lambda bb, dc_, d_: (lambda e: e.tensor_copy(
                d_[:, dc_ * D:(dc_ + 1) * D], bb)))(BANKSBF[wb], dc, dst),
                reads=[BK[wb]], writes=[dnm], skip_waw=True)
        if nxt is not None:
            dma("gpsimd", wstage, nxt.transpose([1, 0, 2]), "s_w", writes=["wstage"])

    def cb_pre(t):
        par = t % 2
        dma("sync", x_stc[par], x_own[t * 128:(t + 1) * 128, :],
            "s_x%d" % par, writes=["x_stc%d" % par])
        dma("sync", g_st[par], g_scr[t * 128:(t + 1) * 128, :],
            "s_g%d" % par, reads=["g_scr"], writes=["g_st%d" % par])
        vop((lambda s, o, t_: (lambda e: e.tensor_scalar(o, s,
            stats_all[:, t_:t_ + 1], stats_all[:, 16 + t_:17 + t_],
            ALU.subtract, ALU.mult)))(x_stc[par], xn_tmc[par], t),
            reads=["x_stc%d" % par, "stats_all"], writes=["xn_tmc%d" % par])

    cb_pre(0)
    for t in range(NT):
        par = t % 2
        if t + 1 < NT:
            cb_pre(t + 1)
        for kc in range(8):
            tp(xn_tmc[par][:, kc * 128:(kc + 1) * 128],
               BANKSBF[4][:, kc * 128:(kc + 1) * 128], identb,
               reads=["xn_tmc%d" % par, "identb"], writes=[BK[4]])
        vop(lambda e: e.tensor_copy(xnfm_cc, BANKSBF[4]),
            reads=[BK[4]], writes=["xnfm_cc"])
        gb = 0 if t % 2 == 0 else 2
        yb = 2 if t % 2 == 0 else 0
        gs_ap, gs_nm = (gate_sb, "gate_sb") if t % 2 == 0 else (gate_sb2, "gate_sb2")
        y2_ap, y2_nm = (v_sb, "v_sb") if t % 2 == 0 else (y2b2, "y2b2")
        yf_ap, yf_nm = (y2fm, "y2fm") if t % 2 == 0 else (y2fm2, "y2fm2")
        for kc in range(8):
            mm(xnfm_cc[:, kc * 128:(kc + 1) * 128], wT1[:, kc * D: kc * D + 512],
               BANKS[gb], kc == 0, kc == 7, reads=["xnfm_cc", "wT1"], writes=[BK[gb]])
            mm(xnfm_cc[:, kc * 128:(kc + 1) * 128], wT1[:, kc * D + 512:(kc + 1) * D],
               BANKS[gb + 1], kc == 0, kc == 7, reads=["xnfm_cc", "wT1"],
               writes=[BK[gb + 1]])
        sop((lambda g, pp: (lambda e: e.activation(g, pp, AF.Silu)))
            (gs_ap, PAIRS[gb // 2]), reads=[BK[gb], BK[gb + 1]], writes=[gs_nm])
        vop((lambda t_, g, y2: (lambda e: e.tensor_mul(y2,
            yln_all[:, t_ * D:(t_ + 1) * D], g)))(t, gs_ap, y2_ap),
            reads=["yln_all", gs_nm], writes=[y2_nm])
        for kc in range(8):
            tp(y2_ap[:, kc * 128:(kc + 1) * 128],
               BANKSBF[yb][:, kc * 128:(kc + 1) * 128],
               identb, reads=[y2_nm, "identb"], writes=[BK[yb]])
        vop((lambda yf, bb: (lambda e: e.tensor_copy(yf, bb)))(yf_ap, BANKSBF[yb]),
            reads=[BK[yb]], writes=[yf_nm])
        for kc in range(8):
            mm(yf_ap[:, kc * 128:(kc + 1) * 128], wT2[:, kc * D: kc * D + 512],
               BANKS[6], kc == 0, kc == 7, reads=[yf_nm, "wT2"], writes=[BK[6]])
            mm(yf_ap[:, kc * 128:(kc + 1) * 128], wT2[:, kc * D + 512:(kc + 1) * D],
               BANKS[7], kc == 0, kc == 7, reads=[yf_nm, "wT2"], writes=[BK[7]])
        ob = out_sb[t % 2]
        onm = "out_sb%d" % (t % 2)
        vop((lambda o, gp: (lambda e: e.tensor_mul(o, PAIRS[3], gp)))
            (ob, g_st[par]),
            reads=[BK[6], BK[7], "g_st%d" % par], writes=[onm])
        vop((lambda o, xb: (lambda e: e.tensor_add(o, o, xb)))(ob, x_stc[par]),
            reads=[onm, "x_stc%d" % par], writes=[onm])
        dma("sync", y_out[t * 128:(t + 1) * 128, :], ob, "s_o%d" % (t % 2),
            reads=[onm])

    fin = [("s_o0", p.cnt.get("s_o0", 0)), ("s_o1", p.cnt.get("s_o1", 0))]
    if _TAPS:
        fin.append(("s_tap", p.cnt.get("s_tap", 0)))
    p.op("sync", None, extra=fin, sem="s_fin", amt=0)

    import contextlib
    with contextlib.ExitStack() as stack:
        sems = {}
        for s in p.sem_names:
            sems[s] = stack.enter_context(nc.semaphore(s))
        blk = stack.enter_context(nc.Block())

        eng_sems = set(ENG_SEM.values())
        needed = {s: v for s, v in p.needed.items() if s in eng_sems}
        plans = {}
        for name in ENGS:
            plan = []
            run = {}
            for fn, waits, sem, amt in p.ops[name]:
                if sem in eng_sems and amt:
                    run[sem] = run.get(sem, 0) + amt
                    post = sum(a for f2, w2, s2, a in p.ops[name] if s2 == sem) if False else None
                    plan.append(run[sem])
                else:
                    plan.append(None)
            plans[name] = plan

        def section(name):
            def go(eng):
                carried = {}
                plan = plans[name]
                for idx, (fn, waits, sem, amt) in enumerate(p.ops[name]):
                    for (s, v) in waits:
                        eng.wait_ge(sems[s], v)
                    if fn is not None:
                        inst = fn(eng)
                        if amt:
                            if sem in eng_sems:
                                carried[sem] = carried.get(sem, 0) + amt
                                if plan[idx] in needed.get(sem, ()):
                                    inst.then_inc(sems[sem], carried[sem])
                                    carried[sem] = 0
                            else:
                                inst.then_inc(sems[sem], amt)
            return go

        blk.tensor(section("tensor"))
        blk.vector(section("vector"))
        blk.scalar(section("scalar"))
        blk.sync(section("sync"))
        blk.gpsimd(section("gpsimd"))
    return nc


def _prep_inputs(inp):
    f = np.float32
    bf = ml_dtypes.bfloat16
    x = np.asarray(inp["L0_attn_x"], f)
    W1 = np.asarray(inp["L0_ffn_W1"], f)
    w1h = np.ascontiguousarray(
        W1.reshape(NE, 8, 128, 16, 2, 128).transpose(0, 3, 2, 4, 1, 5)
    ).reshape(NE, 16, 128, 2048)
    shared = {
        "w1h": w1h,
        "w2d": np.ascontiguousarray(np.asarray(inp["L0_ffn_W2"], f)).reshape(NE, 32, 128, D),
        "w_ind": np.ascontiguousarray(np.asarray(inp["L0_ffn_W_in"], f)).reshape(8, 128, D),
        "w_qd": np.ascontiguousarray(np.asarray(inp["L0_attn_Wq"], f)).reshape(8, 128, D),
        "w_vd": np.ascontiguousarray(np.asarray(inp["L0_attn_Wv"], f)).reshape(8, 128, D),
        "w_gd": np.ascontiguousarray(np.asarray(inp["L0_attn_Wg"], f)).reshape(8, 128, D),
        "w_od": np.ascontiguousarray(np.asarray(inp["L0_attn_Wo"], f)).reshape(8, 128, D),
        "w_outd": np.ascontiguousarray(np.asarray(inp["L0_ffn_W_out"], f)).reshape(8, 128, D),
        "w_gated": np.ascontiguousarray(np.asarray(inp["L0_ffn_W_gate"], f)).reshape(8, 128, 8),
        "b_in_pm": np.ascontiguousarray(np.asarray(inp["L0_ffn_b_in"], f).reshape(8, 128).T),
        "b1_pm": np.ascontiguousarray(
            np.asarray(inp["L0_ffn_b1"], f).reshape(NE, 32, 128).transpose(2, 0, 1)
        ).reshape(128, NE * 32),
        "b2_d": np.ascontiguousarray(np.asarray(inp["L0_ffn_b2"], f)),
        "b_out_d": np.asarray(inp["L0_ffn_b_out"], f).reshape(1, D).copy(),
        "b_gate_d": np.asarray(inp["L0_ffn_b_gate"], f).reshape(1, NE).copy(),
        "gamma_d": np.asarray(inp["L0_attn_ln_gamma"], f).reshape(1, D).copy(),
        "beta_d": np.asarray(inp["L0_attn_ln_beta"], f).reshape(1, D).copy(),
        "decay_c": np.asarray(inp["L0_attn_decay"], f).reshape(16, 1).copy(),
        "decay_r": np.asarray(inp["L0_attn_decay"], f).reshape(1, 16).copy(),
        "ident_bfd": np.eye(128, dtype=bf),
        "ident_fd": np.eye(128, dtype=f),
        "tri_d": np.triu(np.ones((128, 128), f)),
        "iota_rd": np.arange(128, dtype=f).reshape(1, 128),
        "iota1_rd": (np.arange(128, dtype=f) + 1).reshape(1, 128),
        "iotar_rd": (127 - np.arange(128, dtype=f)).reshape(1, 128),
        "ones_bfd": np.ones((1, 512), dtype=bf),
        "ones_fd": np.ones((1, 128), dtype=f),
        "cvals_d": np.tile(np.array(CONST_VALS + [0.0] * (16 - len(CONST_VALS)),
                                    dtype=f), (128, 1)),
    }
    in_maps = []
    for k in range(NCORES):
        b, s0 = k // 4, (k % 4) * T
        xo = np.ascontiguousarray(x[b, s0:s0 + T])
        xp = np.zeros((PRE, D), f)
        if s0 > 0:
            xp[PRE - s0:] = x[b, :s0]
        m = {"x_own": xo, "x_pre": xp}
        m.update(shared)
        in_maps.append(m)
    return in_maps


def kernel(**inputs):
    in_maps = _prep_inputs(inputs)
    nc = build_nc()
    res = run_bass_kernel_spmd(nc, in_maps, core_ids=list(range(NCORES)))
    out = np.empty((2, 8192, D), np.float32)
    for k in range(NCORES):
        b, s0 = k // 4, (k % 4) * T
        out[b, s0:s0 + T] = np.asarray(res.results[k]["y_out"], np.float32)
    return out


# version 28
# speedup vs baseline: 1.1226x; 1.0296x over previous
import numpy as np
import ml_dtypes

import concourse.bass as bass
import concourse.mybir as mybir
from concourse.bass_utils import run_bass_kernel_spmd

F32 = mybir.dt.float32
BF16 = mybir.dt.bfloat16
AF = mybir.ActivationFunctionType
ALU = mybir.AluOpType
AX = mybir.AxisListType

NCORES = 8
D = 1024
H = 16
HD = 64
DM = 4096
NE = 8
T = 2048
NT = T // 128
PRE = 6144
NPRE = PRE // 128
NPASS = 2
TP = T // NPASS
LN_EPS = 1e-5
BIGM = 60.0
BIGL = 1e9

_TAPS = False

CONST_VALS = [0.0, 1.0, LN_EPS, -BIGM, BIGM, 1.0 / D, -0.5, 128.0, -1.0, -BIGL]

ENGS = ("tensor", "vector", "scalar", "sync", "gpsimd")
ENG_SEM = {"tensor": "st", "vector": "sv", "scalar": "ss"}


class Prog:

    def __init__(self):
        self.ops = {e: [] for e in ENGS}
        self.cnt = {}
        self.hwm = {e: {} for e in ENGS}
        self.sem_names = []
        self.needed = {}
        self.regions = {}

    def sem(self, name):
        if name not in self.cnt:
            self.sem_names.append(name)
            self.cnt[name] = 0

    def region(self, name, space, lo, hi):
        self.regions[name] = [space, lo, hi, None, {}]

    def _wait(self, eng, waits, tok):
        if tok is None:
            return
        s, v = tok
        if v <= 0:
            return
        self.needed.setdefault(s, set()).add(v)
        if self.hwm[eng].get(s, 0) >= v:
            return
        self.hwm[eng][s] = v
        waits.append((s, v))

    def op(self, eng, fn, reads=(), writes=(), extra=(), sem=None, amt=1,
           skip_waw=False):
        sem = sem or ENG_SEM[eng]
        self.sem(sem)
        waits = []
        for (s, v) in extra:
            self._wait(eng, waits, (s, v))
        for r in reads:
            R = self.regions[r]
            if R[3] is not None:
                if not (eng == "tensor" and R[3][2] == "tensor"):
                    self._wait(eng, waits, (R[3][0], R[3][1]))
        for r in writes:
            R = self.regions[r]
            if R[3] is not None and not skip_waw:
                if not (eng == "tensor" and R[3][2] == "tensor"):
                    self._wait(eng, waits, (R[3][0], R[3][1]))
            for (e2, s2), v2 in R[4].items():
                if not (eng == "tensor" and e2 == "tensor"):
                    self._wait(eng, waits, (s2, v2))
            sp, lo, hi = R[0], R[1], R[2]
            for q, Q in self.regions.items():
                if q == r or Q[0] != sp or Q[2] <= lo or Q[1] >= hi:
                    continue
                if Q[3] is None and not Q[4]:
                    continue
                if Q[3] is not None:
                    if not (eng == "tensor" and Q[3][2] == "tensor"):
                        self._wait(eng, waits, (Q[3][0], Q[3][1]))
                for (e2, s2), v2 in Q[4].items():
                    if not (eng == "tensor" and e2 == "tensor"):
                        self._wait(eng, waits, (s2, v2))
                Q[3] = None
                Q[4] = {}
        self.ops[eng].append((fn, waits, sem, amt))
        self.cnt[sem] += amt
        post = self.cnt[sem]
        for r in writes:
            R = self.regions[r]
            R[3] = (sem, post, eng)
            R[4] = {}
        for r in reads:
            self.regions[r][4][(eng, sem)] = post
        return post


def build_nc():
    nc = bass.Bass("TRN2", target_bir_lowering=False, debug=False)
    p = Prog()

    def din(name, shape, dtype=F32):
        return nc.dram_tensor(name, list(shape), dtype, kind="ExternalInput").ap()

    x_own = din("x_own", (T, D))
    x_pre = din("x_pre", (PRE, D))
    w1h = din("w1h", (NE, 16, 128, 2048))
    w2d = din("w2d", (NE, 32, 128, D))
    w_ind = din("w_ind", (8, 128, D))
    w_qd = din("w_qd", (8, 128, D))
    w_vd = din("w_vd", (8, 128, D))
    w_gd = din("w_gd", (8, 128, D))
    w_od = din("w_od", (8, 128, D))
    w_outd = din("w_outd", (8, 128, D))
    w_gated = din("w_gated", (8, 128, 8))
    b_in_pm = din("b_in_pm", (128, 8))
    b1_pm = din("b1_pm", (128, NE * 32))
    b2_d = din("b2_d", (NE, D))
    b_out_d = din("b_out_d", (1, D))
    b_gate_d = din("b_gate_d", (1, 4 * NE))
    gamma_d = din("gamma_d", (1, D))
    beta_d = din("beta_d", (1, D))
    decay_c = din("decay_c", (16, 1))
    decay_r = din("decay_r", (1, 16))
    ident_bfd = din("ident_bfd", (128, 128), BF16)
    ident_fd = din("ident_fd", (128, 128))
    tri_d = din("tri_d", (128, 128))
    iota_rd = din("iota_rd", (1, 128))
    iota1_rd = din("iota1_rd", (1, 128))
    iotar_rd = din("iotar_rd", (1, 128))
    ones_bfd = din("ones_bfd", (1, 512), BF16)
    ones_fd = din("ones_fd", (1, 128))
    cvals_d = din("cvals_d", (128, 16))

    y_out = nc.dram_tensor("y_out", [T, D], F32, kind="ExternalOutput").ap()
    g_scr = nc.dram_tensor("g_scr", [T, D], BF16, kind="Internal").ap()
    p.region("g_scr", "dram", 0, T * D * 2)
    w2_bf = nc.dram_tensor("w2_bf", [NE, 32, 128, D], BF16, kind="Internal").ap()
    q_scr = nc.dram_tensor("q_scr", [T, D], BF16, kind="Internal").ap()
    v_scr = nc.dram_tensor("v_scr", [T, D], BF16, kind="Internal").ap()
    p.region("q_scr", "dram", 10 ** 9, 10 ** 9 + T * D * 2)
    p.region("v_scr", "dram", 10 ** 9 + T * D * 2, 10 ** 9 + 2 * T * D * 2)
    base = T * D * 2
    for e in range(NE):
        p.region("w2s%d" % e, "dram", base, base + 32 * 128 * D * 2)
        base += 32 * 128 * D * 2
    taps = {}
    if _TAPS:
        for nm, shape, dt in [
            ("tap_h", (128, 8 * T), BF16), ("tap_route", (128, NT * NE), F32),
            ("tap_moe", (128, 8 * D), BF16), ("tap_c", (1, D), F32),
            ("tap_yln", (128, NT * D), BF16), ("tap_g", (T, D), BF16),
            ("tap_stats", (128, 32), F32),
        ]:
            taps[nm] = nc.dram_tensor(nm, list(shape), dt, kind="ExternalOutput").ap()

    SBUF_BASE = 16896
    CAP = 211500
    _off = [0]
    aps = {}

    def salloc(name, shape, dtype, at=None):
        nbytes = int(np.prod(shape[1:])) * mybir.dt.size(dtype)
        if at is None:
            at = _off[0]
            _off[0] = (at + nbytes + 31) & ~31
        h = nc.alloc_sbuf_tensor_at(name, list(shape), dtype, offset=SBUF_BASE + at)
        p.region(name, "sbuf", at, at + nbytes)
        aps[name] = h.ap()
        return aps[name], at

    route, _ = salloc("route", (128, NT * NE), F32)
    routeT, _ = salloc("routeT", (128, T), BF16)
    c_row, _ = salloc("c_row", (1, D), F32)
    identb, _ = salloc("identb", (128, 128), BF16)
    identf, _ = salloc("identf", (128, 128), F32)
    iota, _ = salloc("iota", (1, 128), F32)
    iota1, _ = salloc("iota1", (1, 128), F32)
    iotar, _ = salloc("iotar", (1, 128), F32)
    onesb, _ = salloc("onesb", (1, 512), BF16)
    onesf, _ = salloc("onesf", (1, 128), F32)
    decayc, _ = salloc("decayc", (16, 1), F32)
    decayr, _ = salloc("decayr", (1, 16), F32)
    loglamc, _ = salloc("loglamc", (16, 1), F32)
    loglamr, _ = salloc("loglamr", (1, 16), F32)
    lam128c, _ = salloc("lam128c", (16, 1), F32)
    w0b, _ = salloc("w0b", (128, 16), BF16)
    binpm, _ = salloc("binpm", (128, 8), F32)
    b1pm, _ = salloc("b1pm", (128, NE * 32), F32)
    b2b, _ = salloc("b2b", (8, D), BF16)
    boutb, _ = salloc("boutb", (1, D), BF16)
    bgateb, _ = salloc("bgateb", (1, 4 * NE), BF16)
    wgateb, _ = salloc("wgateb", (128, 8 * NE), BF16)
    hfm, HFM_AT = salloc("hfm", (128, 8 * T), BF16)
    cvals, _ = salloc("cvals", (128, 16), F32)
    stats, STATS_AT = salloc("stats", (128, 24), F32)
    del p.regions["stats"]
    p.region("statsA", "sbuf", STATS_AT, STATS_AT + 32)
    p.region("statsR", "sbuf", STATS_AT + 32, STATS_AT + 64)
    p.region("statsB", "sbuf", STATS_AT + 64, STATS_AT + 96)
    stats_all, _ = salloc("stats_all", (128, 64), F32)
    SE = _off[0]

    _off[0] = SE
    x_st, xn_tm = [], []
    for i in range(2):
        a, _ = salloc(f"x_st{i}", (128, D), F32)
        x_st.append(a)
    for i in range(2):
        a, _ = salloc(f"xn_tm{i}", (128, D), BF16)
        xn_tm.append(a)
    w_inb, _ = salloc("w_inb", (128, 8 * D), BF16)
    xnfm_g, _ = salloc("xnfm_g", (128, 8 * 512), BF16)
    dumA, _ = salloc("dumA", (128, D), F32)
    t_sm, _ = salloc("t_sm", (128, 32), F32)
    ge1, _ = salloc("ge1", (128, 32), F32)
    ge2, _ = salloc("ge2", (128, 32), F32)
    wvT_a, _ = salloc("wvT_a", (128, 8 * D), BF16)
    assert _off[0] <= SE + 65536, _off[0]
    _off[0] = SE + 65536
    wvnat, _ = salloc("wvnat", (128, 8 * D), BF16)
    pre_st, prexn = [], []
    for i in range(2):
        a, _ = salloc(f"pre_st{i}", (128, D), F32)
        pre_st.append(a)
    for i in range(2):
        a, _ = salloc(f"prexn{i}", (128, D), BF16)
        prexn.append(a)
    A_acc, _ = salloc("A_acc", (16, D), F32)
    A_fm, _ = salloc("A_fm", (128, 128), BF16)
    assert _off[0] <= SE + 131072, _off[0]

    _off[0] = SE
    mid, MID_AT = salloc("mid", (128, 32 * TP), BF16)
    woutb, _ = salloc("woutb", (128, 8 * D), BF16, at=MID_AT)
    w2sb, W2_AT = salloc("w2sb", (128, 32 * D), BF16, at=SE + 65536)
    del p.regions["w2sb"]
    for qd in range(4):
        p.region("w2sb%d" % qd, "sbuf",
                 W2_AT + qd * 8 * D * 2, W2_AT + (qd + 1) * 8 * D * 2)
    _off[0] = SE + 131072
    w1sl = []
    for i in range(2):
        a, _ = salloc(f"w1sl{i}", (128, 2048), BF16)
        w1sl.append(a)
    moe_tm, _ = salloc("moe_tm", (128, 8 * D), BF16)
    moefm, _ = salloc("moefm", (128, D), BF16)
    g_stage, _ = salloc("g_stage", (128, D), BF16)
    assert _off[0] <= CAP, _off[0]

    _off[0] = SE
    x_stc, xn_tmc = [], []
    for i in range(2):
        a, _ = salloc(f"x_stc{i}", (128, D), F32)
        x_stc.append(a)
    for i in range(2):
        a, _ = salloc(f"xn_tmc{i}", (128, D), BF16)
        xn_tmc.append(a)
    xnfm_cc, _ = salloc("xnfm_cc", (128, D), BF16)
    wT1, _ = salloc("wT1", (128, 8 * D), BF16)
    wT2, _ = salloc("wT2", (128, 8 * D), BF16)
    wstage, _ = salloc("wstage", (128, 8 * D), BF16)
    yln_all, _ = salloc("yln_all", (128, NT * D), BF16)
    gamma_f, _ = salloc("gamma_f", (128, D), F32)
    beta_f, _ = salloc("beta_f", (128, D), F32)
    L_bf, _ = salloc("L_bf", (128, H * 128), BF16)
    tri, _ = salloc("tri", (128, 128), F32)
    v_sb, VSB_AT = salloc("v_sb", (128, D), BF16)
    q_sb, QSB_AT = salloc("q_sb", (128, D), F32)
    qs = [salloc("qs0", (128, D), BF16, at=QSB_AT)[0],
          salloc("qs1", (128, D), BF16, at=QSB_AT + 2048)[0]]
    y_sb, _ = salloc("y_sb", (128, D), F32)
    t1f, _ = salloc("t1f", (128, D), F32)
    gate_sb, _ = salloc("gate_sb", (128, D), BF16)
    y2fm, _ = salloc("y2fm", (128, D), BF16)
    g_st = []
    for i in range(2):
        a, _ = salloc(f"g_st{i}", (128, D), BF16)
        g_st.append(a)
    out_sb = []
    for i in range(2):
        a, _ = salloc(f"out_sb{i}", (128, D), F32)
        out_sb.append(a)
    dumC, DUMC_AT = salloc("dumC", (128, D), F32)
    vs = [v_sb, salloc("vs1", (128, D), BF16, at=DUMC_AT)[0]]
    VSNM = ["v_sb", "vs1"]
    y2fm2, _ = salloc("y2fm2", (128, D), BF16, at=DUMC_AT)
    powv, _ = salloc("powv", (1, H * 128), F32)
    gate_sb2, _ = salloc("gate_sb2", (128, D), BF16)
    y2b2, _ = salloc("y2b2", (128, D), BF16, at=DUMC_AT + 2048)
    r1row, _ = salloc("r1row", (1, 128), F32)
    r2row, _ = salloc("r2row", (1, 128), F32)
    xnfm_cc2, _ = salloc("xnfm_cc2", (128, D), BF16)
    assert _off[0] <= CAP, _off[0]
    wstage2, _ = salloc("wstage2", (128, 8 * D), BF16, at=HFM_AT)

    PAIRS = []
    for j in range(4):
        h = nc.alloc_psum_tensor(f"pp{j}", [128, 1024], F32)
        PAIRS.append(h.ap())
    for i in range(8):
        p.region(f"pb{i}", "psum", i * 2048, (i + 1) * 2048)
    BANKS = [PAIRS[i // 2][:, (i % 2) * 512:(i % 2) * 512 + 512] for i in range(8)]
    BK = [f"pb{i}" for i in range(8)]
    PAIRSBF = [pr.bitcast(BF16) for pr in PAIRS]
    BANKSBF = [PAIRSBF[i // 2][:, (i % 2) * 1024:(i % 2) * 1024 + 1024]
               for i in range(8)]

    def dma(eng, out, in_, sem, reads=(), writes=(), extra=(), skip_waw=False):
        p.sem(sem)
        ex = [(sem, p.cnt[sem])] + list(extra)
        return p.op(eng, (lambda o, i: (lambda e: e.dma_start(out=o, in_=i)))(out, in_),
                    reads=reads, writes=writes, extra=ex, sem=sem, amt=16,
                    skip_waw=skip_waw)

    def mm(lhsT, rhs, out, start, stop, reads=(), writes=(), extra=()):
        return p.op("tensor",
                    (lambda l, r, o, b, e2: (lambda e: e.matmul(o, l, r, start=b, stop=e2)))
                    (lhsT, rhs, out, start, stop),
                    reads=reads, writes=writes, extra=extra)

    def tp(in_, out, ident, reads=(), writes=(), extra=()):
        return p.op("tensor",
                    (lambda i, o, d: (lambda e: e.transpose(o, i, d)))(in_, out, ident),
                    reads=reads, writes=writes, extra=extra)

    def vop(fn, reads=(), writes=(), extra=(), skip_waw=False):
        return p.op("vector", fn, reads=reads, writes=writes, extra=extra,
                    skip_waw=skip_waw)

    def sop(fn, reads=(), writes=(), extra=(), skip_waw=False):
        return p.op("scalar", fn, reads=list(reads) + ["cvals"], writes=writes,
                    extra=extra, skip_waw=skip_waw)

    for dst, src, nm in ((identb, ident_bfd, "identb"), (identf, ident_fd, "identf"),
                         (iota, iota_rd, "iota"), (iota1, iota1_rd, "iota1"),
                         (iotar, iotar_rd, "iotar"), (onesb, ones_bfd, "onesb"),
                         (onesf, ones_fd, "onesf"),
                         (decayc, decay_c, "decayc"), (decayr, decay_r, "decayr"),
                         (binpm, b_in_pm, "binpm"), (b1pm, b1_pm, "b1pm"),
                         (cvals, cvals_d, "cvals")):
        dma("sync", dst, src, "s_c", writes=[nm])
    for ci, cv in enumerate(CONST_VALS):
        nc.const_aps.register(F32, cv, cvals[:, ci:ci + 1])
    dma("gpsimd", b2b, b2_d, "s_w", writes=["b2b"])
    dma("gpsimd", boutb, b_out_d, "s_w", writes=["boutb"])
    dma("gpsimd", bgateb, b_gate_d, "s_w", writes=["bgateb"])
    dma("gpsimd", wgateb, w_gated.transpose([1, 0, 2]), "s_w", writes=["wgateb"])
    dma("gpsimd", w_inb, w_ind.transpose([1, 0, 2]), "s_w", writes=["w_inb"])
    dma("gpsimd", wvnat, w_vd.transpose([1, 0, 2]), "s_w", writes=["wvnat"])
    for e in range(2, 5):
        dma("gpsimd", w2_bf[e], w2d[e], "s_pc", writes=["w2s%d" % e])

    sop(lambda e: e.activation(loglamc, decayc, AF.Ln), reads=["decayc"], writes=["loglamc"])
    sop(lambda e: e.activation(loglamr, decayr, AF.Ln), reads=["decayr"], writes=["loglamr"])
    sop(lambda e: e.activation(lam128c, loglamc, AF.Exp, scale=128.0),
        reads=["loglamc"], writes=["lam128c"])
    mm(iotar[0:1, 0:128], loglamr, BANKS[0][0:128, 0:16], True, True,
       reads=["iotar", "loglamr"], writes=[BK[0]])
    sop(lambda e: e.activation(w0b, BANKS[0][0:128, 0:16], AF.Exp),
        reads=[BK[0]], writes=["w0b"])
    for ec in range(8):
        for dc in range(8):
            b = BK[4 + ((ec * 8 + dc) % 2)]
            tp(wvnat[:, ec * D + dc * 128: ec * D + dc * 128 + 128],
               BANKSBF[4 + ((ec * 8 + dc) % 2)][:, 0:128], identb,
               reads=["wvnat", "identb"], writes=[b])
            vop((lambda bb, ec_, dc_: (lambda e: e.tensor_copy(
                wvT_a[:, dc_ * D + ec_ * 128: dc_ * D + ec_ * 128 + 128],
                bb[:, 0:128])))(BANKSBF[4 + ((ec * 8 + dc) % 2)], ec, dc),
                reads=[b], writes=["wvT_a"], skip_waw=True)

    vop(lambda e: e.memset(A_acc, 0.0), writes=["A_acc"])
    vop(lambda e: e.memset(c_row, 0.0), writes=["c_row"])

    def emit_ln(src_ap, src_nm, dst_ap, dst_nm, mu_ap, inv_ap, statkeys,
                dum_ap, dum_nm, c0=0, statnm="statsA"):
        mu_nm, inv_nm = statkeys
        sop((lambda s, dd: (lambda e: e.activation(dd, s, AF.Copy,
            accum_out=stats[:, c0:c0 + 1])))(src_ap, dum_ap),
            reads=[src_nm], writes=[dum_nm, statnm])
        sop((lambda s, dd: (lambda e: e.activation(dd, s, AF.Square,
            accum_out=stats[:, c0 + 1:c0 + 2])))(src_ap, dum_ap),
            reads=[src_nm, statnm], writes=[dum_nm, statnm])
        sop((lambda m: (lambda e: e.activation(m, stats[:, c0:c0 + 1], AF.Copy,
            scale=1.0 / D)))(mu_ap), reads=[statnm], writes=[mu_nm], skip_waw=True)
        sop((lambda m: (lambda e: e.activation(stats[:, c0 + 3:c0 + 4], m,
            AF.Square)))(mu_ap), reads=[mu_nm, statnm], writes=[statnm])
        vop(lambda e: e.tensor_scalar(stats[:, c0 + 4:c0 + 5],
            stats[:, c0 + 1:c0 + 2], 1.0 / D, stats[:, c0 + 3:c0 + 4],
            ALU.mult, ALU.subtract), reads=[statnm], writes=[statnm])
        sop(lambda e: e.activation(stats[:, c0 + 5:c0 + 6], stats[:, c0 + 4:c0 + 5],
            AF.Ln, bias=LN_EPS), reads=[statnm], writes=[statnm])
        sop((lambda iv: (lambda e: e.activation(iv, stats[:, c0 + 5:c0 + 6], AF.Exp,
            scale=-0.5)))(inv_ap), reads=[statnm], writes=[inv_nm], skip_waw=True)
        if dst_ap is not None:
            vop((lambda s, o, m, iv: (lambda e: e.tensor_scalar(o, s, m, iv,
                ALU.subtract, ALU.mult)))(src_ap, dst_ap, mu_ap, inv_ap),
                reads=[src_nm, mu_nm, inv_nm], writes=[dst_nm])

    def emit_P_chunk(cix):
        par = cix % 2
        dma("sync", pre_st[par], x_pre[cix * 128:(cix + 1) * 128, :],
            "s_p%d" % par, writes=["pre_st%d" % par])
        c0, snm = (0, "statsA") if cix % 2 == 0 else (16, "statsB")
        emit_ln(pre_st[par], "pre_st%d" % par, prexn[par], "prexn%d" % par,
                stats[:, c0 + 2:c0 + 3], stats[:, c0 + 6:c0 + 7], (snm, snm),
                dumA, "dumA", c0=c0, statnm=snm)
        mm(w0b, prexn[par][:, 0:512], BANKS[6][0:16, :], True, True,
           reads=["w0b", "prexn%d" % par], writes=[BK[6]])
        mm(w0b, prexn[par][:, 512:1024], BANKS[7][0:16, :], True, True,
           reads=["w0b", "prexn%d" % par], writes=[BK[7]])
        vop(lambda e: e.scalar_tensor_tensor(A_acc, A_acc,
            lam128c, PAIRS[3][0:16, :], ALU.mult, ALU.add),
            reads=[BK[6], BK[7], "A_acc", "lam128c"], writes=["A_acc"])

    for t5 in range(4):
        for cpx in range(12 * t5, 12 * t5 + 12):
            emit_P_chunk(cpx)
        for i in range(4):
            t = t5 * 4 + i
            par = t % 2
            dma("sync", x_st[par], x_own[t * 128:(t + 1) * 128, :],
                "s_x%d" % par, writes=["x_st%d" % par])
            c0, snm = (0, "statsA") if t % 2 == 0 else (16, "statsB")
            emit_ln(x_st[par], "x_st%d" % par, xn_tm[par], "xn_tm%d" % par,
                    stats[:, c0 + 2:c0 + 3], stats[:, c0 + 6:c0 + 7], (snm, snm),
                    dumA, "dumA", c0=c0, statnm=snm)
            tb = 4 + (i % 2)
            for kc in range(8):
                tp(xn_tm[par][:, kc * 128:(kc + 1) * 128],
                   BANKSBF[tb][:, kc * 128:(kc + 1) * 128], identb,
                   reads=["xn_tm%d" % par, "identb"], writes=[BK[tb]])
            vop((lambda bb, i_: (lambda e: e.tensor_copy(
                xnfm_g[:, i_ * 1024:(i_ + 1) * 1024], bb)))(BANKSBF[tb], i),
                reads=[BK[tb]], writes=["xnfm_g"])
        xnv_g = xnfm_g.rearrange("p (i kc j) -> p kc i j", i=4, kc=8)
        for m in range(8):
            bk = BK[m % 4]
            for kc in range(8):
                mm(w_inb[:, kc * D + m * 128: kc * D + (m + 1) * 128],
                   xnv_g[:, kc],
                   BANKS[m % 4], kc == 0, kc == 7,
                   reads=["w_inb", "xnfm_g"], writes=[bk])
            sop((lambda bb, m_, t5_: (lambda e: e.activation(
                hfm[:, m_ * T + t5_ * 512: m_ * T + (t5_ + 1) * 512], bb,
                AF.Identity, bias=binpm[:, m_:m_ + 1])))(BANKS[m % 4], m, t5),
                reads=[bk, "binpm"], writes=["hfm"], skip_waw=True)
        for i in range(4):
            t = t5 * 4 + i
            for kc in range(8):
                mm(hfm[:, kc * T + t * 128: kc * T + (t + 1) * 128],
                   wgateb[:, kc * 8:(kc + 1) * 8],
                   BANKS[4][:, i * 8:(i + 1) * 8], i == 0 and kc == 0, False,
                   reads=["hfm", "wgateb"], writes=[BK[4]])
        mm(onesb[0:1, 0:128], bgateb, BANKS[4][:, 0:32], False, True,
           reads=["onesb", "bgateb"], writes=[BK[4]])
        Lg = BANKS[4][:, 0:32].rearrange("p (g k) -> p g k", g=4)
        m1b = stats[:, 8:12].to_broadcast([128, 4, 8])
        m2b = stats[:, 12:16].to_broadcast([128, 4, 8])
        geg1 = ge1.rearrange("p (g k) -> p g k", g=4)
        geg2 = ge2.rearrange("p (g k) -> p g k", g=4)
        tsg = t_sm.rearrange("p (g k) -> p g k", g=4)
        vop(lambda e: e.reduce_max(stats[:, 8:12], Lg, AX.X),
            reads=[BK[4]], writes=["statsR"])
        vop(lambda e: e.tensor_tensor(geg1, Lg, m1b, ALU.is_ge),
            reads=[BK[4], "statsR"], writes=["ge1"])
        vop(lambda e: e.scalar_tensor_tensor(t_sm, ge1, -BIGL, BANKS[4][:, 0:32],
            ALU.mult, ALU.add), reads=["ge1", BK[4]], writes=["t_sm"])
        vop(lambda e: e.reduce_max(stats[:, 12:16], tsg, AX.X),
            reads=["t_sm"], writes=["statsR"])
        vop(lambda e: e.tensor_tensor(geg2, Lg, m2b, ALU.is_ge),
            reads=[BK[4], "statsR"], writes=["ge2"])
        vop(lambda e: e.tensor_tensor(tsg, Lg, m1b, ALU.subtract),
            reads=[BK[4], "statsR"], writes=["t_sm"])
        sop(lambda e: e.activation(t_sm, t_sm, AF.Exp),
            reads=["t_sm"], writes=["t_sm"])
        vop(lambda e: e.tensor_mul(t_sm, t_sm, ge2),
            reads=["t_sm", "ge2"], writes=["t_sm"])
        vop(lambda e: e.reduce_sum(stats[:, 8:12], tsg, AX.X),
            reads=["t_sm"], writes=["statsR"])
        vop(lambda e: e.reciprocal(stats[:, 12:16], stats[:, 8:12]),
            reads=["statsR"], writes=["statsR"])
        vop((lambda t5_: (lambda e: e.tensor_tensor(
            route[:, t5_ * 32:(t5_ + 1) * 32].rearrange("p (g k) -> p g k", g=4),
            tsg, m2b, ALU.mult)))(t5),
            reads=["t_sm", "statsR"], writes=["route"], skip_waw=True)
        for i in range(4):
            t = t5 * 4 + i
            tp(route[:, t * 8:(t + 1) * 8], BANKS[5][0:8, i * 128:(i + 1) * 128],
               identf, reads=["route", "identf"], writes=[BK[5]])
        vop((lambda t5_: (lambda e: e.tensor_copy(
            routeT[0:8, t5_ * 512:(t5_ + 1) * 512], BANKS[5][0:8, 0:512])))(t5),
            reads=[BK[5]], writes=["routeT"], skip_waw=True)
    for kc in range(8):
        tp(A_acc[:, kc * 128:(kc + 1) * 128], BANKS[4][:, kc * 16:(kc + 1) * 16],
           identf[0:16, 0:16], reads=["A_acc", "identf"], writes=[BK[4]])
    vop(lambda e: e.tensor_copy(A_fm, BANKS[4][:, 0:128]),
        reads=[BK[4]], writes=["A_fm"])
    for h in range(H):
        for kc in range(8):
            mm(A_fm[:, kc * 16 + h: kc * 16 + h + 1],
               wvT_a[:, kc * D + h * HD: kc * D + (h + 1) * HD],
               BANKS[5][0:1, 0:64], kc == 0, kc == 7,
               reads=["A_fm", "wvT_a"], writes=[BK[5]])
        vop((lambda h_: (lambda e: e.tensor_copy(
            c_row[0:1, h_ * HD:(h_ + 1) * HD], BANKS[5][0:1, 0:64])))(h),
            reads=[BK[5]], writes=["c_row"], skip_waw=True)
    if _TAPS:
        dma("sync", taps["tap_c"], c_row, "s_tap", reads=["c_row"])

    if _TAPS:
        dma("sync", taps["tap_h"], hfm, "s_tap", reads=["hfm"])
        dma("sync", taps["tap_route"], route, "s_tap", reads=["route"])

    w1_cnt = [0, 0]

    def load_w1(e, mg):
        slot = mg % 2
        w1_cnt[slot] = dma("gpsimd", w1sl[slot], w1h[e, mg], "s_w1%d" % slot,
                           writes=["w1sl%d" % slot])

    for pas in range(NPASS):
        poff = pas * TP
        for i in range(8):
            tglob = pas * 8 + i
            b0, b1 = BK[2 * (i % 4)], BK[2 * (i % 4) + 1]
            mm(routeT[0:8, tglob * 128:(tglob + 1) * 128], b2b[0:8, 0:512],
               BANKS[2 * (i % 4)], True, True, reads=["routeT", "b2b"], writes=[b0])
            mm(routeT[0:8, tglob * 128:(tglob + 1) * 128], b2b[0:8, 512:1024],
               BANKS[2 * (i % 4) + 1], True, True, reads=["routeT", "b2b"], writes=[b1])
            vop((lambda pp, i_: (lambda e: e.tensor_copy(
                moe_tm[:, i_ * D:(i_ + 1) * D], pp)))(PAIRS[(i % 4)], i),
                reads=[b0, b1], writes=["moe_tm"], skip_waw=True)
        for e in range(NE):
            load_w1(e, 0)
            load_w1(e, 1)
            for qd in range(4):
                if e >= 2:
                    dma("sync", w2sb[:, qd * 8 * D:(qd + 1) * 8 * D],
                        w2_bf[e, 8 * qd:8 * qd + 8].transpose([1, 0, 2]),
                        "s_w2%d" % (qd % 2), reads=["w2s%d" % e],
                        writes=["w2sb%d" % qd])
                else:
                    dma("gpsimd", w2sb[:, qd * 8 * D:(qd + 1) * 8 * D],
                        w2d[e, 8 * qd:8 * qd + 8].transpose([1, 0, 2]),
                        "s_w2g%d" % (qd % 2), writes=["w2sb%d" % qd])
            for mg in range(16):
                slot = mg % 2
                bsel = 0 if mg % 2 == 0 else 4
                for mi in range(2):
                    for kc in range(8):
                        for th in range(2):
                            bk = BK[bsel + 2 * mi + th]
                            mm(w1sl[slot][:, (mi * 8 + kc) * 128:(mi * 8 + kc + 1) * 128],
                               hfm[:, kc * T + poff + th * 512: kc * T + poff + (th + 1) * 512],
                               BANKS[bsel + 2 * mi + th], kc == 0, kc == 7,
                               reads=["w1sl%d" % slot, "hfm"], writes=[bk])
                if mg < 14:
                    load_w1(e, mg + 2)
                for mi in range(2):
                    m = mg * 2 + mi
                    pr2 = (bsel + 2 * mi) // 2
                    sop((lambda pp, m_, e_: (lambda eng: eng.activation(
                        mid[:, m_ * TP:(m_ + 1) * TP], pp, AF.Relu,
                        bias=b1pm[:, e_ * 32 + m_: e_ * 32 + m_ + 1])))
                        (PAIRS[pr2], m, e),
                        reads=[BK[bsel + 2 * mi], BK[bsel + 2 * mi + 1], "b1pm"],
                        writes=["mid"], skip_waw=True)
            for i in range(8):
                tglob = pas * 8 + i
                b0, b1 = BK[2 * (i % 4)], BK[2 * (i % 4) + 1]
                for kc in range(32):
                    w2r = "w2sb%d" % (kc // 8)
                    mm(mid[:, kc * TP + i * 128: kc * TP + (i + 1) * 128],
                       w2sb[:, kc * D: kc * D + 512],
                       BANKS[2 * (i % 4)], kc == 0, kc == 31,
                       reads=["mid", w2r], writes=[b0])
                    mm(mid[:, kc * TP + i * 128: kc * TP + (i + 1) * 128],
                       w2sb[:, kc * D + 512:(kc + 1) * D],
                       BANKS[2 * (i % 4) + 1], kc == 0, kc == 31,
                       reads=["mid", w2r], writes=[b1])
                vop((lambda pp, i_, e_, tg: (lambda eng: eng.scalar_tensor_tensor(
                    moe_tm[:, i_ * D:(i_ + 1) * D], pp,
                    route[:, tg * 8 + e_: tg * 8 + e_ + 1],
                    moe_tm[:, i_ * D:(i_ + 1) * D], ALU.mult, ALU.add)))
                    (PAIRS[(i % 4)], i, e, tglob),
                    reads=[b0, b1, "route", "moe_tm"], writes=["moe_tm"])
            if pas == 0 and e < 3:
                dma("gpsimd", w2_bf[e + 5], w2d[e + 5], "s_pc",
                    writes=["w2s%d" % (e + 5)])
        if _TAPS and pas == 0:
            dma("sync", taps["tap_moe"], moe_tm, "s_tap", reads=["moe_tm"])
        dma("gpsimd", woutb, w_outd.transpose([1, 0, 2]), "s_wo", writes=["woutb"])
        for i in range(8):
            tglob = pas * 8 + i
            for j in range(8):
                tp(moe_tm[:, i * D + j * 128: i * D + (j + 1) * 128],
                   BANKSBF[4][:, j * 128:(j + 1) * 128], identb,
                   reads=["moe_tm", "identb"], writes=[BK[4]])
            vop(lambda e: e.tensor_copy(moefm, BANKSBF[4]),
                reads=[BK[4]], writes=["moefm"])
            for th in range(2):
                for j in range(8):
                    mm(moefm[:, j * 128:(j + 1) * 128],
                       woutb[:, j * D + th * 512: j * D + (th + 1) * 512],
                       BANKS[th], j == 0, False,
                       reads=["moefm", "woutb"], writes=[BK[th]])
                mm(onesb[0:1, 0:128], boutb[0:1, th * 512:(th + 1) * 512],
                   BANKS[th], False, True,
                   reads=["onesb", "boutb"], writes=[BK[th]])
                sop((lambda bb, th_: (lambda e: e.activation(
                    g_stage[:, th_ * 512:(th_ + 1) * 512], bb, AF.Silu)))
                    (BANKS[th], th),
                    reads=[BK[th]], writes=["g_stage"], skip_waw=(th == 1))
            dma("sync", g_scr[tglob * 128:(tglob + 1) * 128, :], g_stage, "s_gw",
                reads=["g_stage"], writes=["g_scr"], skip_waw=True)

    dma("sync", gamma_f[0:1, :], gamma_d, "s_c", writes=["gamma_f"])
    dma("sync", beta_f[0:1, :], beta_d, "s_c", writes=["beta_f"])
    dma("sync", tri, tri_d, "s_c", writes=["tri"])
    for th in range(2):
        mm(onesf[0:1, 0:128], gamma_f[0:1, th * 512:(th + 1) * 512],
           BANKS[4 + th], True, True, reads=["onesf", "gamma_f"], writes=[BK[4 + th]])
        vop((lambda bb, th_: (lambda e: e.tensor_copy(
            gamma_f[:, th_ * 512:(th_ + 1) * 512], bb)))(BANKS[4 + th], th),
            reads=[BK[4 + th]], writes=["gamma_f"], skip_waw=True)
    for th in range(2):
        mm(onesf[0:1, 0:128], beta_f[0:1, th * 512:(th + 1) * 512],
           BANKS[4 + th], True, True, reads=["onesf", "beta_f"], writes=[BK[4 + th]])
        vop((lambda bb, th_: (lambda e: e.tensor_copy(
            beta_f[:, th_ * 512:(th_ + 1) * 512], bb)))(BANKS[4 + th], th),
            reads=[BK[4 + th]], writes=["beta_f"], skip_waw=True)
    for h in range(H):
        vop((lambda h_: (lambda e: e.tensor_scalar(r1row, iota,
            loglamr[0:1, h_:h_ + 1], None, ALU.mult)))(h),
            reads=["iota", "loglamr"], writes=["r1row"])
        vop((lambda h_: (lambda e: e.tensor_scalar(r2row, iota,
            loglamr[0:1, h_:h_ + 1], -1.0, ALU.mult, ALU.mult)))(h),
            reads=["iota", "loglamr"], writes=["r2row"])
        mm(onesf[0:1, 0:128], r1row, BANKS[6][:, 0:128], True, False,
           reads=["onesf", "r1row"], writes=[BK[6]])
        mm(r2row, onesf[0:1, 0:128], BANKS[6][:, 0:128], False, True,
           reads=["r2row", "onesf"], writes=[BK[6]])
        vop(lambda e: e.scalar_tensor_tensor(t1f[:, 0:128], tri, BIGM,
            BANKS[6][:, 0:128], ALU.mult, ALU.add),
            reads=["tri", BK[6]], writes=["t1f"])
        sop((lambda h_: (lambda e: e.activation(L_bf[:, h_ * 128:(h_ + 1) * 128],
            t1f[:, 0:128], AF.Exp, bias=-BIGM)))(h),
            reads=["t1f"], writes=["L_bf"], skip_waw=True)
        vop((lambda h_: (lambda e: e.tensor_scalar(
            powv[0:1, h_ * 128:(h_ + 1) * 128], iota1,
            loglamr[0:1, h_:h_ + 1], None, ALU.mult)))(h),
            reads=["iota1", "loglamr"], writes=["powv"], skip_waw=True)
    sop(lambda e: e.activation(powv, powv, AF.Exp), reads=["powv"], writes=["powv"])
    dma("gpsimd", wstage, w_qd.transpose([1, 0, 2]), "s_w", writes=["wstage"])
    for dst, dnm, nxt in ((wT1, "wT1", w_vd), (wT2, "wT2", None)):
        for dc in range(8):
            wb = 4 + (dc % 2)
            for ec in range(8):
                tp(wstage[:, ec * D + dc * 128: ec * D + dc * 128 + 128],
                   BANKSBF[wb][:, ec * 128:(ec + 1) * 128], identb,
                   reads=["wstage", "identb"], writes=[BK[wb]])
            vop((lambda bb, dc_, d_: (lambda e: e.tensor_copy(
                d_[:, dc_ * D:(dc_ + 1) * D], bb)))(BANKSBF[wb], dc, dst),
                reads=[BK[wb]], writes=[dnm], skip_waw=True)
        if nxt is not None:
            dma("gpsimd", wstage, nxt.transpose([1, 0, 2]), "s_w", writes=["wstage"])
    dma("gpsimd", wstage, w_gd.transpose([1, 0, 2]), "s_w", writes=["wstage"])
    dma("gpsimd", wstage2, w_od.transpose([1, 0, 2]), "s_w", writes=["wstage2"])

    for t in range(NT):
        par = t % 2
        dma("sync", x_stc[par], x_own[t * 128:(t + 1) * 128, :],
            "s_x%d" % par, writes=["x_stc%d" % par])
        c0, snm = (0, "statsA") if t % 2 == 0 else (16, "statsB")
        emit_ln(x_stc[par], "x_stc%d" % par, None, None,
                stats_all[:, t:t + 1], stats_all[:, 16 + t:17 + t],
                ("stats_all", "stats_all"), dumC, "dumC", c0=c0, statnm=snm)

    def ca_pre(t):
        par = t % 2
        dma("sync", x_stc[par], x_own[t * 128:(t + 1) * 128, :],
            "s_x%d" % par, writes=["x_stc%d" % par])
        vop((lambda s, o, t_: (lambda e: e.tensor_scalar(o, s,
            stats_all[:, t_:t_ + 1], stats_all[:, 16 + t_:17 + t_],
            ALU.subtract, ALU.mult)))(x_stc[par], xn_tmc[par], t),
            reads=["x_stc%d" % par, "stats_all"], writes=["xn_tmc%d" % par])

    def ca_project(t):
        par = t % 2
        for kc in range(8):
            tp(xn_tmc[par][:, kc * 128:(kc + 1) * 128],
               BANKSBF[0][:, kc * 128:(kc + 1) * 128], identb,
               reads=["xn_tmc%d" % par, "identb"], writes=[BK[0]])
        vop(lambda e: e.tensor_copy(xnfm_cc, BANKSBF[0]),
            reads=[BK[0]], writes=["xnfm_cc"])
        for kc in range(8):
            mm(xnfm_cc[:, kc * 128:(kc + 1) * 128], wT1[:, kc * D: kc * D + 512],
               BANKS[2], kc == 0, kc == 7, reads=["xnfm_cc", "wT1"], writes=[BK[2]])
            mm(xnfm_cc[:, kc * 128:(kc + 1) * 128], wT1[:, kc * D + 512:(kc + 1) * D],
               BANKS[3], kc == 0, kc == 7, reads=["xnfm_cc", "wT1"], writes=[BK[3]])
        for kc in range(8):
            mm(xnfm_cc[:, kc * 128:(kc + 1) * 128], wT2[:, kc * D: kc * D + 512],
               BANKS[4], kc == 0, kc == 7, reads=["xnfm_cc", "wT2"], writes=[BK[4]])
            mm(xnfm_cc[:, kc * 128:(kc + 1) * 128], wT2[:, kc * D + 512:(kc + 1) * D],
               BANKS[5], kc == 0, kc == 7, reads=["xnfm_cc", "wT2"], writes=[BK[5]])
        vop((lambda q: (lambda e: e.tensor_copy(q, PAIRS[1])))(qs[par]),
            reads=[BK[2], BK[3]], writes=["qs%d" % par])
        vop((lambda v: (lambda e: e.tensor_copy(v, PAIRS[2])))(vs[par]),
            reads=[BK[4], BK[5]], writes=[VSNM[par]])

    def ca_scan(t):
        par = t % 2
        for h in range(H):
            bsl = BANKS[6 + (h // 8)][:, (h % 8) * HD:((h % 8) + 1) * HD]
            mm(L_bf[:, h * 128:(h + 1) * 128], vs[par][:, h * HD:(h + 1) * HD],
               bsl, h % 8 == 0, False,
               reads=["L_bf", VSNM[par]], writes=[BK[6 + (h // 8)]])
        for h in range(H):
            bsl = BANKS[6 + (h // 8)][:, (h % 8) * HD:((h % 8) + 1) * HD]
            mm(powv[0:1, h * 128:(h + 1) * 128], c_row[0:1, h * HD:(h + 1) * HD],
               bsl, False, h % 8 == 7,
               reads=["powv", "c_row"], writes=[BK[6 + (h // 8)]])
        for h in range(H):
            bq = 2 + (h // 8)
            bsl = BANKS[bq][0:1, (h % 8) * HD:((h % 8) + 1) * HD]
            mm(w0b[:, h:h + 1], vs[par][:, h * HD:(h + 1) * HD],
               bsl, h % 8 == 0, False, reads=["w0b", VSNM[par]], writes=[BK[bq]])
        for h in range(H):
            bq = 2 + (h // 8)
            bsl = BANKS[bq][0:1, (h % 8) * HD:((h % 8) + 1) * HD]
            mm(powv[0:1, h * 128 + 127: h * 128 + 128],
               c_row[0:1, h * HD:(h + 1) * HD],
               bsl, False, h % 8 == 7, reads=["powv", "c_row"], writes=[BK[bq]])
        vop((lambda q: (lambda e: e.tensor_mul(y_sb, q, PAIRS[3])))(qs[par]),
            reads=["qs%d" % par, BK[6], BK[7]], writes=["y_sb"])
        vop(lambda e: e.tensor_copy(c_row, PAIRS[1][0:1, :]),
            reads=[BK[2], BK[3]], writes=["c_row"])
        sop(lambda e: e.activation(t1f, y_sb, AF.Copy, accum_out=stats[:, 16:17]),
            reads=["y_sb"], writes=["t1f", "statsB"])
        sop(lambda e: e.activation(t1f, y_sb, AF.Square, accum_out=stats[:, 17:18]),
            reads=["y_sb", "statsB"], writes=["t1f", "statsB"])
        sop(lambda e: e.activation(stats[:, 18:19], stats[:, 16:17], AF.Copy,
            scale=1.0 / D), reads=["statsB"], writes=["statsB"])
        sop(lambda e: e.activation(stats[:, 19:20], stats[:, 18:19], AF.Square),
            reads=["statsB"], writes=["statsB"])
        vop(lambda e: e.tensor_scalar(stats[:, 20:21], stats[:, 17:18], 1.0 / D,
            stats[:, 19:20], ALU.mult, ALU.subtract), reads=["statsB"],
            writes=["statsB"])
        sop(lambda e: e.activation(stats[:, 21:22], stats[:, 20:21], AF.Ln,
            bias=LN_EPS), reads=["statsB"], writes=["statsB"])
        sop(lambda e: e.activation(stats[:, 22:23], stats[:, 21:22], AF.Exp,
            scale=-0.5), reads=["statsB"], writes=["statsB"])
        vop(lambda e: e.scalar_tensor_tensor(t1f, y_sb, stats[:, 18:19], gamma_f,
            ALU.subtract, ALU.mult), reads=["y_sb", "statsB", "gamma_f"],
            writes=["t1f"])
        vop((lambda t_: (lambda e: e.scalar_tensor_tensor(
            yln_all[:, t_ * D:(t_ + 1) * D], t1f, stats[:, 22:23], beta_f,
            ALU.mult, ALU.add)))(t), reads=["t1f", "statsB", "beta_f"],
            writes=["yln_all"], skip_waw=True)

    ca_pre(0)
    ca_pre(1)
    ca_project(0)
    for t in range(1, NT):
        if t + 1 < NT:
            ca_pre(t + 1)
        ca_project(t)
        ca_scan(t - 1)
    ca_scan(NT - 1)
    if _TAPS:
        dma("sync", taps["tap_yln"], yln_all, "s_tap", reads=["yln_all"])
        dma("sync", taps["tap_stats"], stats_all[:, 0:32], "s_tap", reads=["stats_all"])

    for dst, dnm, stg, snm in ((wT1, "wT1", wstage, "wstage"),
                               (wT2, "wT2", wstage2, "wstage2")):
        for dc in range(8):
            wb = 4 + (dc % 2)
            for ec in range(8):
                tp(stg[:, ec * D + dc * 128: ec * D + dc * 128 + 128],
                   BANKSBF[wb][:, ec * 128:(ec + 1) * 128], identb,
                   reads=[snm, "identb"], writes=[BK[wb]])
            vop((lambda bb, dc_, d_: (lambda e: e.tensor_copy(
                d_[:, dc_ * D:(dc_ + 1) * D], bb)))(BANKSBF[wb], dc, dst),
                reads=[BK[wb]], writes=[dnm], skip_waw=True)

    xnfm_cb = [xnfm_cc, xnfm_cc2]
    XNFM_NM = ["xnfm_cc", "xnfm_cc2"]
    Y2AP = [v_sb, y2b2]
    Y2NM = ["v_sb", "y2b2"]
    YFAP = [y2fm, y2fm2]
    YFNM = ["y2fm", "y2fm2"]
    GSAP = [gate_sb, gate_sb2]
    GSNM = ["gate_sb", "gate_sb2"]

    def cb_pre(t):
        par = t % 2
        dma("sync", x_stc[par], x_own[t * 128:(t + 1) * 128, :],
            "s_x%d" % par, writes=["x_stc%d" % par])
        dma("sync", g_st[par], g_scr[t * 128:(t + 1) * 128, :],
            "s_g%d" % par, reads=["g_scr"], writes=["g_st%d" % par])
        vop((lambda s, o, t_: (lambda e: e.tensor_scalar(o, s,
            stats_all[:, t_:t_ + 1], stats_all[:, 16 + t_:17 + t_],
            ALU.subtract, ALU.mult)))(x_stc[par], xn_tmc[par], t),
            reads=["x_stc%d" % par, "stats_all"], writes=["xn_tmc%d" % par])

    def cb_xn(t):
        par = t % 2
        for kc in range(8):
            tp(xn_tmc[par][:, kc * 128:(kc + 1) * 128],
               BANKSBF[4][:, kc * 128:(kc + 1) * 128], identb,
               reads=["xn_tmc%d" % par, "identb"], writes=[BK[4]])
        sop((lambda dst: (lambda e: e.activation(dst, BANKSBF[4], AF.Copy)))
            (xnfm_cb[par]), reads=[BK[4]], writes=[XNFM_NM[par]])

    def cb_ytp(t):
        par = t % 2
        for kc in range(8):
            tp(Y2AP[par][:, kc * 128:(kc + 1) * 128],
               BANKSBF[5][:, kc * 128:(kc + 1) * 128],
               identb, reads=[Y2NM[par], "identb"], writes=[BK[5]])
        sop((lambda yf: (lambda e: e.activation(yf, BANKSBF[5], AF.Copy)))
            (YFAP[par]), reads=[BK[5]], writes=[YFNM[par]])

    def cb_gate(t):
        par = t % 2
        gb = 0 if par == 0 else 2
        for kc in range(8):
            mm(xnfm_cb[par][:, kc * 128:(kc + 1) * 128],
               wT1[:, kc * D: kc * D + 512],
               BANKS[gb], kc == 0, kc == 7,
               reads=[XNFM_NM[par], "wT1"], writes=[BK[gb]])
            mm(xnfm_cb[par][:, kc * 128:(kc + 1) * 128],
               wT1[:, kc * D + 512:(kc + 1) * D],
               BANKS[gb + 1], kc == 0, kc == 7,
               reads=[XNFM_NM[par], "wT1"], writes=[BK[gb + 1]])

    def cb_act(t):
        par = t % 2
        gb = 0 if par == 0 else 2
        sop((lambda g, pp: (lambda e: e.activation(g, pp, AF.Silu)))
            (GSAP[par], PAIRS[gb // 2]), reads=[BK[gb], BK[gb + 1]],
            writes=[GSNM[par]])
        vop((lambda t_, g, y2: (lambda e: e.tensor_mul(y2,
            yln_all[:, t_ * D:(t_ + 1) * D], g)))(t, GSAP[par], Y2AP[par]),
            reads=["yln_all", GSNM[par]], writes=[Y2NM[par]])

    def cb_s(t):
        par = t % 2
        for kc in range(8):
            mm(YFAP[par][:, kc * 128:(kc + 1) * 128], wT2[:, kc * D: kc * D + 512],
               BANKS[6], kc == 0, kc == 7, reads=[YFNM[par], "wT2"], writes=[BK[6]])
            mm(YFAP[par][:, kc * 128:(kc + 1) * 128], wT2[:, kc * D + 512:(kc + 1) * D],
               BANKS[7], kc == 0, kc == 7, reads=[YFNM[par], "wT2"], writes=[BK[7]])

    def cb_out(t):
        par = t % 2
        ob = out_sb[par]
        onm = "out_sb%d" % par
        vop((lambda o, gp: (lambda e: e.tensor_mul(o, PAIRS[3], gp)))
            (ob, g_st[par]),
            reads=[BK[6], BK[7], "g_st%d" % par], writes=[onm])
        vop((lambda o, xb: (lambda e: e.tensor_add(o, o, xb)))(ob, x_stc[par]),
            reads=[onm, "x_stc%d" % par], writes=[onm])
        dma("sync", y_out[t * 128:(t + 1) * 128, :], ob, "s_o%d" % par,
            reads=[onm])

    cb_pre(0)
    cb_pre(1)
    cb_xn(0)
    cb_gate(0)
    cb_act(0)
    for t in range(NT):
        if t + 1 < NT:
            cb_xn(t + 1)
        cb_ytp(t)
        if t + 1 < NT:
            cb_gate(t + 1)
        cb_s(t)
        if t + 1 < NT:
            cb_act(t + 1)
        cb_out(t)
        if t + 2 < NT:
            cb_pre(t + 2)

    fin = [("s_o0", p.cnt.get("s_o0", 0)), ("s_o1", p.cnt.get("s_o1", 0))]
    if _TAPS:
        fin.append(("s_tap", p.cnt.get("s_tap", 0)))
    p.op("sync", None, extra=fin, sem="s_fin", amt=0)

    import contextlib
    with contextlib.ExitStack() as stack:
        sems = {}
        for s in p.sem_names:
            sems[s] = stack.enter_context(nc.semaphore(s))
        blk = stack.enter_context(nc.Block())

        eng_sems = set(ENG_SEM.values())
        needed = {s: v for s, v in p.needed.items() if s in eng_sems}
        plans = {}
        for name in ENGS:
            plan = []
            run = {}
            for fn, waits, sem, amt in p.ops[name]:
                if sem in eng_sems and amt:
                    run[sem] = run.get(sem, 0) + amt
                    post = sum(a for f2, w2, s2, a in p.ops[name] if s2 == sem) if False else None
                    plan.append(run[sem])
                else:
                    plan.append(None)
            plans[name] = plan

        def section(name):
            def go(eng):
                carried = {}
                plan = plans[name]
                for idx, (fn, waits, sem, amt) in enumerate(p.ops[name]):
                    for (s, v) in waits:
                        eng.wait_ge(sems[s], v)
                    if fn is not None:
                        inst = fn(eng)
                        if amt:
                            if sem in eng_sems:
                                carried[sem] = carried.get(sem, 0) + amt
                                if plan[idx] in needed.get(sem, ()):
                                    inst.then_inc(sems[sem], carried[sem])
                                    carried[sem] = 0
                            else:
                                inst.then_inc(sems[sem], amt)
            return go

        blk.tensor(section("tensor"))
        blk.vector(section("vector"))
        blk.scalar(section("scalar"))
        blk.sync(section("sync"))
        blk.gpsimd(section("gpsimd"))
    return nc


def _prep_inputs(inp):
    f = np.float32
    bf = ml_dtypes.bfloat16
    x = np.asarray(inp["L0_attn_x"], f)
    W1 = np.asarray(inp["L0_ffn_W1"], f)
    w1h = np.ascontiguousarray(
        W1.reshape(NE, 8, 128, 16, 2, 128).transpose(0, 3, 2, 4, 1, 5)
    ).reshape(NE, 16, 128, 2048)
    shared = {
        "w1h": w1h,
        "w2d": np.ascontiguousarray(np.asarray(inp["L0_ffn_W2"], f)).reshape(NE, 32, 128, D),
        "w_ind": np.ascontiguousarray(np.asarray(inp["L0_ffn_W_in"], f)).reshape(8, 128, D),
        "w_qd": np.ascontiguousarray(np.asarray(inp["L0_attn_Wq"], f)).reshape(8, 128, D),
        "w_vd": np.ascontiguousarray(np.asarray(inp["L0_attn_Wv"], f)).reshape(8, 128, D),
        "w_gd": np.ascontiguousarray(np.asarray(inp["L0_attn_Wg"], f)).reshape(8, 128, D),
        "w_od": np.ascontiguousarray(np.asarray(inp["L0_attn_Wo"], f)).reshape(8, 128, D),
        "w_outd": np.ascontiguousarray(np.asarray(inp["L0_ffn_W_out"], f)).reshape(8, 128, D),
        "w_gated": np.ascontiguousarray(np.asarray(inp["L0_ffn_W_gate"], f)).reshape(8, 128, 8),
        "b_in_pm": np.ascontiguousarray(np.asarray(inp["L0_ffn_b_in"], f).reshape(8, 128).T),
        "b1_pm": np.ascontiguousarray(
            np.asarray(inp["L0_ffn_b1"], f).reshape(NE, 32, 128).transpose(2, 0, 1)
        ).reshape(128, NE * 32),
        "b2_d": np.ascontiguousarray(np.asarray(inp["L0_ffn_b2"], f)),
        "b_out_d": np.asarray(inp["L0_ffn_b_out"], f).reshape(1, D).copy(),
        "b_gate_d": np.tile(np.asarray(inp["L0_ffn_b_gate"], f).reshape(1, NE), (1, 4)),
        "gamma_d": np.asarray(inp["L0_attn_ln_gamma"], f).reshape(1, D).copy(),
        "beta_d": np.asarray(inp["L0_attn_ln_beta"], f).reshape(1, D).copy(),
        "decay_c": np.asarray(inp["L0_attn_decay"], f).reshape(16, 1).copy(),
        "decay_r": np.asarray(inp["L0_attn_decay"], f).reshape(1, 16).copy(),
        "ident_bfd": np.eye(128, dtype=bf),
        "ident_fd": np.eye(128, dtype=f),
        "tri_d": np.triu(np.ones((128, 128), f)),
        "iota_rd": np.arange(128, dtype=f).reshape(1, 128),
        "iota1_rd": (np.arange(128, dtype=f) + 1).reshape(1, 128),
        "iotar_rd": (127 - np.arange(128, dtype=f)).reshape(1, 128),
        "ones_bfd": np.ones((1, 512), dtype=bf),
        "ones_fd": np.ones((1, 128), dtype=f),
        "cvals_d": np.tile(np.array(CONST_VALS + [0.0] * (16 - len(CONST_VALS)),
                                    dtype=f), (128, 1)),
    }
    in_maps = []
    for k in range(NCORES):
        b, s0 = k // 4, (k % 4) * T
        xo = np.ascontiguousarray(x[b, s0:s0 + T])
        xp = np.zeros((PRE, D), f)
        if s0 > 0:
            xp[PRE - s0:] = x[b, :s0]
        m = {"x_own": xo, "x_pre": xp}
        m.update(shared)
        in_maps.append(m)
    return in_maps


def kernel(**inputs):
    in_maps = _prep_inputs(inputs)
    nc = build_nc()
    res = run_bass_kernel_spmd(nc, in_maps, core_ids=list(range(NCORES)))
    out = np.empty((2, 8192, D), np.float32)
    for k in range(NCORES):
        b, s0 = k // 4, (k % 4) * T
        out[b, s0:s0 + T] = np.asarray(res.results[k]["y_out"], np.float32)
    return out


# version 29
# speedup vs baseline: 1.1444x; 1.0195x over previous
import numpy as np
import ml_dtypes

import concourse.bass as bass
import concourse.mybir as mybir
from concourse.bass_utils import run_bass_kernel_spmd

F32 = mybir.dt.float32
BF16 = mybir.dt.bfloat16
AF = mybir.ActivationFunctionType
ALU = mybir.AluOpType
AX = mybir.AxisListType

NCORES = 8
D = 1024
H = 16
HD = 64
DM = 4096
NE = 8
T = 2048
NT = T // 128
PRE = 6144
NPRE = PRE // 128
NPASS = 2
TP = T // NPASS
LN_EPS = 1e-5
BIGM = 60.0
BIGL = 1e9

_TAPS = False

CONST_VALS = [0.0, 1.0, LN_EPS, -BIGM, BIGM, 1.0 / D, -0.5, 128.0, -1.0, -BIGL]

ENGS = ("tensor", "vector", "scalar", "sync", "gpsimd")
ENG_SEM = {"tensor": "st", "vector": "sv", "scalar": "ss"}


class Prog:

    def __init__(self):
        self.ops = {e: [] for e in ENGS}
        self.cnt = {}
        self.hwm = {e: {} for e in ENGS}
        self.sem_names = []
        self.needed = {}
        self.regions = {}

    def sem(self, name):
        if name not in self.cnt:
            self.sem_names.append(name)
            self.cnt[name] = 0

    def region(self, name, space, lo, hi):
        self.regions[name] = [space, lo, hi, None, {}]

    def _wait(self, eng, waits, tok):
        if tok is None:
            return
        s, v = tok
        if v <= 0:
            return
        self.needed.setdefault(s, set()).add(v)
        if self.hwm[eng].get(s, 0) >= v:
            return
        self.hwm[eng][s] = v
        waits.append((s, v))

    def op(self, eng, fn, reads=(), writes=(), extra=(), sem=None, amt=1,
           skip_waw=False):
        sem = sem or ENG_SEM[eng]
        self.sem(sem)
        waits = []
        for (s, v) in extra:
            self._wait(eng, waits, (s, v))
        for r in reads:
            R = self.regions[r]
            if R[3] is not None:
                if not (eng == "tensor" and R[3][2] == "tensor"):
                    self._wait(eng, waits, (R[3][0], R[3][1]))
        for r in writes:
            R = self.regions[r]
            if R[3] is not None and not skip_waw:
                if not (eng == "tensor" and R[3][2] == "tensor"):
                    self._wait(eng, waits, (R[3][0], R[3][1]))
            for (e2, s2), v2 in R[4].items():
                if not (eng == "tensor" and e2 == "tensor"):
                    self._wait(eng, waits, (s2, v2))
            sp, lo, hi = R[0], R[1], R[2]
            for q, Q in self.regions.items():
                if q == r or Q[0] != sp or Q[2] <= lo or Q[1] >= hi:
                    continue
                if Q[3] is None and not Q[4]:
                    continue
                if Q[3] is not None:
                    if not (eng == "tensor" and Q[3][2] == "tensor"):
                        self._wait(eng, waits, (Q[3][0], Q[3][1]))
                for (e2, s2), v2 in Q[4].items():
                    if not (eng == "tensor" and e2 == "tensor"):
                        self._wait(eng, waits, (s2, v2))
                Q[3] = None
                Q[4] = {}
        self.ops[eng].append((fn, waits, sem, amt))
        self.cnt[sem] += amt
        post = self.cnt[sem]
        for r in writes:
            R = self.regions[r]
            R[3] = (sem, post, eng)
            R[4] = {}
        for r in reads:
            self.regions[r][4][(eng, sem)] = post
        return post


def build_nc():
    nc = bass.Bass("TRN2", target_bir_lowering=False, debug=False)
    p = Prog()

    def din(name, shape, dtype=F32):
        return nc.dram_tensor(name, list(shape), dtype, kind="ExternalInput").ap()

    x_own = din("x_own", (T, D))
    x_pre = din("x_pre", (PRE, D))
    w1h = din("w1h", (NE, 16, 128, 2048))
    w2d = din("w2d", (NE, 32, 128, D))
    w_ind = din("w_ind", (8, 128, D))
    w_qd = din("w_qd", (8, 128, D))
    w_vd = din("w_vd", (8, 128, D))
    w_gd = din("w_gd", (8, 128, D))
    w_od = din("w_od", (8, 128, D))
    w_outd = din("w_outd", (8, 128, D))
    w_gated = din("w_gated", (8, 128, 8))
    b_in_pm = din("b_in_pm", (128, 8))
    b1_pm = din("b1_pm", (128, NE * 32))
    b2_d = din("b2_d", (NE, D))
    b_out_d = din("b_out_d", (1, D))
    b_gate_d = din("b_gate_d", (1, 4 * NE))
    gamma_d = din("gamma_d", (1, D))
    beta_d = din("beta_d", (1, D))
    decay_c = din("decay_c", (16, 1))
    decay_r = din("decay_r", (1, 16))
    ident_bfd = din("ident_bfd", (128, 128), BF16)
    ident_fd = din("ident_fd", (128, 128))
    tri_d = din("tri_d", (128, 128))
    iota_rd = din("iota_rd", (1, 128))
    iota1_rd = din("iota1_rd", (1, 128))
    iotar_rd = din("iotar_rd", (1, 128))
    ones_bfd = din("ones_bfd", (1, 512), BF16)
    ones_fd = din("ones_fd", (1, 128))
    cvals_d = din("cvals_d", (128, 16))

    y_out = nc.dram_tensor("y_out", [T, D], F32, kind="ExternalOutput").ap()
    g_scr = nc.dram_tensor("g_scr", [T, D], BF16, kind="Internal").ap()
    p.region("g_scr", "dram", 0, T * D * 2)
    w2_bf = nc.dram_tensor("w2_bf", [NE, 32, 128, D], BF16, kind="Internal").ap()
    q_scr = nc.dram_tensor("q_scr", [T, D], BF16, kind="Internal").ap()
    v_scr = nc.dram_tensor("v_scr", [T, D], BF16, kind="Internal").ap()
    p.region("q_scr", "dram", 10 ** 9, 10 ** 9 + T * D * 2)
    p.region("v_scr", "dram", 10 ** 9 + T * D * 2, 10 ** 9 + 2 * T * D * 2)
    base = T * D * 2
    for e in range(NE):
        p.region("w2s%d" % e, "dram", base, base + 32 * 128 * D * 2)
        base += 32 * 128 * D * 2
    taps = {}
    if _TAPS:
        for nm, shape, dt in [
            ("tap_h", (128, 8 * T), BF16), ("tap_route", (128, NT * NE), F32),
            ("tap_moe", (128, 8 * D), BF16), ("tap_c", (1, D), F32),
            ("tap_yln", (128, NT * D), BF16), ("tap_g", (T, D), BF16),
            ("tap_stats", (128, 32), F32),
        ]:
            taps[nm] = nc.dram_tensor(nm, list(shape), dt, kind="ExternalOutput").ap()

    SBUF_BASE = 16896
    CAP = 211500
    _off = [0]
    aps = {}

    def salloc(name, shape, dtype, at=None):
        nbytes = int(np.prod(shape[1:])) * mybir.dt.size(dtype)
        if at is None:
            at = _off[0]
            _off[0] = (at + nbytes + 31) & ~31
        h = nc.alloc_sbuf_tensor_at(name, list(shape), dtype, offset=SBUF_BASE + at)
        p.region(name, "sbuf", at, at + nbytes)
        aps[name] = h.ap()
        return aps[name], at

    route, _ = salloc("route", (128, NT * NE), F32)
    routeT, _ = salloc("routeT", (128, T), BF16)
    c_row, _ = salloc("c_row", (1, D), F32)
    identb, _ = salloc("identb", (128, 128), BF16)
    identf, _ = salloc("identf", (128, 128), F32)
    iota, _ = salloc("iota", (1, 128), F32)
    iota1, _ = salloc("iota1", (1, 128), F32)
    iotar, _ = salloc("iotar", (1, 128), F32)
    onesb, _ = salloc("onesb", (1, 512), BF16)
    onesf, _ = salloc("onesf", (1, 128), F32)
    decayc, _ = salloc("decayc", (16, 1), F32)
    decayr, _ = salloc("decayr", (1, 16), F32)
    loglamc, _ = salloc("loglamc", (16, 1), F32)
    loglamr, _ = salloc("loglamr", (1, 16), F32)
    lam128c, _ = salloc("lam128c", (16, 1), F32)
    w0b, _ = salloc("w0b", (128, 16), BF16)
    binpm, _ = salloc("binpm", (128, 8), F32)
    b1pm, _ = salloc("b1pm", (128, NE * 32), F32)
    b2b, _ = salloc("b2b", (8, D), BF16)
    boutb, _ = salloc("boutb", (1, D), BF16)
    bgateb, _ = salloc("bgateb", (1, 4 * NE), BF16)
    wgateb, _ = salloc("wgateb", (128, 8 * NE), BF16)
    hfm, HFM_AT = salloc("hfm", (128, 8 * T), BF16)
    cvals, _ = salloc("cvals", (128, 16), F32)
    stats, STATS_AT = salloc("stats", (128, 24), F32)
    del p.regions["stats"]
    p.region("statsA", "sbuf", STATS_AT, STATS_AT + 32)
    p.region("statsR", "sbuf", STATS_AT + 32, STATS_AT + 64)
    p.region("statsB", "sbuf", STATS_AT + 64, STATS_AT + 96)
    stats_all, _ = salloc("stats_all", (128, 64), F32)
    SE = _off[0]

    _off[0] = SE
    x_st, xn_tm = [], []
    for i in range(2):
        a, _ = salloc(f"x_st{i}", (128, D), F32)
        x_st.append(a)
    for i in range(2):
        a, _ = salloc(f"xn_tm{i}", (128, D), BF16)
        xn_tm.append(a)
    w_inb, _ = salloc("w_inb", (128, 8 * D), BF16)
    xnfm_g, _ = salloc("xnfm_g", (128, 8 * 512), BF16)
    dumA, _ = salloc("dumA", (128, D), F32)
    t_sm, _ = salloc("t_sm", (128, 32), F32)
    ge1, _ = salloc("ge1", (128, 32), F32)
    ge2, _ = salloc("ge2", (128, 32), F32)
    wvT_a, _ = salloc("wvT_a", (128, 8 * D), BF16)
    assert _off[0] <= SE + 65536, _off[0]
    _off[0] = SE + 65536
    wvnat, _ = salloc("wvnat", (128, 8 * D), BF16)
    pre_st, prexn = [], []
    for i in range(2):
        a, _ = salloc(f"pre_st{i}", (128, D), F32)
        pre_st.append(a)
    for i in range(2):
        a, _ = salloc(f"prexn{i}", (128, D), BF16)
        prexn.append(a)
    A_acc, _ = salloc("A_acc", (16, D), F32)
    A_fm, _ = salloc("A_fm", (128, 128), BF16)
    assert _off[0] <= SE + 131072, _off[0]

    _off[0] = SE
    mid, MID_AT = salloc("mid", (128, 32 * TP), BF16)
    woutb, _ = salloc("woutb", (128, 8 * D), BF16, at=MID_AT)
    w2sb, W2_AT = salloc("w2sb", (128, 32 * D), BF16, at=SE + 65536)
    del p.regions["w2sb"]
    for qd in range(4):
        p.region("w2sb%d" % qd, "sbuf",
                 W2_AT + qd * 8 * D * 2, W2_AT + (qd + 1) * 8 * D * 2)
    _off[0] = SE + 131072
    w1sl = []
    for i in range(2):
        a, _ = salloc(f"w1sl{i}", (128, 2048), BF16)
        w1sl.append(a)
    moe_tm, _ = salloc("moe_tm", (128, 8 * D), BF16)
    moefm, _ = salloc("moefm", (128, D), BF16)
    g_stage, _ = salloc("g_stage", (128, D), BF16)
    assert _off[0] <= CAP, _off[0]

    _off[0] = SE
    x_stc, xn_tmc = [], []
    for i in range(2):
        a, _ = salloc(f"x_stc{i}", (128, D), F32)
        x_stc.append(a)
    for i in range(2):
        a, _ = salloc(f"xn_tmc{i}", (128, D), BF16)
        xn_tmc.append(a)
    xnfm_cc, _ = salloc("xnfm_cc", (128, D), BF16)
    wT1, _ = salloc("wT1", (128, 8 * D), BF16)
    wT2, _ = salloc("wT2", (128, 8 * D), BF16)
    wstage, _ = salloc("wstage", (128, 8 * D), BF16)
    yln_all, _ = salloc("yln_all", (128, NT * D), BF16)
    gamma_f, _ = salloc("gamma_f", (128, D), F32)
    beta_f, _ = salloc("beta_f", (128, D), F32)
    L_bf, _ = salloc("L_bf", (128, H * 128), BF16)
    tri, _ = salloc("tri", (128, 128), F32)
    v_sb, VSB_AT = salloc("v_sb", (128, D), BF16)
    q_sb, QSB_AT = salloc("q_sb", (128, D), F32)
    qs = [salloc("qs0", (128, D), BF16, at=QSB_AT)[0],
          salloc("qs1", (128, D), BF16, at=QSB_AT + 2048)[0]]
    y_sb, _ = salloc("y_sb", (128, D), F32)
    t1f, _ = salloc("t1f", (128, D), F32)
    gate_sb, _ = salloc("gate_sb", (128, D), BF16)
    y2fm, _ = salloc("y2fm", (128, D), BF16)
    g_st = []
    for i in range(2):
        a, _ = salloc(f"g_st{i}", (128, D), BF16)
        g_st.append(a)
    out_sb = []
    for i in range(2):
        a, _ = salloc(f"out_sb{i}", (128, D), F32)
        out_sb.append(a)
    dumC, DUMC_AT = salloc("dumC", (128, D), F32)
    vs = [v_sb, salloc("vs1", (128, D), BF16, at=DUMC_AT)[0]]
    VSNM = ["v_sb", "vs1"]
    y2fm2, _ = salloc("y2fm2", (128, D), BF16, at=DUMC_AT)
    powv, _ = salloc("powv", (1, H * 128), F32)
    gate_sb2, _ = salloc("gate_sb2", (128, D), BF16)
    y2b2, _ = salloc("y2b2", (128, D), BF16, at=DUMC_AT + 2048)
    r1row, _ = salloc("r1row", (1, 128), F32)
    r2row, _ = salloc("r2row", (1, 128), F32)
    xnfm_cc2, _ = salloc("xnfm_cc2", (128, D), BF16)
    assert _off[0] <= CAP, _off[0]
    wstage2, _ = salloc("wstage2", (128, 8 * D), BF16, at=HFM_AT)
    _a, _ = salloc("x_stc2", (128, D), F32, at=HFM_AT + 16384)
    x_stc.append(_a)
    _a, _ = salloc("xn_tmc2", (128, D), BF16, at=HFM_AT + 20480)
    xn_tmc.append(_a)
    _a, _ = salloc("g_st2", (128, D), BF16, at=HFM_AT + 22528)
    g_st.append(_a)

    PAIRS = []
    for j in range(4):
        h = nc.alloc_psum_tensor(f"pp{j}", [128, 1024], F32)
        PAIRS.append(h.ap())
    for i in range(8):
        p.region(f"pb{i}", "psum", i * 2048, (i + 1) * 2048)
    BANKS = [PAIRS[i // 2][:, (i % 2) * 512:(i % 2) * 512 + 512] for i in range(8)]
    BK = [f"pb{i}" for i in range(8)]
    PAIRSBF = [pr.bitcast(BF16) for pr in PAIRS]
    BANKSBF = [PAIRSBF[i // 2][:, (i % 2) * 1024:(i % 2) * 1024 + 1024]
               for i in range(8)]

    def dma(eng, out, in_, sem, reads=(), writes=(), extra=(), skip_waw=False):
        p.sem(sem)
        ex = [(sem, p.cnt[sem])] + list(extra)
        return p.op(eng, (lambda o, i: (lambda e: e.dma_start(out=o, in_=i)))(out, in_),
                    reads=reads, writes=writes, extra=ex, sem=sem, amt=16,
                    skip_waw=skip_waw)

    def mm(lhsT, rhs, out, start, stop, reads=(), writes=(), extra=()):
        return p.op("tensor",
                    (lambda l, r, o, b, e2: (lambda e: e.matmul(o, l, r, start=b, stop=e2)))
                    (lhsT, rhs, out, start, stop),
                    reads=reads, writes=writes, extra=extra)

    def tp(in_, out, ident, reads=(), writes=(), extra=()):
        return p.op("tensor",
                    (lambda i, o, d: (lambda e: e.transpose(o, i, d)))(in_, out, ident),
                    reads=reads, writes=writes, extra=extra)

    def vop(fn, reads=(), writes=(), extra=(), skip_waw=False):
        return p.op("vector", fn, reads=reads, writes=writes, extra=extra,
                    skip_waw=skip_waw)

    def sop(fn, reads=(), writes=(), extra=(), skip_waw=False):
        return p.op("scalar", fn, reads=list(reads) + ["cvals"], writes=writes,
                    extra=extra, skip_waw=skip_waw)

    for dst, src, nm in ((identb, ident_bfd, "identb"), (identf, ident_fd, "identf"),
                         (iota, iota_rd, "iota"), (iota1, iota1_rd, "iota1"),
                         (iotar, iotar_rd, "iotar"), (onesb, ones_bfd, "onesb"),
                         (onesf, ones_fd, "onesf"),
                         (decayc, decay_c, "decayc"), (decayr, decay_r, "decayr"),
                         (binpm, b_in_pm, "binpm"), (b1pm, b1_pm, "b1pm"),
                         (cvals, cvals_d, "cvals")):
        dma("sync", dst, src, "s_c", writes=[nm])
    for ci, cv in enumerate(CONST_VALS):
        nc.const_aps.register(F32, cv, cvals[:, ci:ci + 1])
    dma("gpsimd", b2b, b2_d, "s_w", writes=["b2b"])
    dma("gpsimd", boutb, b_out_d, "s_w", writes=["boutb"])
    dma("gpsimd", bgateb, b_gate_d, "s_w", writes=["bgateb"])
    dma("gpsimd", wgateb, w_gated.transpose([1, 0, 2]), "s_w", writes=["wgateb"])
    dma("gpsimd", w_inb, w_ind.transpose([1, 0, 2]), "s_w", writes=["w_inb"])
    dma("gpsimd", wvnat, w_vd.transpose([1, 0, 2]), "s_w", writes=["wvnat"])
    for e in range(2, 5):
        dma("gpsimd", w2_bf[e], w2d[e], "s_pc", writes=["w2s%d" % e])

    sop(lambda e: e.activation(loglamc, decayc, AF.Ln), reads=["decayc"], writes=["loglamc"])
    sop(lambda e: e.activation(loglamr, decayr, AF.Ln), reads=["decayr"], writes=["loglamr"])
    sop(lambda e: e.activation(lam128c, loglamc, AF.Exp, scale=128.0),
        reads=["loglamc"], writes=["lam128c"])
    mm(iotar[0:1, 0:128], loglamr, BANKS[0][0:128, 0:16], True, True,
       reads=["iotar", "loglamr"], writes=[BK[0]])
    sop(lambda e: e.activation(w0b, BANKS[0][0:128, 0:16], AF.Exp),
        reads=[BK[0]], writes=["w0b"])
    for ec in range(8):
        for dc in range(8):
            b = BK[4 + ((ec * 8 + dc) % 2)]
            tp(wvnat[:, ec * D + dc * 128: ec * D + dc * 128 + 128],
               BANKSBF[4 + ((ec * 8 + dc) % 2)][:, 0:128], identb,
               reads=["wvnat", "identb"], writes=[b])
            vop((lambda bb, ec_, dc_: (lambda e: e.tensor_copy(
                wvT_a[:, dc_ * D + ec_ * 128: dc_ * D + ec_ * 128 + 128],
                bb[:, 0:128])))(BANKSBF[4 + ((ec * 8 + dc) % 2)], ec, dc),
                reads=[b], writes=["wvT_a"], skip_waw=True)

    vop(lambda e: e.memset(A_acc, 0.0), writes=["A_acc"])
    vop(lambda e: e.memset(c_row, 0.0), writes=["c_row"])

    def emit_ln(src_ap, src_nm, dst_ap, dst_nm, mu_ap, inv_ap, statkeys,
                dum_ap, dum_nm, c0=0, statnm="statsA"):
        mu_nm, inv_nm = statkeys
        sop((lambda s, dd: (lambda e: e.activation(dd, s, AF.Copy,
            accum_out=stats[:, c0:c0 + 1])))(src_ap, dum_ap),
            reads=[src_nm], writes=[dum_nm, statnm])
        sop((lambda s, dd: (lambda e: e.activation(dd, s, AF.Square,
            accum_out=stats[:, c0 + 1:c0 + 2])))(src_ap, dum_ap),
            reads=[src_nm, statnm], writes=[dum_nm, statnm])
        sop((lambda m: (lambda e: e.activation(m, stats[:, c0:c0 + 1], AF.Copy,
            scale=1.0 / D)))(mu_ap), reads=[statnm], writes=[mu_nm], skip_waw=True)
        sop((lambda m: (lambda e: e.activation(stats[:, c0 + 3:c0 + 4], m,
            AF.Square)))(mu_ap), reads=[mu_nm, statnm], writes=[statnm])
        vop(lambda e: e.tensor_scalar(stats[:, c0 + 4:c0 + 5],
            stats[:, c0 + 1:c0 + 2], 1.0 / D, stats[:, c0 + 3:c0 + 4],
            ALU.mult, ALU.subtract), reads=[statnm], writes=[statnm])
        sop(lambda e: e.activation(stats[:, c0 + 5:c0 + 6], stats[:, c0 + 4:c0 + 5],
            AF.Ln, bias=LN_EPS), reads=[statnm], writes=[statnm])
        sop((lambda iv: (lambda e: e.activation(iv, stats[:, c0 + 5:c0 + 6], AF.Exp,
            scale=-0.5)))(inv_ap), reads=[statnm], writes=[inv_nm], skip_waw=True)
        if dst_ap is not None:
            vop((lambda s, o, m, iv: (lambda e: e.tensor_scalar(o, s, m, iv,
                ALU.subtract, ALU.mult)))(src_ap, dst_ap, mu_ap, inv_ap),
                reads=[src_nm, mu_nm, inv_nm], writes=[dst_nm])

    def emit_P_chunk(cix):
        par = cix % 2
        dma("sync", pre_st[par], x_pre[cix * 128:(cix + 1) * 128, :],
            "s_p%d" % par, writes=["pre_st%d" % par])
        c0, snm = (0, "statsA") if cix % 2 == 0 else (16, "statsB")
        emit_ln(pre_st[par], "pre_st%d" % par, prexn[par], "prexn%d" % par,
                stats[:, c0 + 2:c0 + 3], stats[:, c0 + 6:c0 + 7], (snm, snm),
                dumA, "dumA", c0=c0, statnm=snm)
        mm(w0b, prexn[par][:, 0:512], BANKS[6][0:16, :], True, True,
           reads=["w0b", "prexn%d" % par], writes=[BK[6]])
        mm(w0b, prexn[par][:, 512:1024], BANKS[7][0:16, :], True, True,
           reads=["w0b", "prexn%d" % par], writes=[BK[7]])
        vop(lambda e: e.scalar_tensor_tensor(A_acc, A_acc,
            lam128c, PAIRS[3][0:16, :], ALU.mult, ALU.add),
            reads=[BK[6], BK[7], "A_acc", "lam128c"], writes=["A_acc"])

    for t5 in range(4):
        for cpx in range(12 * t5, 12 * t5 + 12):
            emit_P_chunk(cpx)
        for i in range(4):
            t = t5 * 4 + i
            par = t % 2
            dma("sync", x_st[par], x_own[t * 128:(t + 1) * 128, :],
                "s_x%d" % par, writes=["x_st%d" % par])
            c0, snm = (0, "statsA") if t % 2 == 0 else (16, "statsB")
            emit_ln(x_st[par], "x_st%d" % par, xn_tm[par], "xn_tm%d" % par,
                    stats[:, c0 + 2:c0 + 3], stats[:, c0 + 6:c0 + 7], (snm, snm),
                    dumA, "dumA", c0=c0, statnm=snm)
            tb = 4 + (i % 2)
            for kc in range(8):
                tp(xn_tm[par][:, kc * 128:(kc + 1) * 128],
                   BANKSBF[tb][:, kc * 128:(kc + 1) * 128], identb,
                   reads=["xn_tm%d" % par, "identb"], writes=[BK[tb]])
            vop((lambda bb, i_: (lambda e: e.tensor_copy(
                xnfm_g[:, i_ * 1024:(i_ + 1) * 1024], bb)))(BANKSBF[tb], i),
                reads=[BK[tb]], writes=["xnfm_g"])
        xnv_g = xnfm_g.rearrange("p (i kc j) -> p kc i j", i=4, kc=8)
        for m in range(8):
            bk = BK[m % 4]
            for kc in range(8):
                mm(w_inb[:, kc * D + m * 128: kc * D + (m + 1) * 128],
                   xnv_g[:, kc],
                   BANKS[m % 4], kc == 0, kc == 7,
                   reads=["w_inb", "xnfm_g"], writes=[bk])
            sop((lambda bb, m_, t5_: (lambda e: e.activation(
                hfm[:, m_ * T + t5_ * 512: m_ * T + (t5_ + 1) * 512], bb,
                AF.Identity, bias=binpm[:, m_:m_ + 1])))(BANKS[m % 4], m, t5),
                reads=[bk, "binpm"], writes=["hfm"], skip_waw=True)
        for i in range(4):
            t = t5 * 4 + i
            for kc in range(8):
                mm(hfm[:, kc * T + t * 128: kc * T + (t + 1) * 128],
                   wgateb[:, kc * 8:(kc + 1) * 8],
                   BANKS[4][:, i * 8:(i + 1) * 8], i == 0 and kc == 0, False,
                   reads=["hfm", "wgateb"], writes=[BK[4]])
        mm(onesb[0:1, 0:128], bgateb, BANKS[4][:, 0:32], False, True,
           reads=["onesb", "bgateb"], writes=[BK[4]])
        Lg = BANKS[4][:, 0:32].rearrange("p (g k) -> p g k", g=4)
        m1b = stats[:, 8:12].to_broadcast([128, 4, 8])
        m2b = stats[:, 12:16].to_broadcast([128, 4, 8])
        geg1 = ge1.rearrange("p (g k) -> p g k", g=4)
        geg2 = ge2.rearrange("p (g k) -> p g k", g=4)
        tsg = t_sm.rearrange("p (g k) -> p g k", g=4)
        vop(lambda e: e.reduce_max(stats[:, 8:12], Lg, AX.X),
            reads=[BK[4]], writes=["statsR"])
        vop(lambda e: e.tensor_tensor(geg1, Lg, m1b, ALU.is_ge),
            reads=[BK[4], "statsR"], writes=["ge1"])
        vop(lambda e: e.scalar_tensor_tensor(t_sm, ge1, -BIGL, BANKS[4][:, 0:32],
            ALU.mult, ALU.add), reads=["ge1", BK[4]], writes=["t_sm"])
        vop(lambda e: e.reduce_max(stats[:, 12:16], tsg, AX.X),
            reads=["t_sm"], writes=["statsR"])
        vop(lambda e: e.tensor_tensor(geg2, Lg, m2b, ALU.is_ge),
            reads=[BK[4], "statsR"], writes=["ge2"])
        vop(lambda e: e.tensor_tensor(tsg, Lg, m1b, ALU.subtract),
            reads=[BK[4], "statsR"], writes=["t_sm"])
        sop(lambda e: e.activation(t_sm, t_sm, AF.Exp),
            reads=["t_sm"], writes=["t_sm"])
        vop(lambda e: e.tensor_mul(t_sm, t_sm, ge2),
            reads=["t_sm", "ge2"], writes=["t_sm"])
        vop(lambda e: e.reduce_sum(stats[:, 8:12], tsg, AX.X),
            reads=["t_sm"], writes=["statsR"])
        vop(lambda e: e.reciprocal(stats[:, 12:16], stats[:, 8:12]),
            reads=["statsR"], writes=["statsR"])
        vop((lambda t5_: (lambda e: e.tensor_tensor(
            route[:, t5_ * 32:(t5_ + 1) * 32].rearrange("p (g k) -> p g k", g=4),
            tsg, m2b, ALU.mult)))(t5),
            reads=["t_sm", "statsR"], writes=["route"], skip_waw=True)
        for i in range(4):
            t = t5 * 4 + i
            tp(route[:, t * 8:(t + 1) * 8], BANKS[5][0:8, i * 128:(i + 1) * 128],
               identf, reads=["route", "identf"], writes=[BK[5]])
        vop((lambda t5_: (lambda e: e.tensor_copy(
            routeT[0:8, t5_ * 512:(t5_ + 1) * 512], BANKS[5][0:8, 0:512])))(t5),
            reads=[BK[5]], writes=["routeT"], skip_waw=True)
    for kc in range(8):
        tp(A_acc[:, kc * 128:(kc + 1) * 128], BANKS[4][:, kc * 16:(kc + 1) * 16],
           identf[0:16, 0:16], reads=["A_acc", "identf"], writes=[BK[4]])
    vop(lambda e: e.tensor_copy(A_fm, BANKS[4][:, 0:128]),
        reads=[BK[4]], writes=["A_fm"])
    for h in range(H):
        for kc in range(8):
            mm(A_fm[:, kc * 16 + h: kc * 16 + h + 1],
               wvT_a[:, kc * D + h * HD: kc * D + (h + 1) * HD],
               BANKS[5][0:1, 0:64], kc == 0, kc == 7,
               reads=["A_fm", "wvT_a"], writes=[BK[5]])
        vop((lambda h_: (lambda e: e.tensor_copy(
            c_row[0:1, h_ * HD:(h_ + 1) * HD], BANKS[5][0:1, 0:64])))(h),
            reads=[BK[5]], writes=["c_row"], skip_waw=True)
    if _TAPS:
        dma("sync", taps["tap_c"], c_row, "s_tap", reads=["c_row"])

    if _TAPS:
        dma("sync", taps["tap_h"], hfm, "s_tap", reads=["hfm"])
        dma("sync", taps["tap_route"], route, "s_tap", reads=["route"])

    w1_cnt = [0, 0]

    def load_w1(e, mg):
        slot = mg % 2
        w1_cnt[slot] = dma("gpsimd", w1sl[slot], w1h[e, mg], "s_w1%d" % slot,
                           writes=["w1sl%d" % slot])

    for pas in range(NPASS):
        poff = pas * TP
        for i in range(8):
            tglob = pas * 8 + i
            b0, b1 = BK[2 * (i % 4)], BK[2 * (i % 4) + 1]
            mm(routeT[0:8, tglob * 128:(tglob + 1) * 128], b2b[0:8, 0:512],
               BANKS[2 * (i % 4)], True, True, reads=["routeT", "b2b"], writes=[b0])
            mm(routeT[0:8, tglob * 128:(tglob + 1) * 128], b2b[0:8, 512:1024],
               BANKS[2 * (i % 4) + 1], True, True, reads=["routeT", "b2b"], writes=[b1])
            vop((lambda pp, i_: (lambda e: e.tensor_copy(
                moe_tm[:, i_ * D:(i_ + 1) * D], pp)))(PAIRS[(i % 4)], i),
                reads=[b0, b1], writes=["moe_tm"], skip_waw=True)
        for e in range(NE):
            load_w1(e, 0)
            load_w1(e, 1)
            for qd in range(4):
                if e >= 2:
                    dma("sync", w2sb[:, qd * 8 * D:(qd + 1) * 8 * D],
                        w2_bf[e, 8 * qd:8 * qd + 8].transpose([1, 0, 2]),
                        "s_w2%d" % (qd % 2), reads=["w2s%d" % e],
                        writes=["w2sb%d" % qd])
                else:
                    dma("gpsimd", w2sb[:, qd * 8 * D:(qd + 1) * 8 * D],
                        w2d[e, 8 * qd:8 * qd + 8].transpose([1, 0, 2]),
                        "s_w2g%d" % (qd % 2), writes=["w2sb%d" % qd])
            for mg in range(16):
                slot = mg % 2
                bsel = 0 if mg % 2 == 0 else 4
                for mi in range(2):
                    for kc in range(8):
                        for th in range(2):
                            bk = BK[bsel + 2 * mi + th]
                            mm(w1sl[slot][:, (mi * 8 + kc) * 128:(mi * 8 + kc + 1) * 128],
                               hfm[:, kc * T + poff + th * 512: kc * T + poff + (th + 1) * 512],
                               BANKS[bsel + 2 * mi + th], kc == 0, kc == 7,
                               reads=["w1sl%d" % slot, "hfm"], writes=[bk])
                if mg < 14:
                    load_w1(e, mg + 2)
                for mi in range(2):
                    m = mg * 2 + mi
                    pr2 = (bsel + 2 * mi) // 2
                    sop((lambda pp, m_, e_: (lambda eng: eng.activation(
                        mid[:, m_ * TP:(m_ + 1) * TP], pp, AF.Relu,
                        bias=b1pm[:, e_ * 32 + m_: e_ * 32 + m_ + 1])))
                        (PAIRS[pr2], m, e),
                        reads=[BK[bsel + 2 * mi], BK[bsel + 2 * mi + 1], "b1pm"],
                        writes=["mid"], skip_waw=True)
            for i in range(8):
                tglob = pas * 8 + i
                b0, b1 = BK[2 * (i % 4)], BK[2 * (i % 4) + 1]
                for kc in range(32):
                    w2r = "w2sb%d" % (kc // 8)
                    mm(mid[:, kc * TP + i * 128: kc * TP + (i + 1) * 128],
                       w2sb[:, kc * D: kc * D + 512],
                       BANKS[2 * (i % 4)], kc == 0, kc == 31,
                       reads=["mid", w2r], writes=[b0])
                    mm(mid[:, kc * TP + i * 128: kc * TP + (i + 1) * 128],
                       w2sb[:, kc * D + 512:(kc + 1) * D],
                       BANKS[2 * (i % 4) + 1], kc == 0, kc == 31,
                       reads=["mid", w2r], writes=[b1])
                vop((lambda pp, i_, e_, tg: (lambda eng: eng.scalar_tensor_tensor(
                    moe_tm[:, i_ * D:(i_ + 1) * D], pp,
                    route[:, tg * 8 + e_: tg * 8 + e_ + 1],
                    moe_tm[:, i_ * D:(i_ + 1) * D], ALU.mult, ALU.add)))
                    (PAIRS[(i % 4)], i, e, tglob),
                    reads=[b0, b1, "route", "moe_tm"], writes=["moe_tm"])
            if pas == 0 and e < 3:
                dma("gpsimd", w2_bf[e + 5], w2d[e + 5], "s_pc",
                    writes=["w2s%d" % (e + 5)])
        if _TAPS and pas == 0:
            dma("sync", taps["tap_moe"], moe_tm, "s_tap", reads=["moe_tm"])
        dma("gpsimd", woutb, w_outd.transpose([1, 0, 2]), "s_wo", writes=["woutb"])
        for i in range(8):
            tglob = pas * 8 + i
            for j in range(8):
                tp(moe_tm[:, i * D + j * 128: i * D + (j + 1) * 128],
                   BANKSBF[4][:, j * 128:(j + 1) * 128], identb,
                   reads=["moe_tm", "identb"], writes=[BK[4]])
            vop(lambda e: e.tensor_copy(moefm, BANKSBF[4]),
                reads=[BK[4]], writes=["moefm"])
            for th in range(2):
                for j in range(8):
                    mm(moefm[:, j * 128:(j + 1) * 128],
                       woutb[:, j * D + th * 512: j * D + (th + 1) * 512],
                       BANKS[th], j == 0, False,
                       reads=["moefm", "woutb"], writes=[BK[th]])
                mm(onesb[0:1, 0:128], boutb[0:1, th * 512:(th + 1) * 512],
                   BANKS[th], False, True,
                   reads=["onesb", "boutb"], writes=[BK[th]])
                sop((lambda bb, th_: (lambda e: e.activation(
                    g_stage[:, th_ * 512:(th_ + 1) * 512], bb, AF.Silu)))
                    (BANKS[th], th),
                    reads=[BK[th]], writes=["g_stage"], skip_waw=(th == 1))
            dma("sync", g_scr[tglob * 128:(tglob + 1) * 128, :], g_stage, "s_gw",
                reads=["g_stage"], writes=["g_scr"], skip_waw=True)

    dma("sync", gamma_f[0:1, :], gamma_d, "s_c", writes=["gamma_f"])
    dma("sync", beta_f[0:1, :], beta_d, "s_c", writes=["beta_f"])
    dma("sync", tri, tri_d, "s_c", writes=["tri"])
    for th in range(2):
        mm(onesf[0:1, 0:128], gamma_f[0:1, th * 512:(th + 1) * 512],
           BANKS[4 + th], True, True, reads=["onesf", "gamma_f"], writes=[BK[4 + th]])
        vop((lambda bb, th_: (lambda e: e.tensor_copy(
            gamma_f[:, th_ * 512:(th_ + 1) * 512], bb)))(BANKS[4 + th], th),
            reads=[BK[4 + th]], writes=["gamma_f"], skip_waw=True)
    for th in range(2):
        mm(onesf[0:1, 0:128], beta_f[0:1, th * 512:(th + 1) * 512],
           BANKS[4 + th], True, True, reads=["onesf", "beta_f"], writes=[BK[4 + th]])
        vop((lambda bb, th_: (lambda e: e.tensor_copy(
            beta_f[:, th_ * 512:(th_ + 1) * 512], bb)))(BANKS[4 + th], th),
            reads=[BK[4 + th]], writes=["beta_f"], skip_waw=True)
    for h in range(H):
        vop((lambda h_: (lambda e: e.tensor_scalar(r1row, iota,
            loglamr[0:1, h_:h_ + 1], None, ALU.mult)))(h),
            reads=["iota", "loglamr"], writes=["r1row"])
        vop((lambda h_: (lambda e: e.tensor_scalar(r2row, iota,
            loglamr[0:1, h_:h_ + 1], -1.0, ALU.mult, ALU.mult)))(h),
            reads=["iota", "loglamr"], writes=["r2row"])
        mm(onesf[0:1, 0:128], r1row, BANKS[6][:, 0:128], True, False,
           reads=["onesf", "r1row"], writes=[BK[6]])
        mm(r2row, onesf[0:1, 0:128], BANKS[6][:, 0:128], False, True,
           reads=["r2row", "onesf"], writes=[BK[6]])
        vop(lambda e: e.scalar_tensor_tensor(t1f[:, 0:128], tri, BIGM,
            BANKS[6][:, 0:128], ALU.mult, ALU.add),
            reads=["tri", BK[6]], writes=["t1f"])
        sop((lambda h_: (lambda e: e.activation(L_bf[:, h_ * 128:(h_ + 1) * 128],
            t1f[:, 0:128], AF.Exp, bias=-BIGM)))(h),
            reads=["t1f"], writes=["L_bf"], skip_waw=True)
        vop((lambda h_: (lambda e: e.tensor_scalar(
            powv[0:1, h_ * 128:(h_ + 1) * 128], iota1,
            loglamr[0:1, h_:h_ + 1], None, ALU.mult)))(h),
            reads=["iota1", "loglamr"], writes=["powv"], skip_waw=True)
    sop(lambda e: e.activation(powv, powv, AF.Exp), reads=["powv"], writes=["powv"])
    dma("gpsimd", wstage, w_qd.transpose([1, 0, 2]), "s_w", writes=["wstage"])
    for dst, dnm, nxt in ((wT1, "wT1", w_vd), (wT2, "wT2", None)):
        for dc in range(8):
            wb = 4 + (dc % 2)
            for ec in range(8):
                tp(wstage[:, ec * D + dc * 128: ec * D + dc * 128 + 128],
                   BANKSBF[wb][:, ec * 128:(ec + 1) * 128], identb,
                   reads=["wstage", "identb"], writes=[BK[wb]])
            vop((lambda bb, dc_, d_: (lambda e: e.tensor_copy(
                d_[:, dc_ * D:(dc_ + 1) * D], bb)))(BANKSBF[wb], dc, dst),
                reads=[BK[wb]], writes=[dnm], skip_waw=True)
        if nxt is not None:
            dma("gpsimd", wstage, nxt.transpose([1, 0, 2]), "s_w", writes=["wstage"])
    dma("gpsimd", wstage, w_gd.transpose([1, 0, 2]), "s_w", writes=["wstage"])
    dma("gpsimd", wstage2, w_od.transpose([1, 0, 2]), "s_w", writes=["wstage2"])

    def ca_pre(t):
        par = t % 3
        dma("sync", x_stc[par], x_own[t * 128:(t + 1) * 128, :],
            "s_x%d" % par, writes=["x_stc%d" % par])
        c0, snm = (0, "statsA") if t % 2 == 0 else (8, "statsR")
        emit_ln(x_stc[par], "x_stc%d" % par, xn_tmc[par], "xn_tmc%d" % par,
                stats_all[:, t:t + 1], stats_all[:, 16 + t:17 + t],
                ("stats_all", "stats_all"), xn_tmc[par], "xn_tmc%d" % par,
                c0=c0, statnm=snm)

    def ca_project(t):
        par = t % 2
        par3 = t % 3
        for kc in range(8):
            tp(xn_tmc[par3][:, kc * 128:(kc + 1) * 128],
               BANKSBF[0][:, kc * 128:(kc + 1) * 128], identb,
               reads=["xn_tmc%d" % par3, "identb"], writes=[BK[0]])
        vop(lambda e: e.tensor_copy(xnfm_cc, BANKSBF[0]),
            reads=[BK[0]], writes=["xnfm_cc"])
        for kc in range(8):
            mm(xnfm_cc[:, kc * 128:(kc + 1) * 128], wT1[:, kc * D: kc * D + 512],
               BANKS[2], kc == 0, kc == 7, reads=["xnfm_cc", "wT1"], writes=[BK[2]])
            mm(xnfm_cc[:, kc * 128:(kc + 1) * 128], wT1[:, kc * D + 512:(kc + 1) * D],
               BANKS[3], kc == 0, kc == 7, reads=["xnfm_cc", "wT1"], writes=[BK[3]])
        for kc in range(8):
            mm(xnfm_cc[:, kc * 128:(kc + 1) * 128], wT2[:, kc * D: kc * D + 512],
               BANKS[4], kc == 0, kc == 7, reads=["xnfm_cc", "wT2"], writes=[BK[4]])
            mm(xnfm_cc[:, kc * 128:(kc + 1) * 128], wT2[:, kc * D + 512:(kc + 1) * D],
               BANKS[5], kc == 0, kc == 7, reads=["xnfm_cc", "wT2"], writes=[BK[5]])
        vop((lambda q: (lambda e: e.tensor_copy(q, PAIRS[1])))(qs[par]),
            reads=[BK[2], BK[3]], writes=["qs%d" % par])
        vop((lambda v: (lambda e: e.tensor_copy(v, PAIRS[2])))(vs[par]),
            reads=[BK[4], BK[5]], writes=[VSNM[par]])

    def ca_scan(t):
        par = t % 2
        for h in range(H):
            bsl = BANKS[6 + (h // 8)][:, (h % 8) * HD:((h % 8) + 1) * HD]
            mm(L_bf[:, h * 128:(h + 1) * 128], vs[par][:, h * HD:(h + 1) * HD],
               bsl, h % 8 == 0, False,
               reads=["L_bf", VSNM[par]], writes=[BK[6 + (h // 8)]])
        for h in range(H):
            bsl = BANKS[6 + (h // 8)][:, (h % 8) * HD:((h % 8) + 1) * HD]
            mm(powv[0:1, h * 128:(h + 1) * 128], c_row[0:1, h * HD:(h + 1) * HD],
               bsl, False, h % 8 == 7,
               reads=["powv", "c_row"], writes=[BK[6 + (h // 8)]])
        for h in range(H):
            bq = 2 + (h // 8)
            bsl = BANKS[bq][0:1, (h % 8) * HD:((h % 8) + 1) * HD]
            mm(w0b[:, h:h + 1], vs[par][:, h * HD:(h + 1) * HD],
               bsl, h % 8 == 0, False, reads=["w0b", VSNM[par]], writes=[BK[bq]])
        for h in range(H):
            bq = 2 + (h // 8)
            bsl = BANKS[bq][0:1, (h % 8) * HD:((h % 8) + 1) * HD]
            mm(powv[0:1, h * 128 + 127: h * 128 + 128],
               c_row[0:1, h * HD:(h + 1) * HD],
               bsl, False, h % 8 == 7, reads=["powv", "c_row"], writes=[BK[bq]])
        vop((lambda q: (lambda e: e.tensor_mul(y_sb, q, PAIRS[3])))(qs[par]),
            reads=["qs%d" % par, BK[6], BK[7]], writes=["y_sb"])
        vop(lambda e: e.tensor_copy(c_row, PAIRS[1][0:1, :]),
            reads=[BK[2], BK[3]], writes=["c_row"])
        sop(lambda e: e.activation(t1f, y_sb, AF.Copy, accum_out=stats[:, 16:17]),
            reads=["y_sb"], writes=["t1f", "statsB"])
        sop(lambda e: e.activation(t1f, y_sb, AF.Square, accum_out=stats[:, 17:18]),
            reads=["y_sb", "statsB"], writes=["t1f", "statsB"])
        sop(lambda e: e.activation(stats[:, 18:19], stats[:, 16:17], AF.Copy,
            scale=1.0 / D), reads=["statsB"], writes=["statsB"])
        sop(lambda e: e.activation(stats[:, 19:20], stats[:, 18:19], AF.Square),
            reads=["statsB"], writes=["statsB"])
        vop(lambda e: e.tensor_scalar(stats[:, 20:21], stats[:, 17:18], 1.0 / D,
            stats[:, 19:20], ALU.mult, ALU.subtract), reads=["statsB"],
            writes=["statsB"])
        sop(lambda e: e.activation(stats[:, 21:22], stats[:, 20:21], AF.Ln,
            bias=LN_EPS), reads=["statsB"], writes=["statsB"])
        sop(lambda e: e.activation(stats[:, 22:23], stats[:, 21:22], AF.Exp,
            scale=-0.5), reads=["statsB"], writes=["statsB"])
        vop(lambda e: e.scalar_tensor_tensor(t1f, y_sb, stats[:, 18:19], gamma_f,
            ALU.subtract, ALU.mult), reads=["y_sb", "statsB", "gamma_f"],
            writes=["t1f"])
        vop((lambda t_: (lambda e: e.scalar_tensor_tensor(
            yln_all[:, t_ * D:(t_ + 1) * D], t1f, stats[:, 22:23], beta_f,
            ALU.mult, ALU.add)))(t), reads=["t1f", "statsB", "beta_f"],
            writes=["yln_all"], skip_waw=True)

    ca_pre(0)
    ca_pre(1)
    ca_pre(2)
    ca_project(0)
    for t in range(1, NT):
        if t + 2 < NT:
            ca_pre(t + 2)
        ca_project(t)
        ca_scan(t - 1)
    ca_scan(NT - 1)
    if _TAPS:
        dma("sync", taps["tap_yln"], yln_all, "s_tap", reads=["yln_all"])
        dma("sync", taps["tap_stats"], stats_all[:, 0:32], "s_tap", reads=["stats_all"])

    for dst, dnm, stg, snm in ((wT1, "wT1", wstage, "wstage"),
                               (wT2, "wT2", wstage2, "wstage2")):
        for dc in range(8):
            wb = 4 + (dc % 2)
            for ec in range(8):
                tp(stg[:, ec * D + dc * 128: ec * D + dc * 128 + 128],
                   BANKSBF[wb][:, ec * 128:(ec + 1) * 128], identb,
                   reads=[snm, "identb"], writes=[BK[wb]])
            vop((lambda bb, dc_, d_: (lambda e: e.tensor_copy(
                d_[:, dc_ * D:(dc_ + 1) * D], bb)))(BANKSBF[wb], dc, dst),
                reads=[BK[wb]], writes=[dnm], skip_waw=True)

    xnfm_cb = [xnfm_cc, xnfm_cc2]
    XNFM_NM = ["xnfm_cc", "xnfm_cc2"]
    Y2AP = [v_sb, y2b2]
    Y2NM = ["v_sb", "y2b2"]
    YFAP = [y2fm, y2fm2]
    YFNM = ["y2fm", "y2fm2"]
    GSAP = [gate_sb, gate_sb2]
    GSNM = ["gate_sb", "gate_sb2"]

    def cb_dma(t):
        par = t % 3
        dma("sync", x_stc[par], x_own[t * 128:(t + 1) * 128, :],
            "s_x%d" % par, writes=["x_stc%d" % par])
        dma("sync", g_st[par], g_scr[t * 128:(t + 1) * 128, :],
            "s_g%d" % par, reads=["g_scr"], writes=["g_st%d" % par])

    def cb_prets(t):
        par = t % 3
        vop((lambda s, o, t_: (lambda e: e.tensor_scalar(o, s,
            stats_all[:, t_:t_ + 1], stats_all[:, 16 + t_:17 + t_],
            ALU.subtract, ALU.mult)))(x_stc[par], xn_tmc[par], t),
            reads=["x_stc%d" % par, "stats_all"], writes=["xn_tmc%d" % par])

    def cb_xn(t):
        par3 = t % 3
        par = t % 2
        for kc in range(8):
            tp(xn_tmc[par3][:, kc * 128:(kc + 1) * 128],
               BANKSBF[4][:, kc * 128:(kc + 1) * 128], identb,
               reads=["xn_tmc%d" % par3, "identb"], writes=[BK[4]])
        sop((lambda dst: (lambda e: e.activation(dst, BANKSBF[4], AF.Copy)))
            (xnfm_cb[par]), reads=[BK[4]], writes=[XNFM_NM[par]])

    def cb_ytp(t):
        par = t % 2
        for kc in range(8):
            tp(Y2AP[par][:, kc * 128:(kc + 1) * 128],
               BANKSBF[5][:, kc * 128:(kc + 1) * 128],
               identb, reads=[Y2NM[par], "identb"], writes=[BK[5]])
        sop((lambda yf: (lambda e: e.activation(yf, BANKSBF[5], AF.Copy)))
            (YFAP[par]), reads=[BK[5]], writes=[YFNM[par]])

    def cb_gate(t):
        par = t % 2
        gb = 0 if par == 0 else 2
        for kc in range(8):
            mm(xnfm_cb[par][:, kc * 128:(kc + 1) * 128],
               wT1[:, kc * D: kc * D + 512],
               BANKS[gb], kc == 0, kc == 7,
               reads=[XNFM_NM[par], "wT1"], writes=[BK[gb]])
            mm(xnfm_cb[par][:, kc * 128:(kc + 1) * 128],
               wT1[:, kc * D + 512:(kc + 1) * D],
               BANKS[gb + 1], kc == 0, kc == 7,
               reads=[XNFM_NM[par], "wT1"], writes=[BK[gb + 1]])

    def cb_act(t):
        par = t % 2
        gb = 0 if par == 0 else 2
        sop((lambda g, pp: (lambda e: e.activation(g, pp, AF.Silu)))
            (GSAP[par], PAIRS[gb // 2]), reads=[BK[gb], BK[gb + 1]],
            writes=[GSNM[par]])
        vop((lambda t_, g, y2: (lambda e: e.tensor_mul(y2,
            yln_all[:, t_ * D:(t_ + 1) * D], g)))(t, GSAP[par], Y2AP[par]),
            reads=["yln_all", GSNM[par]], writes=[Y2NM[par]])

    def cb_s(t):
        par = t % 2
        for kc in range(8):
            mm(YFAP[par][:, kc * 128:(kc + 1) * 128], wT2[:, kc * D: kc * D + 512],
               BANKS[6], kc == 0, kc == 7, reads=[YFNM[par], "wT2"], writes=[BK[6]])
            mm(YFAP[par][:, kc * 128:(kc + 1) * 128], wT2[:, kc * D + 512:(kc + 1) * D],
               BANKS[7], kc == 0, kc == 7, reads=[YFNM[par], "wT2"], writes=[BK[7]])

    def cb_out(t):
        par3 = t % 3
        par = t % 2
        ob = out_sb[par]
        onm = "out_sb%d" % par
        vop((lambda o, gp: (lambda e: e.tensor_mul(o, PAIRS[3], gp)))
            (ob, g_st[par3]),
            reads=[BK[6], BK[7], "g_st%d" % par3], writes=[onm])
        vop((lambda o, xb: (lambda e: e.tensor_add(o, o, xb)))(ob, x_stc[par3]),
            reads=[onm, "x_stc%d" % par3], writes=[onm])
        dma("sync", y_out[t * 128:(t + 1) * 128, :], ob, "s_o%d" % par,
            reads=[onm])

    cb_dma(0)
    cb_dma(1)
    cb_dma(2)
    cb_prets(0)
    cb_prets(1)
    cb_xn(0)
    cb_gate(0)
    cb_act(0)
    for t in range(NT):
        if t + 2 < NT:
            cb_prets(t + 2)
        if t + 1 < NT:
            cb_xn(t + 1)
        cb_ytp(t)
        if t + 1 < NT:
            cb_gate(t + 1)
        cb_s(t)
        if t + 1 < NT:
            cb_act(t + 1)
        cb_out(t)
        if t + 3 < NT:
            cb_dma(t + 3)

    fin = [("s_o0", p.cnt.get("s_o0", 0)), ("s_o1", p.cnt.get("s_o1", 0))]
    if _TAPS:
        fin.append(("s_tap", p.cnt.get("s_tap", 0)))
    p.op("sync", None, extra=fin, sem="s_fin", amt=0)

    import contextlib
    with contextlib.ExitStack() as stack:
        sems = {}
        for s in p.sem_names:
            sems[s] = stack.enter_context(nc.semaphore(s))
        blk = stack.enter_context(nc.Block())

        eng_sems = set(ENG_SEM.values())
        needed = {s: v for s, v in p.needed.items() if s in eng_sems}
        plans = {}
        for name in ENGS:
            plan = []
            run = {}
            for fn, waits, sem, amt in p.ops[name]:
                if sem in eng_sems and amt:
                    run[sem] = run.get(sem, 0) + amt
                    post = sum(a for f2, w2, s2, a in p.ops[name] if s2 == sem) if False else None
                    plan.append(run[sem])
                else:
                    plan.append(None)
            plans[name] = plan

        def section(name):
            def go(eng):
                carried = {}
                plan = plans[name]
                for idx, (fn, waits, sem, amt) in enumerate(p.ops[name]):
                    for (s, v) in waits:
                        eng.wait_ge(sems[s], v)
                    if fn is not None:
                        inst = fn(eng)
                        if amt:
                            if sem in eng_sems:
                                carried[sem] = carried.get(sem, 0) + amt
                                if plan[idx] in needed.get(sem, ()):
                                    inst.then_inc(sems[sem], carried[sem])
                                    carried[sem] = 0
                            else:
                                inst.then_inc(sems[sem], amt)
            return go

        blk.tensor(section("tensor"))
        blk.vector(section("vector"))
        blk.scalar(section("scalar"))
        blk.sync(section("sync"))
        blk.gpsimd(section("gpsimd"))
    return nc


def _prep_inputs(inp):
    f = np.float32
    bf = ml_dtypes.bfloat16
    x = np.asarray(inp["L0_attn_x"], f)
    W1 = np.asarray(inp["L0_ffn_W1"], f)
    w1h = np.ascontiguousarray(
        W1.reshape(NE, 8, 128, 16, 2, 128).transpose(0, 3, 2, 4, 1, 5)
    ).reshape(NE, 16, 128, 2048)
    shared = {
        "w1h": w1h,
        "w2d": np.ascontiguousarray(np.asarray(inp["L0_ffn_W2"], f)).reshape(NE, 32, 128, D),
        "w_ind": np.ascontiguousarray(np.asarray(inp["L0_ffn_W_in"], f)).reshape(8, 128, D),
        "w_qd": np.ascontiguousarray(np.asarray(inp["L0_attn_Wq"], f)).reshape(8, 128, D),
        "w_vd": np.ascontiguousarray(np.asarray(inp["L0_attn_Wv"], f)).reshape(8, 128, D),
        "w_gd": np.ascontiguousarray(np.asarray(inp["L0_attn_Wg"], f)).reshape(8, 128, D),
        "w_od": np.ascontiguousarray(np.asarray(inp["L0_attn_Wo"], f)).reshape(8, 128, D),
        "w_outd": np.ascontiguousarray(np.asarray(inp["L0_ffn_W_out"], f)).reshape(8, 128, D),
        "w_gated": np.ascontiguousarray(np.asarray(inp["L0_ffn_W_gate"], f)).reshape(8, 128, 8),
        "b_in_pm": np.ascontiguousarray(np.asarray(inp["L0_ffn_b_in"], f).reshape(8, 128).T),
        "b1_pm": np.ascontiguousarray(
            np.asarray(inp["L0_ffn_b1"], f).reshape(NE, 32, 128).transpose(2, 0, 1)
        ).reshape(128, NE * 32),
        "b2_d": np.ascontiguousarray(np.asarray(inp["L0_ffn_b2"], f)),
        "b_out_d": np.asarray(inp["L0_ffn_b_out"], f).reshape(1, D).copy(),
        "b_gate_d": np.tile(np.asarray(inp["L0_ffn_b_gate"], f).reshape(1, NE), (1, 4)),
        "gamma_d": np.asarray(inp["L0_attn_ln_gamma"], f).reshape(1, D).copy(),
        "beta_d": np.asarray(inp["L0_attn_ln_beta"], f).reshape(1, D).copy(),
        "decay_c": np.asarray(inp["L0_attn_decay"], f).reshape(16, 1).copy(),
        "decay_r": np.asarray(inp["L0_attn_decay"], f).reshape(1, 16).copy(),
        "ident_bfd": np.eye(128, dtype=bf),
        "ident_fd": np.eye(128, dtype=f),
        "tri_d": np.triu(np.ones((128, 128), f)),
        "iota_rd": np.arange(128, dtype=f).reshape(1, 128),
        "iota1_rd": (np.arange(128, dtype=f) + 1).reshape(1, 128),
        "iotar_rd": (127 - np.arange(128, dtype=f)).reshape(1, 128),
        "ones_bfd": np.ones((1, 512), dtype=bf),
        "ones_fd": np.ones((1, 128), dtype=f),
        "cvals_d": np.tile(np.array(CONST_VALS + [0.0] * (16 - len(CONST_VALS)),
                                    dtype=f), (128, 1)),
    }
    in_maps = []
    for k in range(NCORES):
        b, s0 = k // 4, (k % 4) * T
        xo = np.ascontiguousarray(x[b, s0:s0 + T])
        xp = np.zeros((PRE, D), f)
        if s0 > 0:
            xp[PRE - s0:] = x[b, :s0]
        m = {"x_own": xo, "x_pre": xp}
        m.update(shared)
        in_maps.append(m)
    return in_maps


def kernel(**inputs):
    in_maps = _prep_inputs(inputs)
    nc = build_nc()
    res = run_bass_kernel_spmd(nc, in_maps, core_ids=list(range(NCORES)))
    out = np.empty((2, 8192, D), np.float32)
    for k in range(NCORES):
        b, s0 = k // 4, (k % 4) * T
        out[b, s0:s0 + T] = np.asarray(res.results[k]["y_out"], np.float32)
    return out


# version 30 (final)
# speedup vs baseline: 1.1711x; 1.0233x over previous
import numpy as np
import ml_dtypes

import concourse.bass as bass
import concourse.mybir as mybir
from concourse.bass_utils import run_bass_kernel_spmd

F32 = mybir.dt.float32
BF16 = mybir.dt.bfloat16
AF = mybir.ActivationFunctionType
ALU = mybir.AluOpType
AX = mybir.AxisListType

NCORES = 8
D = 1024
H = 16
HD = 64
DM = 4096
NE = 8
T = 2048
NT = T // 128
PRE = 6144
NPRE = PRE // 128
NPASS = 2
TP = T // NPASS
LN_EPS = 1e-5
BIGM = 60.0
BIGL = 1e9

_TAPS = False

CONST_VALS = [0.0, 1.0, LN_EPS, -BIGM, BIGM, 1.0 / D, -0.5, 128.0, -1.0, -BIGL]

ENGS = ("tensor", "vector", "scalar", "sync", "gpsimd")
ENG_SEM = {"tensor": "st", "vector": "sv", "scalar": "ss"}


class Prog:

    def __init__(self):
        self.ops = {e: [] for e in ENGS}
        self.cnt = {}
        self.hwm = {e: {} for e in ENGS}
        self.sem_names = []
        self.needed = {}
        self.regions = {}

    def sem(self, name):
        if name not in self.cnt:
            self.sem_names.append(name)
            self.cnt[name] = 0

    def region(self, name, space, lo, hi):
        self.regions[name] = [space, lo, hi, None, {}]

    def _wait(self, eng, waits, tok):
        if tok is None:
            return
        s, v = tok
        if v <= 0:
            return
        self.needed.setdefault(s, set()).add(v)
        if self.hwm[eng].get(s, 0) >= v:
            return
        self.hwm[eng][s] = v
        waits.append((s, v))

    def op(self, eng, fn, reads=(), writes=(), extra=(), sem=None, amt=1,
           skip_waw=False):
        sem = sem or ENG_SEM[eng]
        self.sem(sem)
        waits = []
        for (s, v) in extra:
            self._wait(eng, waits, (s, v))
        for r in reads:
            R = self.regions[r]
            if R[3] is not None:
                if not (eng == "tensor" and R[3][2] == "tensor"):
                    self._wait(eng, waits, (R[3][0], R[3][1]))
        for r in writes:
            R = self.regions[r]
            if R[3] is not None and not skip_waw:
                if not (eng == "tensor" and R[3][2] == "tensor"):
                    self._wait(eng, waits, (R[3][0], R[3][1]))
            for (e2, s2), v2 in R[4].items():
                if not (eng == "tensor" and e2 == "tensor"):
                    self._wait(eng, waits, (s2, v2))
            sp, lo, hi = R[0], R[1], R[2]
            for q, Q in self.regions.items():
                if q == r or Q[0] != sp or Q[2] <= lo or Q[1] >= hi:
                    continue
                if Q[3] is None and not Q[4]:
                    continue
                if Q[3] is not None:
                    if not (eng == "tensor" and Q[3][2] == "tensor"):
                        self._wait(eng, waits, (Q[3][0], Q[3][1]))
                for (e2, s2), v2 in Q[4].items():
                    if not (eng == "tensor" and e2 == "tensor"):
                        self._wait(eng, waits, (s2, v2))
                Q[3] = None
                Q[4] = {}
        self.ops[eng].append((fn, waits, sem, amt))
        self.cnt[sem] += amt
        post = self.cnt[sem]
        for r in writes:
            R = self.regions[r]
            R[3] = (sem, post, eng)
            R[4] = {}
        for r in reads:
            self.regions[r][4][(eng, sem)] = post
        return post


def build_nc():
    nc = bass.Bass("TRN2", target_bir_lowering=False, debug=False)
    p = Prog()

    def din(name, shape, dtype=F32):
        return nc.dram_tensor(name, list(shape), dtype, kind="ExternalInput").ap()

    x_own = din("x_own", (T, D))
    x_pre = din("x_pre", (PRE, D))
    w1h = din("w1h", (NE, 16, 128, 2048))
    w2d = din("w2d", (NE, 32, 128, D))
    w_ind = din("w_ind", (8, 128, D))
    w_qd = din("w_qd", (8, 128, D))
    w_vd = din("w_vd", (8, 128, D))
    w_gd = din("w_gd", (8, 128, D))
    w_od = din("w_od", (8, 128, D))
    w_outd = din("w_outd", (8, 128, D))
    w_gated = din("w_gated", (8, 128, 8))
    b_in_pm = din("b_in_pm", (128, 8))
    b1_pm = din("b1_pm", (128, NE * 32))
    b2_d = din("b2_d", (NE, D))
    b_out_d = din("b_out_d", (1, D))
    b_gate_d = din("b_gate_d", (1, 4 * NE))
    gamma_d = din("gamma_d", (1, D))
    beta_d = din("beta_d", (1, D))
    decay_c = din("decay_c", (16, 1))
    decay_r = din("decay_r", (1, 16))
    ident_bfd = din("ident_bfd", (128, 128), BF16)
    ident_fd = din("ident_fd", (128, 128))
    tri_d = din("tri_d", (128, 128))
    iota_rd = din("iota_rd", (1, 128))
    iota1_rd = din("iota1_rd", (1, 128))
    iotar_rd = din("iotar_rd", (1, 128))
    ones_bfd = din("ones_bfd", (1, 512), BF16)
    ones_fd = din("ones_fd", (1, 128))
    cvals_d = din("cvals_d", (128, 16))

    y_out = nc.dram_tensor("y_out", [T, D], F32, kind="ExternalOutput").ap()
    g_scr = nc.dram_tensor("g_scr", [T, D], BF16, kind="Internal").ap()
    p.region("g_scr", "dram", 0, T * D * 2)
    w2_bf = nc.dram_tensor("w2_bf", [NE, 32, 128, D], BF16, kind="Internal").ap()
    q_scr = nc.dram_tensor("q_scr", [T, D], BF16, kind="Internal").ap()
    v_scr = nc.dram_tensor("v_scr", [T, D], BF16, kind="Internal").ap()
    p.region("q_scr", "dram", 10 ** 9, 10 ** 9 + T * D * 2)
    p.region("v_scr", "dram", 10 ** 9 + T * D * 2, 10 ** 9 + 2 * T * D * 2)
    base = T * D * 2
    for e in range(NE):
        p.region("w2s%d" % e, "dram", base, base + 32 * 128 * D * 2)
        base += 32 * 128 * D * 2
    taps = {}
    if _TAPS:
        for nm, shape, dt in [
            ("tap_h", (128, 8 * T), BF16), ("tap_route", (128, NT * NE), F32),
            ("tap_moe", (128, 8 * D), BF16), ("tap_c", (1, D), F32),
            ("tap_yln", (128, NT * D), BF16), ("tap_g", (T, D), BF16),
            ("tap_stats", (128, 32), F32),
        ]:
            taps[nm] = nc.dram_tensor(nm, list(shape), dt, kind="ExternalOutput").ap()

    SBUF_BASE = 16896
    CAP = 211500
    _off = [0]
    aps = {}

    def salloc(name, shape, dtype, at=None):
        nbytes = int(np.prod(shape[1:])) * mybir.dt.size(dtype)
        if at is None:
            at = _off[0]
            _off[0] = (at + nbytes + 31) & ~31
        h = nc.alloc_sbuf_tensor_at(name, list(shape), dtype, offset=SBUF_BASE + at)
        p.region(name, "sbuf", at, at + nbytes)
        aps[name] = h.ap()
        return aps[name], at

    route, _ = salloc("route", (128, NT * NE), F32)
    routeT, _ = salloc("routeT", (128, T), BF16)
    c_row, _ = salloc("c_row", (1, D), F32)
    identb, _ = salloc("identb", (128, 128), BF16)
    identf, _ = salloc("identf", (128, 128), F32)
    iota, _ = salloc("iota", (1, 128), F32)
    iota1, _ = salloc("iota1", (1, 128), F32)
    iotar, _ = salloc("iotar", (1, 128), F32)
    onesb, _ = salloc("onesb", (1, 512), BF16)
    onesf, _ = salloc("onesf", (1, 128), F32)
    decayc, _ = salloc("decayc", (16, 1), F32)
    decayr, _ = salloc("decayr", (1, 16), F32)
    loglamc, _ = salloc("loglamc", (16, 1), F32)
    loglamr, _ = salloc("loglamr", (1, 16), F32)
    lam128c, _ = salloc("lam128c", (16, 1), F32)
    w0b, _ = salloc("w0b", (128, 16), BF16)
    binpm, _ = salloc("binpm", (128, 8), F32)
    b1pm, _ = salloc("b1pm", (128, NE * 32), F32)
    b2b, _ = salloc("b2b", (8, D), BF16)
    boutb, _ = salloc("boutb", (1, D), BF16)
    bgateb, _ = salloc("bgateb", (1, 4 * NE), BF16)
    wgateb, _ = salloc("wgateb", (128, 8 * NE), BF16)
    hfm, HFM_AT = salloc("hfm", (128, 8 * T), BF16)
    cvals, _ = salloc("cvals", (128, 16), F32)
    stats, STATS_AT = salloc("stats", (128, 24), F32)
    del p.regions["stats"]
    p.region("statsA", "sbuf", STATS_AT, STATS_AT + 32)
    p.region("statsR", "sbuf", STATS_AT + 32, STATS_AT + 64)
    p.region("statsB", "sbuf", STATS_AT + 64, STATS_AT + 96)
    stats_all, _ = salloc("stats_all", (128, 64), F32)
    SE = _off[0]

    _off[0] = SE
    x_st, xn_tm = [], []
    for i in range(2):
        a, _ = salloc(f"x_st{i}", (128, D), F32)
        x_st.append(a)
    for i in range(2):
        a, _ = salloc(f"xn_tm{i}", (128, D), BF16)
        xn_tm.append(a)
    w_inb, _ = salloc("w_inb", (128, 8 * D), BF16)
    xnfm_g, _ = salloc("xnfm_g", (128, 8 * 512), BF16)
    dumA, _ = salloc("dumA", (128, D), F32)
    t_sm, _ = salloc("t_sm", (128, 32), F32)
    ge1, _ = salloc("ge1", (128, 32), F32)
    ge2, _ = salloc("ge2", (128, 32), F32)
    wvT_a, _ = salloc("wvT_a", (128, 8 * D), BF16)
    assert _off[0] <= SE + 65536, _off[0]
    _off[0] = SE + 65536
    wvnat, _ = salloc("wvnat", (128, 8 * D), BF16)
    pre_st, prexn = [], []
    for i in range(2):
        a, _ = salloc(f"pre_st{i}", (128, D), F32)
        pre_st.append(a)
    for i in range(2):
        a, _ = salloc(f"prexn{i}", (128, D), BF16)
        prexn.append(a)
    A_acc, _ = salloc("A_acc", (16, D), F32)
    A_fm, _ = salloc("A_fm", (128, 128), BF16)
    assert _off[0] <= SE + 131072, _off[0]

    _off[0] = SE
    mid, MID_AT = salloc("mid", (128, 32 * TP), BF16)
    woutb, _ = salloc("woutb", (128, 8 * D), BF16, at=MID_AT)
    w2sb, W2_AT = salloc("w2sb", (128, 32 * D), BF16, at=SE + 65536)
    del p.regions["w2sb"]
    for qd in range(4):
        p.region("w2sb%d" % qd, "sbuf",
                 W2_AT + qd * 8 * D * 2, W2_AT + (qd + 1) * 8 * D * 2)
    _off[0] = SE + 131072
    w1sl = []
    for i in range(2):
        a, _ = salloc(f"w1sl{i}", (128, 2048), BF16)
        w1sl.append(a)
    moe_tm, _ = salloc("moe_tm", (128, 8 * D), BF16)
    moefm, _ = salloc("moefm", (128, D), BF16)
    g_stage, _ = salloc("g_stage", (128, D), BF16)
    assert _off[0] <= CAP, _off[0]

    _off[0] = SE
    x_stc, xn_tmc = [], []
    for i in range(2):
        a, _ = salloc(f"x_stc{i}", (128, D), F32)
        x_stc.append(a)
    for i in range(2):
        a, _ = salloc(f"xn_tmc{i}", (128, D), BF16)
        xn_tmc.append(a)
    xnfm_cc, _ = salloc("xnfm_cc", (128, D), BF16)
    wT1, _ = salloc("wT1", (128, 8 * D), BF16)
    wT2, _ = salloc("wT2", (128, 8 * D), BF16)
    wstage, _ = salloc("wstage", (128, 8 * D), BF16)
    yln_all, _ = salloc("yln_all", (128, NT * D), BF16)
    gamma_f, _ = salloc("gamma_f", (128, D), F32)
    beta_f, _ = salloc("beta_f", (128, D), F32)
    L_bf, _ = salloc("L_bf", (128, H * 128), BF16)
    tri, _ = salloc("tri", (128, 128), F32)
    v_sb, VSB_AT = salloc("v_sb", (128, D), BF16)
    q_sb, QSB_AT = salloc("q_sb", (128, D), F32)
    qs = [salloc("qs0", (128, D), BF16, at=QSB_AT)[0],
          salloc("qs1", (128, D), BF16, at=QSB_AT + 2048)[0]]
    y_sb, _ = salloc("y_sb", (128, D), F32)
    t1f, _ = salloc("t1f", (128, D), F32)
    gate_sb, _ = salloc("gate_sb", (128, D), BF16)
    y2fm, _ = salloc("y2fm", (128, D), BF16)
    g_st = []
    for i in range(2):
        a, _ = salloc(f"g_st{i}", (128, D), BF16)
        g_st.append(a)
    out_sb = []
    for i in range(2):
        a, _ = salloc(f"out_sb{i}", (128, D), F32)
        out_sb.append(a)
    dumC, DUMC_AT = salloc("dumC", (128, D), F32)
    vs = [v_sb, salloc("vs1", (128, D), BF16, at=DUMC_AT)[0]]
    VSNM = ["v_sb", "vs1"]
    y2fm2, _ = salloc("y2fm2", (128, D), BF16, at=DUMC_AT)
    powv, _ = salloc("powv", (1, H * 128), F32)
    gate_sb2, _ = salloc("gate_sb2", (128, D), BF16)
    y2b2, _ = salloc("y2b2", (128, D), BF16, at=DUMC_AT + 2048)
    r1row, _ = salloc("r1row", (1, 128), F32)
    r2row, _ = salloc("r2row", (1, 128), F32)
    xnfm_cc2, _ = salloc("xnfm_cc2", (128, D), BF16)
    assert _off[0] <= CAP, _off[0]
    wstage2, _ = salloc("wstage2", (128, 8 * D), BF16, at=HFM_AT)
    _a, _ = salloc("x_stc2", (128, D), F32, at=HFM_AT + 16384)
    x_stc.append(_a)
    _a, _ = salloc("xn_tmc2", (128, D), BF16, at=HFM_AT + 20480)
    xn_tmc.append(_a)
    _a, _ = salloc("g_st2", (128, D), BF16, at=HFM_AT + 22528)
    g_st.append(_a)

    PAIRS = []
    for j in range(4):
        h = nc.alloc_psum_tensor(f"pp{j}", [128, 1024], F32)
        PAIRS.append(h.ap())
    for i in range(8):
        p.region(f"pb{i}", "psum", i * 2048, (i + 1) * 2048)
    BANKS = [PAIRS[i // 2][:, (i % 2) * 512:(i % 2) * 512 + 512] for i in range(8)]
    BK = [f"pb{i}" for i in range(8)]
    PAIRSBF = [pr.bitcast(BF16) for pr in PAIRS]
    BANKSBF = [PAIRSBF[i // 2][:, (i % 2) * 1024:(i % 2) * 1024 + 1024]
               for i in range(8)]

    def dma(eng, out, in_, sem, reads=(), writes=(), extra=(), skip_waw=False):
        p.sem(sem)
        ex = [(sem, p.cnt[sem])] + list(extra)
        return p.op(eng, (lambda o, i: (lambda e: e.dma_start(out=o, in_=i)))(out, in_),
                    reads=reads, writes=writes, extra=ex, sem=sem, amt=16,
                    skip_waw=skip_waw)

    def mm(lhsT, rhs, out, start, stop, reads=(), writes=(), extra=()):
        return p.op("tensor",
                    (lambda l, r, o, b, e2: (lambda e: e.matmul(o, l, r, start=b, stop=e2)))
                    (lhsT, rhs, out, start, stop),
                    reads=reads, writes=writes, extra=extra)

    def tp(in_, out, ident, reads=(), writes=(), extra=()):
        return p.op("tensor",
                    (lambda i, o, d: (lambda e: e.transpose(o, i, d)))(in_, out, ident),
                    reads=reads, writes=writes, extra=extra)

    def vop(fn, reads=(), writes=(), extra=(), skip_waw=False):
        return p.op("vector", fn, reads=reads, writes=writes, extra=extra,
                    skip_waw=skip_waw)

    def sop(fn, reads=(), writes=(), extra=(), skip_waw=False):
        return p.op("scalar", fn, reads=list(reads) + ["cvals"], writes=writes,
                    extra=extra, skip_waw=skip_waw)

    for dst, src, nm in ((identb, ident_bfd, "identb"), (identf, ident_fd, "identf"),
                         (iota, iota_rd, "iota"), (iota1, iota1_rd, "iota1"),
                         (iotar, iotar_rd, "iotar"), (onesb, ones_bfd, "onesb"),
                         (onesf, ones_fd, "onesf"),
                         (decayc, decay_c, "decayc"), (decayr, decay_r, "decayr"),
                         (binpm, b_in_pm, "binpm"), (b1pm, b1_pm, "b1pm"),
                         (cvals, cvals_d, "cvals")):
        dma("sync", dst, src, "s_c", writes=[nm])
    for ci, cv in enumerate(CONST_VALS):
        nc.const_aps.register(F32, cv, cvals[:, ci:ci + 1])
    dma("gpsimd", b2b, b2_d, "s_w", writes=["b2b"])
    dma("gpsimd", boutb, b_out_d, "s_w", writes=["boutb"])
    dma("gpsimd", bgateb, b_gate_d, "s_w", writes=["bgateb"])
    dma("gpsimd", wgateb, w_gated.transpose([1, 0, 2]), "s_w", writes=["wgateb"])
    dma("gpsimd", w_inb, w_ind.transpose([1, 0, 2]), "s_w", writes=["w_inb"])
    dma("gpsimd", wvnat, w_vd.transpose([1, 0, 2]), "s_w", writes=["wvnat"])
    for e in range(2, 5):
        dma("gpsimd", w2_bf[e], w2d[e], "s_pc", writes=["w2s%d" % e])

    sop(lambda e: e.activation(loglamc, decayc, AF.Ln), reads=["decayc"], writes=["loglamc"])
    sop(lambda e: e.activation(loglamr, decayr, AF.Ln), reads=["decayr"], writes=["loglamr"])
    sop(lambda e: e.activation(lam128c, loglamc, AF.Exp, scale=128.0),
        reads=["loglamc"], writes=["lam128c"])
    mm(iotar[0:1, 0:128], loglamr, BANKS[0][0:128, 0:16], True, True,
       reads=["iotar", "loglamr"], writes=[BK[0]])
    sop(lambda e: e.activation(w0b, BANKS[0][0:128, 0:16], AF.Exp),
        reads=[BK[0]], writes=["w0b"])
    for ec in range(8):
        for dc in range(8):
            b = BK[4 + ((ec * 8 + dc) % 2)]
            tp(wvnat[:, ec * D + dc * 128: ec * D + dc * 128 + 128],
               BANKSBF[4 + ((ec * 8 + dc) % 2)][:, 0:128], identb,
               reads=["wvnat", "identb"], writes=[b])
            vop((lambda bb, ec_, dc_: (lambda e: e.tensor_copy(
                wvT_a[:, dc_ * D + ec_ * 128: dc_ * D + ec_ * 128 + 128],
                bb[:, 0:128])))(BANKSBF[4 + ((ec * 8 + dc) % 2)], ec, dc),
                reads=[b], writes=["wvT_a"], skip_waw=True)

    vop(lambda e: e.memset(A_acc, 0.0), writes=["A_acc"])
    vop(lambda e: e.memset(c_row, 0.0), writes=["c_row"])

    def emit_ln(src_ap, src_nm, dst_ap, dst_nm, mu_ap, inv_ap, statkeys,
                dum_ap, dum_nm, c0=0, statnm="statsA"):
        mu_nm, inv_nm = statkeys
        sop((lambda s, dd: (lambda e: e.activation(dd, s, AF.Copy,
            accum_out=stats[:, c0:c0 + 1])))(src_ap, dum_ap),
            reads=[src_nm], writes=[dum_nm, statnm])
        sop((lambda s, dd: (lambda e: e.activation(dd, s, AF.Square,
            accum_out=stats[:, c0 + 1:c0 + 2])))(src_ap, dum_ap),
            reads=[src_nm, statnm], writes=[dum_nm, statnm])
        sop((lambda m: (lambda e: e.activation(m, stats[:, c0:c0 + 1], AF.Copy,
            scale=1.0 / D)))(mu_ap), reads=[statnm], writes=[mu_nm], skip_waw=True)
        sop((lambda m: (lambda e: e.activation(stats[:, c0 + 3:c0 + 4], m,
            AF.Square)))(mu_ap), reads=[mu_nm, statnm], writes=[statnm])
        vop(lambda e: e.tensor_scalar(stats[:, c0 + 4:c0 + 5],
            stats[:, c0 + 1:c0 + 2], 1.0 / D, stats[:, c0 + 3:c0 + 4],
            ALU.mult, ALU.subtract), reads=[statnm], writes=[statnm])
        sop(lambda e: e.activation(stats[:, c0 + 5:c0 + 6], stats[:, c0 + 4:c0 + 5],
            AF.Ln, bias=LN_EPS), reads=[statnm], writes=[statnm])
        sop((lambda iv: (lambda e: e.activation(iv, stats[:, c0 + 5:c0 + 6], AF.Exp,
            scale=-0.5)))(inv_ap), reads=[statnm], writes=[inv_nm], skip_waw=True)
        if dst_ap is not None:
            vop((lambda s, o, m, iv: (lambda e: e.tensor_scalar(o, s, m, iv,
                ALU.subtract, ALU.mult)))(src_ap, dst_ap, mu_ap, inv_ap),
                reads=[src_nm, mu_nm, inv_nm], writes=[dst_nm])

    def emit_P_chunk(cix):
        par = cix % 2
        dma("sync", pre_st[par], x_pre[cix * 128:(cix + 1) * 128, :],
            "s_p%d" % par, writes=["pre_st%d" % par])
        c0, snm = (0, "statsA") if cix % 2 == 0 else (16, "statsB")
        emit_ln(pre_st[par], "pre_st%d" % par, prexn[par], "prexn%d" % par,
                stats[:, c0 + 2:c0 + 3], stats[:, c0 + 6:c0 + 7], (snm, snm),
                dumA, "dumA", c0=c0, statnm=snm)
        mm(w0b, prexn[par][:, 0:512], BANKS[6][0:16, :], True, True,
           reads=["w0b", "prexn%d" % par], writes=[BK[6]])
        mm(w0b, prexn[par][:, 512:1024], BANKS[7][0:16, :], True, True,
           reads=["w0b", "prexn%d" % par], writes=[BK[7]])
        vop(lambda e: e.scalar_tensor_tensor(A_acc, A_acc,
            lam128c, PAIRS[3][0:16, :], ALU.mult, ALU.add),
            reads=[BK[6], BK[7], "A_acc", "lam128c"], writes=["A_acc"])

    for t5 in range(4):
        for cpx in range(12 * t5, 12 * t5 + 12):
            emit_P_chunk(cpx)
        for i in range(4):
            t = t5 * 4 + i
            par = t % 2
            dma("sync", x_st[par], x_own[t * 128:(t + 1) * 128, :],
                "s_x%d" % par, writes=["x_st%d" % par])
            c0, snm = (0, "statsA") if t % 2 == 0 else (16, "statsB")
            emit_ln(x_st[par], "x_st%d" % par, xn_tm[par], "xn_tm%d" % par,
                    stats[:, c0 + 2:c0 + 3], stats[:, c0 + 6:c0 + 7], (snm, snm),
                    dumA, "dumA", c0=c0, statnm=snm)
            tb = 4 + (i % 2)
            for kc in range(8):
                tp(xn_tm[par][:, kc * 128:(kc + 1) * 128],
                   BANKSBF[tb][:, kc * 128:(kc + 1) * 128], identb,
                   reads=["xn_tm%d" % par, "identb"], writes=[BK[tb]])
            vop((lambda bb, i_: (lambda e: e.tensor_copy(
                xnfm_g[:, i_ * 1024:(i_ + 1) * 1024], bb)))(BANKSBF[tb], i),
                reads=[BK[tb]], writes=["xnfm_g"])
        xnv_g = xnfm_g.rearrange("p (i kc j) -> p kc i j", i=4, kc=8)
        for m in range(8):
            bk = BK[m % 4]
            for kc in range(8):
                mm(w_inb[:, kc * D + m * 128: kc * D + (m + 1) * 128],
                   xnv_g[:, kc],
                   BANKS[m % 4], kc == 0, kc == 7,
                   reads=["w_inb", "xnfm_g"], writes=[bk])
            sop((lambda bb, m_, t5_: (lambda e: e.activation(
                hfm[:, m_ * T + t5_ * 512: m_ * T + (t5_ + 1) * 512], bb,
                AF.Identity, bias=binpm[:, m_:m_ + 1])))(BANKS[m % 4], m, t5),
                reads=[bk, "binpm"], writes=["hfm"], skip_waw=True)
        for i in range(4):
            t = t5 * 4 + i
            for kc in range(8):
                mm(hfm[:, kc * T + t * 128: kc * T + (t + 1) * 128],
                   wgateb[:, kc * 8:(kc + 1) * 8],
                   BANKS[4][:, i * 8:(i + 1) * 8], i == 0 and kc == 0, False,
                   reads=["hfm", "wgateb"], writes=[BK[4]])
        mm(onesb[0:1, 0:128], bgateb, BANKS[4][:, 0:32], False, True,
           reads=["onesb", "bgateb"], writes=[BK[4]])
        Lg = BANKS[4][:, 0:32].rearrange("p (g k) -> p g k", g=4)
        m1b = stats[:, 8:12].to_broadcast([128, 4, 8])
        m2b = stats[:, 12:16].to_broadcast([128, 4, 8])
        geg1 = ge1.rearrange("p (g k) -> p g k", g=4)
        geg2 = ge2.rearrange("p (g k) -> p g k", g=4)
        tsg = t_sm.rearrange("p (g k) -> p g k", g=4)
        vop(lambda e: e.reduce_max(stats[:, 8:12], Lg, AX.X),
            reads=[BK[4]], writes=["statsR"])
        vop(lambda e: e.tensor_tensor(geg1, Lg, m1b, ALU.is_ge),
            reads=[BK[4], "statsR"], writes=["ge1"])
        vop(lambda e: e.scalar_tensor_tensor(t_sm, ge1, -BIGL, BANKS[4][:, 0:32],
            ALU.mult, ALU.add), reads=["ge1", BK[4]], writes=["t_sm"])
        vop(lambda e: e.reduce_max(stats[:, 12:16], tsg, AX.X),
            reads=["t_sm"], writes=["statsR"])
        vop(lambda e: e.tensor_tensor(geg2, Lg, m2b, ALU.is_ge),
            reads=[BK[4], "statsR"], writes=["ge2"])
        vop(lambda e: e.tensor_tensor(tsg, Lg, m1b, ALU.subtract),
            reads=[BK[4], "statsR"], writes=["t_sm"])
        sop(lambda e: e.activation(t_sm, t_sm, AF.Exp),
            reads=["t_sm"], writes=["t_sm"])
        vop(lambda e: e.tensor_mul(t_sm, t_sm, ge2),
            reads=["t_sm", "ge2"], writes=["t_sm"])
        vop(lambda e: e.reduce_sum(stats[:, 8:12], tsg, AX.X),
            reads=["t_sm"], writes=["statsR"])
        vop(lambda e: e.reciprocal(stats[:, 12:16], stats[:, 8:12]),
            reads=["statsR"], writes=["statsR"])
        vop((lambda t5_: (lambda e: e.tensor_tensor(
            route[:, t5_ * 32:(t5_ + 1) * 32].rearrange("p (g k) -> p g k", g=4),
            tsg, m2b, ALU.mult)))(t5),
            reads=["t_sm", "statsR"], writes=["route"], skip_waw=True)
        for i in range(4):
            t = t5 * 4 + i
            tp(route[:, t * 8:(t + 1) * 8], BANKS[5][0:8, i * 128:(i + 1) * 128],
               identf, reads=["route", "identf"], writes=[BK[5]])
        vop((lambda t5_: (lambda e: e.tensor_copy(
            routeT[0:8, t5_ * 512:(t5_ + 1) * 512], BANKS[5][0:8, 0:512])))(t5),
            reads=[BK[5]], writes=["routeT"], skip_waw=True)
    for kc in range(8):
        tp(A_acc[:, kc * 128:(kc + 1) * 128], BANKS[4][:, kc * 16:(kc + 1) * 16],
           identf[0:16, 0:16], reads=["A_acc", "identf"], writes=[BK[4]])
    vop(lambda e: e.tensor_copy(A_fm, BANKS[4][:, 0:128]),
        reads=[BK[4]], writes=["A_fm"])
    for h in range(H):
        for kc in range(8):
            mm(A_fm[:, kc * 16 + h: kc * 16 + h + 1],
               wvT_a[:, kc * D + h * HD: kc * D + (h + 1) * HD],
               BANKS[5][0:1, 0:64], kc == 0, kc == 7,
               reads=["A_fm", "wvT_a"], writes=[BK[5]])
        vop((lambda h_: (lambda e: e.tensor_copy(
            c_row[0:1, h_ * HD:(h_ + 1) * HD], BANKS[5][0:1, 0:64])))(h),
            reads=[BK[5]], writes=["c_row"], skip_waw=True)
    if _TAPS:
        dma("sync", taps["tap_c"], c_row, "s_tap", reads=["c_row"])

    if _TAPS:
        dma("sync", taps["tap_h"], hfm, "s_tap", reads=["hfm"])
        dma("sync", taps["tap_route"], route, "s_tap", reads=["route"])

    w1_cnt = [0, 0]

    def load_w1(e, mg):
        slot = mg % 2
        w1_cnt[slot] = dma("gpsimd", w1sl[slot], w1h[e, mg], "s_w1%d" % slot,
                           writes=["w1sl%d" % slot])

    for pas in range(NPASS):
        poff = pas * TP
        for i in range(8):
            tglob = pas * 8 + i
            b0, b1 = BK[2 * (i % 4)], BK[2 * (i % 4) + 1]
            mm(routeT[0:8, tglob * 128:(tglob + 1) * 128], b2b[0:8, 0:512],
               BANKS[2 * (i % 4)], True, True, reads=["routeT", "b2b"], writes=[b0])
            mm(routeT[0:8, tglob * 128:(tglob + 1) * 128], b2b[0:8, 512:1024],
               BANKS[2 * (i % 4) + 1], True, True, reads=["routeT", "b2b"], writes=[b1])
            vop((lambda pp, i_: (lambda e: e.tensor_copy(
                moe_tm[:, i_ * D:(i_ + 1) * D], pp)))(PAIRS[(i % 4)], i),
                reads=[b0, b1], writes=["moe_tm"], skip_waw=True)
        for e in range(NE):
            load_w1(e, 0)
            load_w1(e, 1)
            for qd in range(4):
                if e >= 2:
                    dma("sync", w2sb[:, qd * 8 * D:(qd + 1) * 8 * D],
                        w2_bf[e, 8 * qd:8 * qd + 8].transpose([1, 0, 2]),
                        "s_w2%d" % (qd % 2), reads=["w2s%d" % e],
                        writes=["w2sb%d" % qd])
                else:
                    dma("gpsimd", w2sb[:, qd * 8 * D:(qd + 1) * 8 * D],
                        w2d[e, 8 * qd:8 * qd + 8].transpose([1, 0, 2]),
                        "s_w2g%d" % (qd % 2), writes=["w2sb%d" % qd])
            for mg in range(16):
                slot = mg % 2
                bsel = 0 if mg % 2 == 0 else 4
                for mi in range(2):
                    for kc in range(8):
                        for th in range(2):
                            bk = BK[bsel + 2 * mi + th]
                            mm(w1sl[slot][:, (mi * 8 + kc) * 128:(mi * 8 + kc + 1) * 128],
                               hfm[:, kc * T + poff + th * 512: kc * T + poff + (th + 1) * 512],
                               BANKS[bsel + 2 * mi + th], kc == 0, kc == 7,
                               reads=["w1sl%d" % slot, "hfm"], writes=[bk])
                if mg < 14:
                    load_w1(e, mg + 2)
                for mi in range(2):
                    m = mg * 2 + mi
                    pr2 = (bsel + 2 * mi) // 2
                    sop((lambda pp, m_, e_: (lambda eng: eng.activation(
                        mid[:, m_ * TP:(m_ + 1) * TP], pp, AF.Relu,
                        bias=b1pm[:, e_ * 32 + m_: e_ * 32 + m_ + 1])))
                        (PAIRS[pr2], m, e),
                        reads=[BK[bsel + 2 * mi], BK[bsel + 2 * mi + 1], "b1pm"],
                        writes=["mid"], skip_waw=True)
            for i in range(8):
                tglob = pas * 8 + i
                b0, b1 = BK[2 * (i % 4)], BK[2 * (i % 4) + 1]
                for kc in range(32):
                    w2r = "w2sb%d" % (kc // 8)
                    mm(mid[:, kc * TP + i * 128: kc * TP + (i + 1) * 128],
                       w2sb[:, kc * D: kc * D + 512],
                       BANKS[2 * (i % 4)], kc == 0, kc == 31,
                       reads=["mid", w2r], writes=[b0])
                    mm(mid[:, kc * TP + i * 128: kc * TP + (i + 1) * 128],
                       w2sb[:, kc * D + 512:(kc + 1) * D],
                       BANKS[2 * (i % 4) + 1], kc == 0, kc == 31,
                       reads=["mid", w2r], writes=[b1])
                vop((lambda pp, i_, e_, tg: (lambda eng: eng.scalar_tensor_tensor(
                    moe_tm[:, i_ * D:(i_ + 1) * D], pp,
                    route[:, tg * 8 + e_: tg * 8 + e_ + 1],
                    moe_tm[:, i_ * D:(i_ + 1) * D], ALU.mult, ALU.add)))
                    (PAIRS[(i % 4)], i, e, tglob),
                    reads=[b0, b1, "route", "moe_tm"], writes=["moe_tm"])
            if pas == 0 and e < 3:
                dma("gpsimd", w2_bf[e + 5], w2d[e + 5], "s_pc",
                    writes=["w2s%d" % (e + 5)])
        if _TAPS and pas == 0:
            dma("sync", taps["tap_moe"], moe_tm, "s_tap", reads=["moe_tm"])
        dma("gpsimd", woutb, w_outd.transpose([1, 0, 2]), "s_wo", writes=["woutb"])
        for i in range(8):
            tglob = pas * 8 + i
            for j in range(8):
                tp(moe_tm[:, i * D + j * 128: i * D + (j + 1) * 128],
                   BANKSBF[4][:, j * 128:(j + 1) * 128], identb,
                   reads=["moe_tm", "identb"], writes=[BK[4]])
            vop(lambda e: e.tensor_copy(moefm, BANKSBF[4]),
                reads=[BK[4]], writes=["moefm"])
            for th in range(2):
                for j in range(8):
                    mm(moefm[:, j * 128:(j + 1) * 128],
                       woutb[:, j * D + th * 512: j * D + (th + 1) * 512],
                       BANKS[th], j == 0, False,
                       reads=["moefm", "woutb"], writes=[BK[th]])
                mm(onesb[0:1, 0:128], boutb[0:1, th * 512:(th + 1) * 512],
                   BANKS[th], False, True,
                   reads=["onesb", "boutb"], writes=[BK[th]])
                sop((lambda bb, th_: (lambda e: e.activation(
                    g_stage[:, th_ * 512:(th_ + 1) * 512], bb, AF.Silu)))
                    (BANKS[th], th),
                    reads=[BK[th]], writes=["g_stage"], skip_waw=(th == 1))
            dma("sync", g_scr[tglob * 128:(tglob + 1) * 128, :], g_stage, "s_gw",
                reads=["g_stage"], writes=["g_scr"], skip_waw=True)

    dma("sync", gamma_f[0:1, :], gamma_d, "s_c", writes=["gamma_f"])
    dma("sync", beta_f[0:1, :], beta_d, "s_c", writes=["beta_f"])
    dma("sync", tri, tri_d, "s_c", writes=["tri"])
    for th in range(2):
        mm(onesf[0:1, 0:128], gamma_f[0:1, th * 512:(th + 1) * 512],
           BANKS[4 + th], True, True, reads=["onesf", "gamma_f"], writes=[BK[4 + th]])
        vop((lambda bb, th_: (lambda e: e.tensor_copy(
            gamma_f[:, th_ * 512:(th_ + 1) * 512], bb)))(BANKS[4 + th], th),
            reads=[BK[4 + th]], writes=["gamma_f"], skip_waw=True)
    for th in range(2):
        mm(onesf[0:1, 0:128], beta_f[0:1, th * 512:(th + 1) * 512],
           BANKS[4 + th], True, True, reads=["onesf", "beta_f"], writes=[BK[4 + th]])
        vop((lambda bb, th_: (lambda e: e.tensor_copy(
            beta_f[:, th_ * 512:(th_ + 1) * 512], bb)))(BANKS[4 + th], th),
            reads=[BK[4 + th]], writes=["beta_f"], skip_waw=True)
    for h in range(H):
        vop((lambda h_: (lambda e: e.tensor_scalar(r1row, iota,
            loglamr[0:1, h_:h_ + 1], None, ALU.mult)))(h),
            reads=["iota", "loglamr"], writes=["r1row"])
        vop((lambda h_: (lambda e: e.tensor_scalar(r2row, iota,
            loglamr[0:1, h_:h_ + 1], -1.0, ALU.mult, ALU.mult)))(h),
            reads=["iota", "loglamr"], writes=["r2row"])
        mm(onesf[0:1, 0:128], r1row, BANKS[6][:, 0:128], True, False,
           reads=["onesf", "r1row"], writes=[BK[6]])
        mm(r2row, onesf[0:1, 0:128], BANKS[6][:, 0:128], False, True,
           reads=["r2row", "onesf"], writes=[BK[6]])
        vop(lambda e: e.scalar_tensor_tensor(t1f[:, 0:128], tri, BIGM,
            BANKS[6][:, 0:128], ALU.mult, ALU.add),
            reads=["tri", BK[6]], writes=["t1f"])
        sop((lambda h_: (lambda e: e.activation(L_bf[:, h_ * 128:(h_ + 1) * 128],
            t1f[:, 0:128], AF.Exp, bias=-BIGM)))(h),
            reads=["t1f"], writes=["L_bf"], skip_waw=True)
        vop((lambda h_: (lambda e: e.tensor_scalar(
            powv[0:1, h_ * 128:(h_ + 1) * 128], iota1,
            loglamr[0:1, h_:h_ + 1], None, ALU.mult)))(h),
            reads=["iota1", "loglamr"], writes=["powv"], skip_waw=True)
    sop(lambda e: e.activation(powv, powv, AF.Exp), reads=["powv"], writes=["powv"])
    dma("gpsimd", wstage, w_qd.transpose([1, 0, 2]), "s_w", writes=["wstage"])
    for dst, dnm, nxt in ((wT1, "wT1", w_vd), (wT2, "wT2", None)):
        for dc in range(8):
            wb = 4 + (dc % 2)
            for ec in range(8):
                tp(wstage[:, ec * D + dc * 128: ec * D + dc * 128 + 128],
                   BANKSBF[wb][:, ec * 128:(ec + 1) * 128], identb,
                   reads=["wstage", "identb"], writes=[BK[wb]])
            vop((lambda bb, dc_, d_: (lambda e: e.tensor_copy(
                d_[:, dc_ * D:(dc_ + 1) * D], bb)))(BANKSBF[wb], dc, dst),
                reads=[BK[wb]], writes=[dnm], skip_waw=True)
        if nxt is not None:
            dma("gpsimd", wstage, nxt.transpose([1, 0, 2]), "s_w", writes=["wstage"])
    dma("gpsimd", wstage, w_gd.transpose([1, 0, 2]), "s_w", writes=["wstage"])
    dma("gpsimd", wstage2, w_od.transpose([1, 0, 2]), "s_w", writes=["wstage2"])

    def ca_pre(t):
        par = t % 3
        dma("sync", x_stc[par], x_own[t * 128:(t + 1) * 128, :],
            "s_x%d" % par, writes=["x_stc%d" % par])
        c0, snm = (0, "statsA") if t % 2 == 0 else (8, "statsR")
        emit_ln(x_stc[par], "x_stc%d" % par, xn_tmc[par], "xn_tmc%d" % par,
                stats_all[:, t:t + 1], stats_all[:, 16 + t:17 + t],
                ("stats_all", "stats_all"), xn_tmc[par], "xn_tmc%d" % par,
                c0=c0, statnm=snm)

    xnfm_ca = [xnfm_cc, xnfm_cc2]
    XNCA_NM = ["xnfm_cc", "xnfm_cc2"]

    def ca_project(t):
        par = t % 2
        par3 = t % 3
        for kc in range(8):
            tp(xn_tmc[par3][:, kc * 128:(kc + 1) * 128],
               BANKSBF[0][:, kc * 128:(kc + 1) * 128], identb,
               reads=["xn_tmc%d" % par3, "identb"], writes=[BK[0]])
        xf, xfn = xnfm_ca[par], XNCA_NM[par]
        vop((lambda xf_: (lambda e: e.tensor_copy(xf_, BANKSBF[0])))(xf),
            reads=[BK[0]], writes=[xfn])
        for kc in range(8):
            mm(xf[:, kc * 128:(kc + 1) * 128], wT1[:, kc * D: kc * D + 512],
               BANKS[2], kc == 0, kc == 7, reads=[xfn, "wT1"], writes=[BK[2]])
            mm(xf[:, kc * 128:(kc + 1) * 128], wT1[:, kc * D + 512:(kc + 1) * D],
               BANKS[3], kc == 0, kc == 7, reads=[xfn, "wT1"], writes=[BK[3]])
        for kc in range(8):
            mm(xf[:, kc * 128:(kc + 1) * 128], wT2[:, kc * D: kc * D + 512],
               BANKS[4], kc == 0, kc == 7, reads=[xfn, "wT2"], writes=[BK[4]])
            mm(xf[:, kc * 128:(kc + 1) * 128], wT2[:, kc * D + 512:(kc + 1) * D],
               BANKS[5], kc == 0, kc == 7, reads=[xfn, "wT2"], writes=[BK[5]])
        vop((lambda q: (lambda e: e.tensor_copy(q, PAIRS[1])))(qs[par]),
            reads=[BK[2], BK[3]], writes=["qs%d" % par])
        vop((lambda v: (lambda e: e.tensor_copy(v, PAIRS[2])))(vs[par]),
            reads=[BK[4], BK[5]], writes=[VSNM[par]])

    def ca_scan(t):
        par = t % 2
        for h in range(H):
            bsl = BANKS[6 + (h // 8)][:, (h % 8) * HD:((h % 8) + 1) * HD]
            mm(L_bf[:, h * 128:(h + 1) * 128], vs[par][:, h * HD:(h + 1) * HD],
               bsl, h % 8 == 0, False,
               reads=["L_bf", VSNM[par]], writes=[BK[6 + (h // 8)]])
        for h in range(H):
            bsl = BANKS[6 + (h // 8)][:, (h % 8) * HD:((h % 8) + 1) * HD]
            mm(powv[0:1, h * 128:(h + 1) * 128], c_row[0:1, h * HD:(h + 1) * HD],
               bsl, False, h % 8 == 7,
               reads=["powv", "c_row"], writes=[BK[6 + (h // 8)]])
        for h in range(H):
            bq = 2 + (h // 8)
            bsl = BANKS[bq][0:1, (h % 8) * HD:((h % 8) + 1) * HD]
            mm(w0b[:, h:h + 1], vs[par][:, h * HD:(h + 1) * HD],
               bsl, h % 8 == 0, False, reads=["w0b", VSNM[par]], writes=[BK[bq]])
        for h in range(H):
            bq = 2 + (h // 8)
            bsl = BANKS[bq][0:1, (h % 8) * HD:((h % 8) + 1) * HD]
            mm(powv[0:1, h * 128 + 127: h * 128 + 128],
               c_row[0:1, h * HD:(h + 1) * HD],
               bsl, False, h % 8 == 7, reads=["powv", "c_row"], writes=[BK[bq]])
        vop((lambda q: (lambda e: e.tensor_mul(y_sb, q, PAIRS[3])))(qs[par]),
            reads=["qs%d" % par, BK[6], BK[7]], writes=["y_sb"])
        vop(lambda e: e.tensor_copy(c_row, PAIRS[1][0:1, :]),
            reads=[BK[2], BK[3]], writes=["c_row"])
        sop(lambda e: e.activation(t1f, y_sb, AF.Copy, accum_out=stats[:, 16:17]),
            reads=["y_sb"], writes=["t1f", "statsB"])
        sop(lambda e: e.activation(t1f, y_sb, AF.Square, accum_out=stats[:, 17:18]),
            reads=["y_sb", "statsB"], writes=["t1f", "statsB"])
        sop(lambda e: e.activation(stats[:, 18:19], stats[:, 16:17], AF.Copy,
            scale=1.0 / D), reads=["statsB"], writes=["statsB"])
        sop(lambda e: e.activation(stats[:, 19:20], stats[:, 18:19], AF.Square),
            reads=["statsB"], writes=["statsB"])
        vop(lambda e: e.tensor_scalar(stats[:, 20:21], stats[:, 17:18], 1.0 / D,
            stats[:, 19:20], ALU.mult, ALU.subtract), reads=["statsB"],
            writes=["statsB"])
        sop(lambda e: e.activation(stats[:, 21:22], stats[:, 20:21], AF.Ln,
            bias=LN_EPS), reads=["statsB"], writes=["statsB"])
        sop(lambda e: e.activation(stats[:, 22:23], stats[:, 21:22], AF.Exp,
            scale=-0.5), reads=["statsB"], writes=["statsB"])
        vop(lambda e: e.scalar_tensor_tensor(t1f, y_sb, stats[:, 18:19], gamma_f,
            ALU.subtract, ALU.mult), reads=["y_sb", "statsB", "gamma_f"],
            writes=["t1f"])
        vop((lambda t_: (lambda e: e.scalar_tensor_tensor(
            yln_all[:, t_ * D:(t_ + 1) * D], t1f, stats[:, 22:23], beta_f,
            ALU.mult, ALU.add)))(t), reads=["t1f", "statsB", "beta_f"],
            writes=["yln_all"], skip_waw=True)

    ca_pre(0)
    ca_pre(1)
    ca_pre(2)
    ca_project(0)
    for t in range(1, NT):
        if t + 2 < NT:
            ca_pre(t + 2)
        ca_project(t)
        ca_scan(t - 1)
    ca_scan(NT - 1)
    if _TAPS:
        dma("sync", taps["tap_yln"], yln_all, "s_tap", reads=["yln_all"])
        dma("sync", taps["tap_stats"], stats_all[:, 0:32], "s_tap", reads=["stats_all"])

    for dst, dnm, stg, snm in ((wT1, "wT1", wstage, "wstage"),
                               (wT2, "wT2", wstage2, "wstage2")):
        for dc in range(8):
            wb = 4 + (dc % 2)
            for ec in range(8):
                tp(stg[:, ec * D + dc * 128: ec * D + dc * 128 + 128],
                   BANKSBF[wb][:, ec * 128:(ec + 1) * 128], identb,
                   reads=[snm, "identb"], writes=[BK[wb]])
            vop((lambda bb, dc_, d_: (lambda e: e.tensor_copy(
                d_[:, dc_ * D:(dc_ + 1) * D], bb)))(BANKSBF[wb], dc, dst),
                reads=[BK[wb]], writes=[dnm], skip_waw=True)

    xnfm_cb = [xnfm_cc, xnfm_cc2]
    XNFM_NM = ["xnfm_cc", "xnfm_cc2"]
    Y2AP = [v_sb, y2b2]
    Y2NM = ["v_sb", "y2b2"]
    YFAP = [y2fm, y2fm2]
    YFNM = ["y2fm", "y2fm2"]
    GSAP = [gate_sb, gate_sb2]
    GSNM = ["gate_sb", "gate_sb2"]

    def cb_dma(t):
        par = t % 3
        dma("sync", x_stc[par], x_own[t * 128:(t + 1) * 128, :],
            "s_x%d" % par, writes=["x_stc%d" % par])
        dma("sync", g_st[par], g_scr[t * 128:(t + 1) * 128, :],
            "s_g%d" % par, reads=["g_scr"], writes=["g_st%d" % par])

    def cb_prets(t):
        par = t % 3
        vop((lambda s, o, t_: (lambda e: e.tensor_scalar(o, s,
            stats_all[:, t_:t_ + 1], stats_all[:, 16 + t_:17 + t_],
            ALU.subtract, ALU.mult)))(x_stc[par], xn_tmc[par], t),
            reads=["x_stc%d" % par, "stats_all"], writes=["xn_tmc%d" % par])

    def cb_xn(t):
        par3 = t % 3
        par = t % 2
        for kc in range(8):
            tp(xn_tmc[par3][:, kc * 128:(kc + 1) * 128],
               BANKSBF[4][:, kc * 128:(kc + 1) * 128], identb,
               reads=["xn_tmc%d" % par3, "identb"], writes=[BK[4]])
        sop((lambda dst: (lambda e: e.activation(dst, BANKSBF[4], AF.Copy)))
            (xnfm_cb[par]), reads=[BK[4]], writes=[XNFM_NM[par]])

    def cb_ytp(t):
        par = t % 2
        for kc in range(8):
            tp(Y2AP[par][:, kc * 128:(kc + 1) * 128],
               BANKSBF[5][:, kc * 128:(kc + 1) * 128],
               identb, reads=[Y2NM[par], "identb"], writes=[BK[5]])
        sop((lambda yf: (lambda e: e.activation(yf, BANKSBF[5], AF.Copy)))
            (YFAP[par]), reads=[BK[5]], writes=[YFNM[par]])

    def cb_gate(t):
        par = t % 2
        gb = 0 if par == 0 else 2
        for kc in range(8):
            mm(xnfm_cb[par][:, kc * 128:(kc + 1) * 128],
               wT1[:, kc * D: kc * D + 512],
               BANKS[gb], kc == 0, kc == 7,
               reads=[XNFM_NM[par], "wT1"], writes=[BK[gb]])
            mm(xnfm_cb[par][:, kc * 128:(kc + 1) * 128],
               wT1[:, kc * D + 512:(kc + 1) * D],
               BANKS[gb + 1], kc == 0, kc == 7,
               reads=[XNFM_NM[par], "wT1"], writes=[BK[gb + 1]])

    def cb_act(t):
        par = t % 2
        gb = 0 if par == 0 else 2
        sop((lambda g, pp: (lambda e: e.activation(g, pp, AF.Silu)))
            (GSAP[par], PAIRS[gb // 2]), reads=[BK[gb], BK[gb + 1]],
            writes=[GSNM[par]])
        vop((lambda t_, g, y2: (lambda e: e.tensor_mul(y2,
            yln_all[:, t_ * D:(t_ + 1) * D], g)))(t, GSAP[par], Y2AP[par]),
            reads=["yln_all", GSNM[par]], writes=[Y2NM[par]])

    def cb_s(t):
        par = t % 2
        for kc in range(8):
            mm(YFAP[par][:, kc * 128:(kc + 1) * 128], wT2[:, kc * D: kc * D + 512],
               BANKS[6], kc == 0, kc == 7, reads=[YFNM[par], "wT2"], writes=[BK[6]])
            mm(YFAP[par][:, kc * 128:(kc + 1) * 128], wT2[:, kc * D + 512:(kc + 1) * D],
               BANKS[7], kc == 0, kc == 7, reads=[YFNM[par], "wT2"], writes=[BK[7]])

    def cb_out(t):
        par3 = t % 3
        par = t % 2
        ob = out_sb[par]
        onm = "out_sb%d" % par
        vop((lambda o, gp: (lambda e: e.tensor_mul(o, PAIRS[3], gp)))
            (ob, g_st[par3]),
            reads=[BK[6], BK[7], "g_st%d" % par3], writes=[onm])
        vop((lambda o, xb: (lambda e: e.tensor_add(o, o, xb)))(ob, x_stc[par3]),
            reads=[onm, "x_stc%d" % par3], writes=[onm])
        dma("sync", y_out[t * 128:(t + 1) * 128, :], ob, "s_o%d" % par,
            reads=[onm])

    cb_dma(0)
    cb_dma(1)
    cb_dma(2)
    cb_prets(0)
    cb_prets(1)
    cb_xn(0)
    cb_gate(0)
    cb_act(0)
    for t in range(NT):
        if t + 2 < NT:
            cb_prets(t + 2)
        if t + 1 < NT:
            cb_xn(t + 1)
        cb_ytp(t)
        if t + 1 < NT:
            cb_gate(t + 1)
        cb_s(t)
        if t + 1 < NT:
            cb_act(t + 1)
        cb_out(t)
        if t + 3 < NT:
            cb_dma(t + 3)

    fin = [("s_o0", p.cnt.get("s_o0", 0)), ("s_o1", p.cnt.get("s_o1", 0))]
    if _TAPS:
        fin.append(("s_tap", p.cnt.get("s_tap", 0)))
    p.op("sync", None, extra=fin, sem="s_fin", amt=0)

    import contextlib
    with contextlib.ExitStack() as stack:
        sems = {}
        for s in p.sem_names:
            sems[s] = stack.enter_context(nc.semaphore(s))
        blk = stack.enter_context(nc.Block())

        eng_sems = set(ENG_SEM.values())
        needed = {s: v for s, v in p.needed.items() if s in eng_sems}
        plans = {}
        for name in ENGS:
            plan = []
            run = {}
            for fn, waits, sem, amt in p.ops[name]:
                if sem in eng_sems and amt:
                    run[sem] = run.get(sem, 0) + amt
                    post = sum(a for f2, w2, s2, a in p.ops[name] if s2 == sem) if False else None
                    plan.append(run[sem])
                else:
                    plan.append(None)
            plans[name] = plan

        def section(name):
            def go(eng):
                carried = {}
                plan = plans[name]
                for idx, (fn, waits, sem, amt) in enumerate(p.ops[name]):
                    for (s, v) in waits:
                        eng.wait_ge(sems[s], v)
                    if fn is not None:
                        inst = fn(eng)
                        if amt:
                            if sem in eng_sems:
                                carried[sem] = carried.get(sem, 0) + amt
                                if plan[idx] in needed.get(sem, ()):
                                    inst.then_inc(sems[sem], carried[sem])
                                    carried[sem] = 0
                            else:
                                inst.then_inc(sems[sem], amt)
            return go

        blk.tensor(section("tensor"))
        blk.vector(section("vector"))
        blk.scalar(section("scalar"))
        blk.sync(section("sync"))
        blk.gpsimd(section("gpsimd"))
    return nc


def _prep_inputs(inp):
    f = np.float32
    bf = ml_dtypes.bfloat16
    x = np.asarray(inp["L0_attn_x"], f)
    W1 = np.asarray(inp["L0_ffn_W1"], f)
    w1h = np.ascontiguousarray(
        W1.reshape(NE, 8, 128, 16, 2, 128).transpose(0, 3, 2, 4, 1, 5)
    ).reshape(NE, 16, 128, 2048)
    shared = {
        "w1h": w1h,
        "w2d": np.ascontiguousarray(np.asarray(inp["L0_ffn_W2"], f)).reshape(NE, 32, 128, D),
        "w_ind": np.ascontiguousarray(np.asarray(inp["L0_ffn_W_in"], f)).reshape(8, 128, D),
        "w_qd": np.ascontiguousarray(np.asarray(inp["L0_attn_Wq"], f)).reshape(8, 128, D),
        "w_vd": np.ascontiguousarray(np.asarray(inp["L0_attn_Wv"], f)).reshape(8, 128, D),
        "w_gd": np.ascontiguousarray(np.asarray(inp["L0_attn_Wg"], f)).reshape(8, 128, D),
        "w_od": np.ascontiguousarray(np.asarray(inp["L0_attn_Wo"], f)).reshape(8, 128, D),
        "w_outd": np.ascontiguousarray(np.asarray(inp["L0_ffn_W_out"], f)).reshape(8, 128, D),
        "w_gated": np.ascontiguousarray(np.asarray(inp["L0_ffn_W_gate"], f)).reshape(8, 128, 8),
        "b_in_pm": np.ascontiguousarray(np.asarray(inp["L0_ffn_b_in"], f).reshape(8, 128).T),
        "b1_pm": np.ascontiguousarray(
            np.asarray(inp["L0_ffn_b1"], f).reshape(NE, 32, 128).transpose(2, 0, 1)
        ).reshape(128, NE * 32),
        "b2_d": np.ascontiguousarray(np.asarray(inp["L0_ffn_b2"], f)),
        "b_out_d": np.asarray(inp["L0_ffn_b_out"], f).reshape(1, D).copy(),
        "b_gate_d": np.tile(np.asarray(inp["L0_ffn_b_gate"], f).reshape(1, NE), (1, 4)),
        "gamma_d": np.asarray(inp["L0_attn_ln_gamma"], f).reshape(1, D).copy(),
        "beta_d": np.asarray(inp["L0_attn_ln_beta"], f).reshape(1, D).copy(),
        "decay_c": np.asarray(inp["L0_attn_decay"], f).reshape(16, 1).copy(),
        "decay_r": np.asarray(inp["L0_attn_decay"], f).reshape(1, 16).copy(),
        "ident_bfd": np.eye(128, dtype=bf),
        "ident_fd": np.eye(128, dtype=f),
        "tri_d": np.triu(np.ones((128, 128), f)),
        "iota_rd": np.arange(128, dtype=f).reshape(1, 128),
        "iota1_rd": (np.arange(128, dtype=f) + 1).reshape(1, 128),
        "iotar_rd": (127 - np.arange(128, dtype=f)).reshape(1, 128),
        "ones_bfd": np.ones((1, 512), dtype=bf),
        "ones_fd": np.ones((1, 128), dtype=f),
        "cvals_d": np.tile(np.array(CONST_VALS + [0.0] * (16 - len(CONST_VALS)),
                                    dtype=f), (128, 1)),
    }
    in_maps = []
    for k in range(NCORES):
        b, s0 = k // 4, (k % 4) * T
        xo = np.ascontiguousarray(x[b, s0:s0 + T])
        xp = np.zeros((PRE, D), f)
        if s0 > 0:
            xp[PRE - s0:] = x[b, :s0]
        m = {"x_own": xo, "x_pre": xp}
        m.update(shared)
        in_maps.append(m)
    return in_maps


def kernel(**inputs):
    in_maps = _prep_inputs(inputs)
    nc = build_nc()
    res = run_bass_kernel_spmd(nc, in_maps, core_ids=list(range(NCORES)))
    out = np.empty((2, 8192, D), np.float32)
    for k in range(NCORES):
        b, s0 = k // 4, (k % 4) * T
        out[b, s0:s0 + T] = np.asarray(res.results[k]["y_out"], np.float32)
    return out
